# version 10
# baseline (speedup 1.0000x reference)
import numpy as np
import ml_dtypes
from contextlib import ExitStack

import concourse.bass as bass
import concourse.mybir as mybir
from concourse.bass import IndirectOffsetOnAxis
from concourse.bass_utils import run_bass_kernel_spmd

F32 = mybir.dt.float32
BF16 = mybir.dt.bfloat16
U32 = mybir.dt.uint32
AF = mybir.ActivationFunctionType
ALU = mybir.AluOpType
AX = mybir.AxisListType
BF = ml_dtypes.bfloat16

B, S, D = 2, 8192, 2048
H, KVH, HD = 16, 4, 128
GQ = H // KVH
SCALE = HD ** -0.5
EPS = 1e-5
NG = S // 512
NT = 8
MASKW = 1920
NEGBIG = -1.0e6


class Sem:
    __slots__ = ("h", "v")

    def __init__(self, h):
        self.h = h
        self.v = 0


def build_nc():
    nc = bass.Bass("TRN2", target_bir_lowering=False)
    es = ExitStack()

    xb = nc.dram_tensor("xb", [S, D], F32, kind="ExternalInput").ap()
    xown = nc.dram_tensor("xown", [2048, D], F32, kind="ExternalInput").ap()
    wq = nc.dram_tensor("wq", [D, GQ * HD], F32, kind="ExternalInput").ap()
    wk = nc.dram_tensor("wk", [D, HD], F32, kind="ExternalInput").ap()
    wv = nc.dram_tensor("wv", [D, HD], F32, kind="ExternalInput").ap()
    wo = nc.dram_tensor("wo", [D, D], F32, kind="ExternalInput").ap()
    wf = nc.dram_tensor("wf", [D, D], F32, kind="ExternalInput").ap()
    lnwb = nc.dram_tensor("lnwb", [128, D], F32, kind="ExternalInput").ap()
    lnbb = nc.dram_tensor("lnbb", [128, D], F32, kind="ExternalInput").ap()
    rcd = nc.dram_tensor("rcd", [64, S], F32, kind="ExternalInput").ap()
    rsd = nc.dram_tensor("rsd", [64, S], F32, kind="ExternalInput").ap()
    mstripd = nc.dram_tensor("mstripd", [128, MASKW], F32, kind="ExternalInput").ap()
    identd = nc.dram_tensor("identd", [128, 128], BF16, kind="ExternalInput").ap()
    onesd = nc.dram_tensor("onesd", [128, 128], F32, kind="ExternalInput").ap()
    agidxd = nc.dram_tensor("agidxd", [128, 64], U32, kind="ExternalInput").ap()
    y = nc.dram_tensor("y", [2048, D], F32, kind="ExternalOutput").ap()

    k_dram = nc.dram_tensor("k_dram", [128, S], BF16).ap()
    v_dram = nc.dram_tensor("v_dram", [64, 128, 132], BF16).ap()
    qt_dram = nc.dram_tensor("qt_dram", [GQ, 128, S], BF16).ap()
    a_dram = nc.dram_tensor("a_dram", [64, 128, 512], BF16).ap()
    ag_dram = nc.dram_tensor("ag_dram", [8 * 8192, 512], BF16).ap()
    ln2_dram = nc.dram_tensor("ln2_dram", [4, 128, 8192], BF16).ap()
    wob_dram = nc.dram_tensor("wob_dram", [D, D], BF16).ap()
    wfq_dram = nc.dram_tensor("wfq_dram", [D, D], BF16).ap()

    sP = Sem(es.enter_context(nc.semaphore("sP")))
    sV = Sem(es.enter_context(nc.semaphore("sV")))
    sA = Sem(es.enter_context(nc.semaphore("sA")))
    sC = Sem(es.enter_context(nc.semaphore("sC")))

    def dsem(name):
        return Sem(es.enter_context(nc.semaphore(name)))

    sInit = dsem("sInit")
    sWI = dsem("sWI")
    sXA = dsem("sXA")
    sRP = [dsem(f"sRP{i}") for i in range(2)]
    sMX = [dsem(f"sMX{i}") for i in range(4)]
    sKST = [dsem(f"sKST{i}") for i in range(2)]
    sVST = [dsem(f"sVST{i}") for i in range(2)]
    sQST = [dsem(f"sQST{i}") for i in range(4)]
    sLNST = [dsem(f"sLNST{i}") for i in range(2)]
    sQT = [dsem(f"sQT{i}") for i in range(2)]
    sSC = [dsem(f"sSC{i}") for i in range(2)]
    sAST = [dsem(f"sAST{i}") for i in range(2)]
    sWR = [dsem(f"sWR{i}") for i in range(2)]
    sWST = [dsem(f"sWST{i}") for i in range(2)]
    sLF = [dsem(f"sLF{i}") for i in range(2)]
    sWL = [dsem(f"sWL{i}") for i in range(2)]
    sXG = [dsem(f"sXG{i}") for i in range(2)]
    sAG = [dsem(f"sAG{i}") for i in range(2)]
    sYS = [dsem(f"sYS{i}") for i in range(2)]
    sMSK = dsem("sMSK")

    def dinc(instr, sem):
        instr.then_inc(sem.h, 16)
        sem.v += 16

    def dwait(e, sem):
        e.wait_ge(sem.h, sem.v)

    def sb(name, shape, dt):
        return es.enter_context(nc.sbuf_tensor(name, shape, dt))

    esAsb = ExitStack()
    esBsb = ExitStack()
    esQsb = ExitStack()
    esCsb = ExitStack()

    esA1sb = ExitStack()

    def sbA(name, shape, dt):
        return esAsb.enter_context(nc.sbuf_tensor(name, shape, dt))

    def sbA1(name, shape, dt):
        return esA1sb.enter_context(nc.sbuf_tensor(name, shape, dt))

    def sbB(name, shape, dt):
        return esBsb.enter_context(nc.sbuf_tensor(name, shape, dt))

    def sbQ(name, shape, dt):
        return esQsb.enter_context(nc.sbuf_tensor(name, shape, dt))

    def sbC(name, shape, dt):
        return esCsb.enter_context(nc.sbuf_tensor(name, shape, dt))

    ident_s = sb("ident_s", [128, 128], BF16)
    ones_s = sb("ones_s", [128, 128], F32)
    bias_s = sb("bias_s", [128, 1], F32)
    idx_s = sb("idx_s", [128, 64], U32)

    lnw_s = sbA("lnw_s", [128, D], F32)
    lnb_s = sbA("lnb_s", [128, D], F32)
    x_ring = sbA("x_ring", [128, 4 * D], F32)
    xn_ring = sbA("xn_ring", [128, 4 * D], BF16)
    sums = sbA("sums", [128, 4], F32)
    means = sbA("means", [128, 4], F32)
    varp = sbA("varp", [128, 4], F32)
    rstd = sbA("rstd", [128, 4], F32)
    v2p = sbA("v2p", [128, 4], F32)
    r2f = sbA("r2f", [128, 4], F32)
    ln2t1 = sbA("ln2t1", [128, D], F32)
    ln2t2 = sbA("ln2t2", [128, D], F32)
    ln2bf = sbA("ln2bf", [128, 2 * D], BF16)
    lnfm_st = sbA("lnfm_st", [128, 2 * 2048], BF16)
    xnfm = sbA1("xnfm", [128, 2 * 16 * 512], BF16)
    wq_s = sbA1("wq_s", [128, 16 * 512], BF16)
    wk_s = sbA1("wk_s", [128, 16 * 128], BF16)
    wv_s = sbA1("wv_s", [128, 16 * 128], BF16)
    rc_s = sbA1("rc_s", [64, 2 * 512], F32)
    rs_s = sbA1("rs_s", [64, 2 * 512], F32)
    rt1 = sbA1("rt1", [64, 512], F32)
    rt2 = sbA1("rt2", [64, 512], F32)
    rt3 = sbA1("rt3", [64, 512], F32)
    rt4 = sbA1("rt4", [64, 512], F32)
    ktg = sbA1("ktg", [128, 2 * 512], BF16)
    vtg = sbA1("vtg", [128, 2 * 4 * 132], BF16)
    qtg = sbA1("qtg", [128, 4 * 512], BF16)
    mx_in = sbA1("mx_in", [128, 4 * 512], BF16)
    sqt = sbA1("sqt", [128, 4 * 512], F32)
    mxsb = sbA1("mxsb", [1, 12 * 512], BF16)
    mkq = sbA1("mkq", [1, 4], F32)

    ms = {}

    with nc.Block() as block:
        def init_sync(e):
            for dst, srcp in ((ident_s, identd), (ones_s, onesd), (lnw_s, lnwb),
                              (lnb_s, lnbb), (idx_s, agidxd)):
                dinc(e.dma_start(out=dst[:], in_=srcp[:]), sInit)
        block.sync(init_sync)

        def init_gp(e):
            dinc(e.dma_start(out=wq_s[:].rearrange("p (c n) -> p c n", c=16),
                             in_=wq.rearrange("(c p) n -> p c n", p=128)), sWI)
            dinc(e.dma_start(out=wk_s[:].rearrange("p (c n) -> p c n", c=16),
                             in_=wk.rearrange("(c p) n -> p c n", p=128)), sWI)
            dinc(e.dma_start(out=wv_s[:].rearrange("p (c n) -> p c n", c=16),
                             in_=wv.rearrange("(c p) n -> p c n", p=128)), sWI)
            dinc(e.dma_start(out=wob_dram, in_=wo), sWI)
        block.gpsimd(init_gp)

        def init_v(e):
            e.memset(mkq[:], 1e-12)
            vv = vtg[:].rearrange("p (c n) -> p c n", c=8)
            e.memset(vv[:, :, 128:129], 1.0).then_inc(sV.h, 1)
            sV.v += 1
        block.vector(init_v)

        esA = ExitStack()
        psT0 = esA.enter_context(nc.psum_tensor("psT0", [128, 1024], BF16))
        psT1 = esA.enter_context(nc.psum_tensor("psT1", [128, 1024], BF16))
        psQ0 = esA.enter_context(nc.psum_tensor("psQ0", [128, 512], F32))
        psQ1 = esA.enter_context(nc.psum_tensor("psQ1", [128, 512], F32))
        psK = esA.enter_context(nc.psum_tensor("psK", [128, 512], F32))
        psVb = esA.enter_context(nc.psum_tensor("psVb", [128, 512], F32))
        psM0 = esA.enter_context(nc.psum_tensor("psM0", [1, 512], F32))
        psM1 = esA.enter_context(nc.psum_tensor("psM1", [1, 512], F32))

        x_cons = {}
        xn_ms = {}
        evac_ms = {}
        tb_v = {"T0": 0, "T1": 0}
        tb_p = {"T0": 0, "T1": 0}
        mm_ms = {}
        rope_ms = {}
        st_ms = {}
        ld_ms = {}
        tr_ms = {}

        def ln_group(tiles, consume_key, tr_guard=None):
            for p0 in (0, 2):
                pair = tiles[p0:p0 + 2]
                cols = [t % 4 for t in pair]
                c0, c1 = min(cols), max(cols) + 1

                def v1(e, pair=pair, cols=cols):
                    for t, cc in zip(pair, cols):
                        xs = x_ring[:, (t % 4) * D:(t % 4 + 1) * D]
                        ii = e.reduce_sum(sums[:, cc:cc + 1], xs, axis=AX.X)
                    ii.then_inc(sV.h, 1); sV.v += 1
                    e.wait_ge(sV.h, sV.v)
                    for t, cc in zip(pair, cols):
                        ii = e.tensor_scalar(means[:, cc:cc + 1], sums[:, cc:cc + 1],
                                             1.0 / D, None, op0=ALU.mult)
                    ii.then_inc(sV.h, 1); sV.v += 1
                    e.wait_ge(sV.h, sV.v)
                    for t, cc in zip(pair, cols):
                        xs = x_ring[:, (t % 4) * D:(t % 4 + 1) * D]
                        ii = e.scalar_tensor_tensor(ln2t1[:], xs, means[:, cc:cc + 1], xs,
                                                    op0=ALU.subtract, op1=ALU.mult,
                                                    accum_out=sums[:, cc:cc + 1])
                    ii.then_inc(sV.h, 1); sV.v += 1
                    e.wait_ge(sV.h, sV.v)
                    for t, cc in zip(pair, cols):
                        ii = e.tensor_scalar(varp[:, cc:cc + 1], sums[:, cc:cc + 1],
                                             1.0 / D, EPS, op0=ALU.mult, op1=ALU.add)
                    ii.then_inc(sV.h, 1); sV.v += 1
                block.vector(v1)
                mv = sV.v

                def s1(e, c0=c0, c1=c1, mv=mv):
                    e.wait_ge(sV.h, mv)
                    e.activation(out=rstd[:, c0:c1], in_=varp[:, c0:c1],
                                 func=AF.Ln).then_inc(sA.h, 1)
                    sA.v += 1
                    e.wait_ge(sA.h, sA.v)
                    e.activation(out=rstd[:, c0:c1], in_=rstd[:, c0:c1],
                                 func=AF.Exp, scale=-0.5).then_inc(sA.h, 1)
                    sA.v += 1
                block.scalar(s1)
                ma = sA.v

                def v2(e, pair=pair, cols=cols, ma=ma):
                    e.wait_ge(sA.h, ma)
                    if tr_guard is not None:
                        for t in pair:
                            gv = tr_guard(t)
                            if gv:
                                e.wait_ge(sP.h, gv)
                    for t, cc in zip(pair, cols):
                        xs = x_ring[:, (t % 4) * D:(t % 4 + 1) * D]
                        ii = e.tensor_scalar(xn_ring[:, (t % 4) * D:(t % 4 + 1) * D], xs,
                                             means[:, cc:cc + 1], rstd[:, cc:cc + 1],
                                             op0=ALU.subtract, op1=ALU.mult)
                    ii.then_inc(sV.h, 1); sV.v += 1
                    for t in pair:
                        xn_ms[t] = sV.v
                        if consume_key is not None:
                            ms[(consume_key, t)] = sV.v
                block.vector(v2)

        def rope6(e, psrc, d0, d1, g):
            sl = (g % 2) * 512
            cs = rc_s[:, sl:sl + 512]
            sn = rs_s[:, sl:sl + 512]
            e.tensor_tensor(rt1[:], psrc[0:64, :], cs, op=ALU.mult)
            e.tensor_tensor(rt2[:], psrc[64:128, :], sn, op=ALU.mult)
            e.tensor_tensor(rt3[:], psrc[0:64, :], sn, op=ALU.mult)
            ii = e.tensor_tensor(rt4[:], psrc[64:128, :], cs, op=ALU.mult)
            ii.then_inc(sV.h, 1); sV.v += 1
            e.wait_ge(sV.h, sV.v)
            e.tensor_tensor(d0, rt1[:], rt2[:], op=ALU.subtract)
            ii = e.tensor_tensor(d1, rt3[:], rt4[:], op=ALU.add)
            ii.then_inc(sV.h, 1); sV.v += 1

        mx_n = [0]
        mx_rn = [0]

        def emit_mx_reds(upto):
            while mx_rn[0] < upto:
                n = mx_rn[0]
                mx_rn[0] += 1
                kind = "k" if n % 5 == 0 else "q"

                def mx_red(e, n=n, kind=kind):
                    e.wait_ge(sA.h, ms[("mxcp", n)])
                    acc = mkq[0:1, 0:1] if kind == "k" else mkq[0:1, 1:2]
                    scr = mkq[0:1, 2:3]
                    sl = (n % 12) * 512
                    ii = e.reduce_max(scr, mxsb[0:1, sl:sl + 512], axis=AX.X)
                    ii.then_inc(sV.h, 1); sV.v += 1
                    e.wait_ge(sV.h, sV.v)
                    ii = e.tensor_tensor(acc, acc, scr, op=ALU.max)
                    ii.then_inc(sV.h, 1); sV.v += 1
                    ms[("mxred", n)] = sV.v
                block.vector(mx_red)

        def emit_mx_items(g):
            for item in [("k", 0, g)] + [("q", j, g) for j in range(GQ)]:
                kind, j, i = item
                n = mx_n[0]
                mx_n[0] += 1
                slot = n % 4
                pslot = n % 2

                def mx_ld(e, n=n, kind=kind, j=j, i=i, slot=slot):
                    if kind == "k":
                        dwait(e, sKST[i % 2])
                    else:
                        dwait(e, sQST[j])
                    if ("mxsq", n - 4) in ms:
                        e.wait_ge(sA.h, ms[("mxsq", n - 4)])
                    srcp = k_dram[:, i * 512:(i + 1) * 512] if kind == "k" else \
                        qt_dram[j][:, i * 512:(i + 1) * 512]
                    dinc(e.dma_start(out=mx_in[:, slot * 512:(slot + 1) * 512],
                                     in_=srcp), sMX[slot])
                block.gpsimd(mx_ld)

                def mx_sq(e, n=n, slot=slot):
                    dwait(e, sMX[slot])
                    if ("mxmm", n - 4) in ms:
                        e.wait_ge(sP.h, ms[("mxmm", n - 4)])
                    e.activation(out=sqt[:, slot * 512:(slot + 1) * 512],
                                 in_=mx_in[:, slot * 512:(slot + 1) * 512],
                                 func=AF.Square).then_inc(sA.h, 1)
                    sA.v += 1
                    ms[("mxsq", n)] = sA.v
                block.scalar(mx_sq)

                def mx_mm(e, n=n, slot=slot, pslot=pslot):
                    if n == 0:
                        dwait(e, sInit)
                    e.wait_ge(sA.h, ms[("mxsq", n)])
                    if ("mxcp", n - 2) in ms:
                        e.wait_ge(sA.h, ms[("mxcp", n - 2)])
                    psm = psM0 if pslot == 0 else psM1
                    ii = e.matmul(psm[:], ones_s[:, 0:1], sqt[:, slot * 512:(slot + 1) * 512],
                                  start=True, stop=True)
                    ii.then_inc(sP.h, 1); sP.v += 1
                    ms[("mxmm", n)] = sP.v
                block.tensor(mx_mm)

                def mx_cp(e, n=n, pslot=pslot):
                    e.wait_ge(sP.h, ms[("mxmm", n)])
                    if ("mxred", n - 12) in ms:
                        e.wait_ge(sV.h, ms[("mxred", n - 12)])
                    psm = psM0 if pslot == 0 else psM1
                    sl = (n % 12) * 512
                    e.copy(out=mxsb[0:1, sl:sl + 512], in_=psm[:]).then_inc(sA.h, 1)
                    sA.v += 1
                    ms[("mxcp", n)] = sA.v
                block.scalar(mx_cp)


        for g in range(NG):
            tiles = [4 * g + i for i in range(4)]

            if g >= 2:
                emit_mx_reds(5 * (g - 1))
            if g >= 1:
                emit_mx_items(g - 1)

            def a_ld(e, g=g, tiles=tiles):
                for t in tiles:
                    if ("xcons", t - 4) in ms:
                        e.wait_ge(sV.h, ms[("xcons", t - 4)])
                    dinc(e.dma_start(out=x_ring[:, (t % 4) * D:(t % 4 + 1) * D],
                                     in_=xb[t * 128:(t + 1) * 128, :]), sXA)
                if ("ropeuse", g - 2) in ms:
                    e.wait_ge(sV.h, ms[("ropeuse", g - 2)])
                sl = (g % 2) * 512
                dinc(e.dma_start(out=rc_s[:, sl:sl + 512],
                                 in_=rcd[:, g * 512:(g + 1) * 512]), sRP[g % 2])
                dinc(e.dma_start(out=rs_s[:, sl:sl + 512],
                                 in_=rsd[:, g * 512:(g + 1) * 512]), sRP[g % 2])
            block.sync(a_ld)

            def a_lnw(e, g=g, tiles=tiles):
                dwait(e, sXA)
                if g == 0:
                    dwait(e, sInit)
            block.vector(a_lnw)
            ln_group(tiles, "xcons", tr_guard=lambda t: tr_ms.get(t - 4))

            for t in tiles:
                def a_tr(e, t=t, g=g):
                    if g == 0 and t == tiles[0]:
                        dwait(e, sInit)
                    e.wait_ge(sV.h, xn_ms[t])
                    for half, bank, bn in ((0, psT0, "T0"), (1, psT1, "T1")):
                        if tb_v[bn]:
                            e.wait_ge(sV.h, tb_v[bn])
                        for i in range(8):
                            c = half * 8 + i
                            ii = e.transpose(bank[:, i * 128:(i + 1) * 128],
                                             xn_ring[:, (t % 4) * D + c * 128:(t % 4) * D + (c + 1) * 128],
                                             ident_s[:])
                        ii.then_inc(sP.h, 1); sP.v += 1
                        tb_p[bn] = sP.v
                    tr_ms[t] = sP.v
                block.tensor(a_tr)

                def a_ev(e, t=t, g=g):
                    if ("mmdone", g - 2) in ms:
                        e.wait_ge(sP.h, ms[("mmdone", g - 2)])
                    base = (g % 2) * 16 * 512
                    xv = xnfm[:, base:base + 16 * 512].rearrange("p (c n) -> p c n", c=16)
                    for half, bank, bn in ((0, psT0, "T0"), (1, psT1, "T1")):
                        e.wait_ge(sP.h, tb_p[bn])
                        dst = xv[:, half * 8:(half + 1) * 8, (t % 4) * 128:(t % 4 + 1) * 128]
                        ii = e.tensor_copy(dst, bank[:].rearrange("p (c n) -> p c n", c=8))
                        ii.then_inc(sV.h, 1); sV.v += 1
                        tb_v[bn] = sV.v
                    evac_ms[t] = sV.v
                block.vector(a_ev)

            base = (g % 2) * 16 * 512

            def a_kmm(e, g=g, base=base):
                e.wait_ge(sV.h, evac_ms[4 * g + 3])
                if g == 0:
                    dwait(e, sWI)
                if ("ropeK", g - 1) in rope_ms:
                    e.wait_ge(sV.h, rope_ms[("ropeK", g - 1)])
                for c in range(16):
                    ii = e.matmul(psK[:], wk_s[:, c * 128:(c + 1) * 128],
                                  xnfm[:, base + c * 512:base + (c + 1) * 512],
                                  start=(c == 0), stop=(c == 15))
                ii.then_inc(sP.h, 1); sP.v += 1
                mm_ms[("K", g)] = sP.v
            block.tensor(a_kmm)

            def a_krope(e, g=g):
                dwait(e, sRP[g % 2])
                e.wait_ge(sP.h, mm_ms[("K", g)])
                dwait(e, sKST[g % 2])
                kb = (g % 2) * 512
                rope6(e, psK, ktg[0:64, kb:kb + 512], ktg[64:128, kb:kb + 512], g)
                rope_ms[("ropeK", g)] = sV.v
            block.vector(a_krope)

            def a_kst(e, g=g):
                e.wait_ge(sV.h, rope_ms[("ropeK", g)])
                kb = (g % 2) * 512
                dinc(e.dma_start(out=k_dram[:, g * 512:(g + 1) * 512],
                                 in_=ktg[:, kb:kb + 512]), sKST[g % 2])
            block.gpsimd(a_kst)

            def a_vmm(e, g=g, base=base):
                if ("vev", g - 1) in rope_ms:
                    e.wait_ge(sV.h, rope_ms[("vev", g - 1)])
                for ts in range(4):
                    for c in range(16):
                        ii = e.matmul(psVb[:, ts * 128:(ts + 1) * 128],
                                      xnfm[:, base + c * 512 + ts * 128:base + c * 512 + (ts + 1) * 128],
                                      wv_s[:, c * 128:(c + 1) * 128],
                                      start=(ts == 0 and c == 0), stop=(c == 15),
                                      skip_group_check=True)
                ii.then_inc(sP.h, 1); sP.v += 1
                mm_ms[("V", g)] = sP.v
            block.tensor(a_vmm)

            def a_vev(e, g=g):
                e.wait_ge(sP.h, mm_ms[("V", g)])
                dwait(e, sVST[g % 2])
                vv = vtg[:].rearrange("p (c n) -> p c n", c=8)
                sl = (g % 2) * 4
                ii = e.tensor_copy(vv[:, sl:sl + 4, 0:128],
                                   psVb[:].rearrange("p (c n) -> p c n", c=4))
                ii.then_inc(sV.h, 1); sV.v += 1
                rope_ms[("vev", g)] = sV.v
            block.vector(a_vev)

            def a_vst(e, g=g):
                e.wait_ge(sV.h, rope_ms[("vev", g)])
                vv = vtg[:].rearrange("p (c n) -> p c n", c=8)
                sl = (g % 2) * 4
                dinc(e.dma_start(out=v_dram[4 * g:4 * g + 4].rearrange("c p d -> p c d"),
                                 in_=vv[:, sl:sl + 4, :]), sVST[g % 2])
            block.gpsimd(a_vst)

            for j in range(GQ):
                def a_qmm(e, g=g, j=j, base=base):
                    key = ("ropeQ", g, j - 2) if j >= 2 else ("ropeQ", g - 1, j + 2)
                    if key in rope_ms:
                        e.wait_ge(sV.h, rope_ms[key])
                    bankq = psQ0 if j % 2 == 0 else psQ1
                    for c in range(16):
                        ii = e.matmul(bankq[:], wq_s[:, c * 512 + j * 128:c * 512 + (j + 1) * 128],
                                      xnfm[:, base + c * 512:base + (c + 1) * 512],
                                      start=(c == 0), stop=(c == 15))
                    ii.then_inc(sP.h, 1); sP.v += 1
                    mm_ms[("Q", g, j)] = sP.v
                block.tensor(a_qmm)

                def a_qrope(e, g=g, j=j):
                    dwait(e, sRP[g % 2])
                    e.wait_ge(sP.h, mm_ms[("Q", g, j)])
                    dwait(e, sQST[j])
                    bankq = psQ0 if j % 2 == 0 else psQ1
                    qb = j * 512
                    rope6(e, bankq, qtg[0:64, qb:qb + 512], qtg[64:128, qb:qb + 512], g)
                    rope_ms[("ropeQ", g, j)] = sV.v
                block.vector(a_qrope)

                def a_qst(e, g=g, j=j):
                    e.wait_ge(sV.h, rope_ms[("ropeQ", g, j)])
                    dinc(e.dma_start(out=qt_dram[j][:, g * 512:(g + 1) * 512],
                                     in_=qtg[:, j * 512:(j + 1) * 512]), sQST[j])
                block.gpsimd(a_qst)

            ms[("mmdone", g)] = sP.v
            ms[("ropeuse", g)] = sV.v

        emit_mx_items(NG - 1)
        emit_mx_reds(5 * NG)

        def mx_fin_v(e):
            e.wait_ge(sV.h, ms[("mxred", mx_n[0] - 1)])
            ii = e.tensor_tensor(mkq[0:1, 2:3], mkq[0:1, 0:1], mkq[0:1, 1:2], op=ALU.mult)
            ii.then_inc(sV.h, 1); sV.v += 1
            ms["mxmul"] = sV.v
        block.vector(mx_fin_v)

        def mx_fin_a(e):
            e.wait_ge(sV.h, ms["mxmul"])
            e.activation(out=mkq[0:1, 3:4], in_=mkq[0:1, 2:3], func=AF.Ln).then_inc(sA.h, 1)
            sA.v += 1
            e.wait_ge(sA.h, sA.v)
            e.activation(out=mkq[0:1, 2:3], in_=mkq[0:1, 3:4], func=AF.Exp,
                         scale=0.5).then_inc(sA.h, 1)
            sA.v += 1
            ms["mxsqrt"] = sA.v
        block.scalar(mx_fin_a)

        def mx_fin_v2(e):
            e.wait_ge(sA.h, ms["mxsqrt"])
            ii = e.tensor_scalar(mkq[0:1, 2:3], mkq[0:1, 2:3], -SCALE, None, op0=ALU.mult)
            ii.then_inc(sV.h, 1); sV.v += 1
            ms["mxneg"] = sV.v
        block.vector(mx_fin_v2)

        def mx_bc(e):
            e.wait_ge(sV.h, ms["mxneg"])
            ii = e.matmul(psK[:, 0:1], ones_s[0:1, :], mkq[0:1, 2:3], start=True, stop=True)
            ii.then_inc(sP.h, 1); sP.v += 1
            ms["mxbc"] = sP.v
        block.tensor(mx_bc)

        def mx_bcv(e):
            e.wait_ge(sP.h, ms["mxbc"])
            ii = e.tensor_copy(bias_s[:], psK[:, 0:1])
            ii.then_inc(sV.h, 1); sV.v += 1
            ms["biasdone"] = sV.v
        block.vector(mx_bcv)

        esA.close()
        esA1sb.close()

        esB = ExitStack()
        psS_ = [esB.enter_context(nc.psum_tensor(f"psS{i}", [128, 1024], F32)) for i in range(2)]
        psO_ = [esB.enter_context(nc.psum_tensor(f"psO{i}", [128, 512], F32)) for i in range(3)]
        psT2b = esB.enter_context(nc.psum_tensor("psT2b", [128, 1024], BF16))

        mstrip_s = sbB("mstrip_s", [128, MASKW], F32)
        qt_sb = sbB("qt_sb", [128, 2 * 1024], BF16)
        ksc_sb = sbB("ksc_sb", [128, 2 * 1024], BF16)
        vsc_sb = sbB("vsc_sb", [128, 2 * 8 * 132], BF16)
        pt_sb = sbB("pt_sb", [128, 2 * 1024], BF16)
        linv_sb = sbB("linv_sb", [128, 8], F32)
        a_sb = sbB("a_sb", [128, 2 * 8 * 128], BF16)

        ln2tr_ms = {}
        tb2 = {"p": 0, "v": 0}

        def emit_a2_group(g2):
            tiles = [4 * g2 + i for i in range(4)]

            def a2_ld(e, g2=g2, tiles=tiles):
                for t in tiles:
                    if ("a2cons", t - 4) in ms:
                        e.wait_ge(sV.h, ms[("a2cons", t - 4)])
                    elif g2 == 0:
                        e.wait_ge(sV.h, ms[("xcons", 60 + (t % 4))])
                    dinc(e.dma_start(out=x_ring[:, (t % 4) * D:(t % 4 + 1) * D],
                                     in_=xown[t * 128:(t + 1) * 128, :]), sXA)
            block.sync(a2_ld)

            def a2_w(e, g2=g2, tiles=tiles):
                dwait(e, sXA)
            block.vector(a2_w)
            ln_group(tiles, None,
                     tr_guard=(lambda t: tr_ms.get(60 + (t % 4))) if g2 == 0 else None)

            for p0 in (0, 2):
                pair = tiles[p0:p0 + 2]
                cols = [t % 4 for t in pair]
                c0, c1 = min(cols), max(cols) + 1

                def a2_v1(e, pair=pair, c0=c0, c1=c1):
                    e.wait_ge(sV.h, xn_ms[pair[-1]])
                    ii = e.scalar_tensor_tensor(v2p[:, c0:c1], varp[:, c0:c1], -EPS,
                                                rstd[:, c0:c1], op0=ALU.add, op1=ALU.mult)
                    ii.then_inc(sV.h, 1); sV.v += 1
                    e.wait_ge(sV.h, sV.v)
                    for cc in range(c0, c1):
                        ii = e.tensor_scalar(v2p[:, cc:cc + 1], v2p[:, cc:cc + 1],
                                             rstd[:, cc:cc + 1], EPS,
                                             op0=ALU.mult, op1=ALU.add)
                    ii.then_inc(sV.h, 1); sV.v += 1
                    ms["a2mv"] = sV.v
                block.vector(a2_v1)
                mv = sV.v

                def a2_s(e, c0=c0, c1=c1, mv=mv):
                    e.wait_ge(sV.h, mv)
                    e.activation(out=r2f[:, c0:c1], in_=v2p[:, c0:c1],
                                 func=AF.Ln).then_inc(sA.h, 1)
                    sA.v += 1
                    e.wait_ge(sA.h, sA.v)
                    e.activation(out=r2f[:, c0:c1], in_=r2f[:, c0:c1],
                                 func=AF.Exp, scale=-0.5).then_inc(sA.h, 1)
                    sA.v += 1
                block.scalar(a2_s)
                mact = sA.v

                def a2_v2(e, pair=pair, mact=mact):
                    e.wait_ge(sA.h, mact)
                    for t in pair:
                        cc = t % 4
                        ii = e.tensor_tensor(r2f[:, cc:cc + 1], r2f[:, cc:cc + 1],
                                             rstd[:, cc:cc + 1], op=ALU.mult)
                    ii.then_inc(sV.h, 1); sV.v += 1
                    e.wait_ge(sV.h, sV.v)
                    for t in pair:
                        cc = t % 4
                        xs = x_ring[:, cc * D:(cc + 1) * D]
                        ii = e.tensor_scalar(ln2t1[:], xs, means[:, cc:cc + 1],
                                             r2f[:, cc:cc + 1], op0=ALU.subtract, op1=ALU.mult)
                        ii.then_inc(sV.h, 1); sV.v += 1
                        e.wait_ge(sV.h, sV.v)
                        ii = e.tensor_tensor(ln2t2[:], ln2t1[:], lnw_s[:], op=ALU.mult)
                        ii.then_inc(sV.h, 1); sV.v += 1
                        e.wait_ge(sV.h, sV.v)
                        if ("ln2tr", t - 2) in ln2tr_ms:
                            e.wait_ge(sP.h, ln2tr_ms[("ln2tr", t - 2)])
                        ii = e.tensor_tensor(ln2bf[:, (t % 2) * D:(t % 2 + 1) * D],
                                             ln2t2[:], lnb_s[:], op=ALU.add)
                        ii.then_inc(sV.h, 1); sV.v += 1
                        ms[("ln2bf", t)] = sV.v
                        ms[("a2cons", t)] = sV.v
                block.vector(a2_v2)
                for t in pair:
                    for half in (0, 1):
                        def a2_tr(e, t=t, half=half):
                            if half == 0:
                                e.wait_ge(sV.h, ms[("ln2bf", t)])
                            if tb2["v"]:
                                e.wait_ge(sV.h, tb2["v"])
                            for i in range(8):
                                c = half * 8 + i
                                ii = e.transpose(psT2b[:, i * 128:(i + 1) * 128],
                                                 ln2bf[:, (t % 2) * D + c * 128:(t % 2) * D + (c + 1) * 128],
                                                 ident_s[:])
                            ii.then_inc(sP.h, 1); sP.v += 1
                            tb2["p"] = sP.v
                        block.tensor(a2_tr)
                        ln2tr_ms[("ln2tr", t)] = sP.v

                        def a2_ev(e, t=t, half=half):
                            if half == 0:
                                dwait(e, sLNST[t % 2])
                            e.wait_ge(sP.h, tb2["p"])
                            dst = lnfm_st[:, (t % 2) * 2048 + half * 1024:(t % 2) * 2048 + (half + 1) * 1024] \
                                .rearrange("p (c n) -> p c n", c=8)
                            ii = e.tensor_copy(dst, psT2b[:].rearrange("p (c n) -> p c n", c=8))
                            ii.then_inc(sV.h, 1); sV.v += 1
                            tb2["v"] = sV.v
                        block.vector(a2_ev)
                        ms[("lnev", t)] = sV.v

                    def a2_st(e, t=t, g2=g2):
                        e.wait_ge(sV.h, ms[("lnev", t)])
                        lnview = ln2_dram[g2].rearrange("p (c n) -> p c n", c=16)
                        for half in range(2):
                            dst = lnview[:, half * 8:(half + 1) * 8, (t % 4) * 128:(t % 4 + 1) * 128]
                            src = lnfm_st[:, (t % 2) * 2048 + half * 1024:(t % 2) * 2048 + (half + 1) * 1024] \
                                .rearrange("p (c n) -> p c n", c=8)
                            dinc(e.dma_start(out=dst, in_=src), sLNST[t % 2])
                    block.gpsimd(a2_st)



        def b_join(e):
            for s_ in (sKST[0], sKST[1], sVST[0], sVST[1], sQST[0], sQST[1],
                       sQST[2], sQST[3], sLNST[0], sLNST[1]):
                dwait(e, s_)
            e.wait_ge(sV.h, ms["biasdone"])
            dinc(e.dma_start(out=mstrip_s[:], in_=mstripd[:]), sMSK)
        block.sync(b_join)

        def emit_sc_load(e, sc, occ, scld_ms):
            slot = sc % 2
            if occ is not None:
                e.wait_ge(sP.h, occ)
            dinc(e.dma_start(out=ksc_sb[:, slot * 1024:(slot + 1) * 1024],
                             in_=k_dram[:, sc * 1024:(sc + 1) * 1024]), sSC[slot])
            vv = vsc_sb[:].rearrange("p (c n) -> p c n", c=16)
            dinc(e.dma_start(out=vv[:, slot * 8:(slot + 1) * 8, :],
                             in_=v_dram[8 * sc:8 * sc + 8].rearrange("c p d -> p c d")),
                 sSC[slot])
            scld_ms[sc] = sSC[slot].v

        sc_occ = [None, None]
        qk_last = {}
        pv_last = {}
        exp_last = {}
        norm_ms = {}

        tj = 0
        for t in range(NT):
            nk = 8 * (t + 1)
            nsc = t + 1
            for j in range(GQ):
                slot = tj % 2
                sPb, sAb, sVb = sP.v, sA.v, sV.v
                scld_ms = {}
                qk_ms = {0: sPb + 1}
                pv_ms = {}
                for k in range(1, nk):
                    qk_ms[k] = sPb + 2 * k
                for k in range(nk):
                    pv_ms[k] = sPb + 2 * k + 3 if k < nk - 1 else sPb + 2 * nk
                exp_ms = {k: sAb + k + 1 for k in range(nk)}
                mask_ms = {k: sVb + (k - 8 * t) + 1 for k in range(8 * t, nk)}

                def b_ld(e, t=t, j=j, tj=tj, nsc=nsc, slot=slot,
                         pv_ms=pv_ms, occ0=sc_occ[0], occ1=sc_occ[1],
                         scld_ms=scld_ms):
                    if tj - 2 >= 0:
                        e.wait_ge(sP.h, qk_last[tj - 2])
                    dinc(e.dma_start(out=qt_sb[:, slot * 1024:(slot + 1) * 1024],
                                     in_=qt_dram[j][:, t * 1024:(t + 1) * 1024]), sQT[tj % 2])
                    for sc in range(min(2, nsc)):
                        emit_sc_load(e, sc, occ0 if sc == 0 else occ1, scld_ms)
                    for sc2 in range(2, nsc):
                        emit_sc_load(e, sc2, pv_ms[8 * (sc2 - 2) + 7], scld_ms)
                block.sync(b_ld)

                def b_pe(e, t=t, j=j, tj=tj, nk=nk, slot=slot,
                         qk_ms=qk_ms, pv_ms=pv_ms, exp_ms=exp_ms, scld_ms=scld_ms):
                    qa = qt_sb[:, slot * 1024:(slot + 1) * 1024]
                    vv = vsc_sb[:].rearrange("p (c n) -> p c n", c=16)

                    def emit_qk(k):
                        scs = (k // 8) % 2
                        if k == 0:
                            dwait(e, sQT[tj % 2])
                        if k % 8 == 0:
                            e.wait_ge(sSC[scs].h, scld_ms[k // 8])
                        if k >= 2:
                            e.wait_ge(sA.h, exp_ms[k - 2])
                        elif tj - 1 >= 0:
                            e.wait_ge(sA.h, exp_last[tj - 1])
                        ka = ksc_sb[:, scs * 1024 + (k % 8) * 128:scs * 1024 + (k % 8 + 1) * 128]
                        e.matmul(psS_[k % 2][:, 0:512], ka, qa[:, 0:512], start=True, stop=True)
                        ii = e.matmul(psS_[k % 2][:, 512:1024], ka, qa[:, 512:1024],
                                      start=True, stop=True)
                        ii.then_inc(sP.h, 1); sP.v += 1
                        assert sP.v == qk_ms[k], (tj, k, sP.v, qk_ms[k])

                    def emit_pv(k):
                        e.wait_ge(sA.h, exp_ms[k])
                        if k == 0 and tj - 1 >= 0:
                            e.wait_ge(sV.h, norm_ms[tj - 1])
                        va = vv[:, ((k // 8) % 2) * 8 + (k % 8), 0:129]
                        for qs in range(8):
                            bank = psO_[qs // 3]
                            off = 160 * (qs % 3)
                            ii = e.matmul(bank[:, off:off + 129],
                                          pt_sb[:, (k % 2) * 1024 + qs * 128:(k % 2) * 1024 + (qs + 1) * 128],
                                          va, start=(k == 0 and qs % 3 == 0),
                                          stop=(k == nk - 1), skip_group_check=True)
                        ii.then_inc(sP.h, 1); sP.v += 1
                        assert sP.v == pv_ms[k], (tj, k, sP.v, pv_ms[k])

                    emit_qk(0)
                    for k in range(nk):
                        if k + 1 < nk:
                            emit_qk(k + 1)
                        emit_pv(k)
                block.tensor(b_pe)
                qk_last[tj] = qk_ms[nk - 1]
                pv_last[tj] = pv_ms[nk - 1]
                sc_occ[0] = pv_ms[min(8 * (nsc - 1 - (nsc - 1) % 2) + 7, nk - 1)]
                if nsc >= 2:
                    sc_occ[1] = pv_ms[min(8 * (nsc - 1 - (nsc % 2)) + 7, nk - 1)]

                def b_msk(e, t=t, tj=tj, nk=nk, qk_ms=qk_ms):
                    if tj == 0:
                        dwait(e, sMSK)
                    for k in range(8 * t, nk):
                        e.wait_ge(sP.h, qk_ms[k])
                        off = 896 - (128 * k - 1024 * t)
                        ii = e.tensor_tensor(psS_[k % 2][:], psS_[k % 2][:],
                                             mstrip_s[:, off:off + 1024], op=ALU.add)
                        ii.then_inc(sV.h, 1); sV.v += 1
                        assert sV.v == mask_ms[k]
                block.vector(b_msk)

                def b_act(e, t=t, tj=tj, nk=nk, qk_ms=qk_ms, pv_ms=pv_ms,
                          mask_ms=mask_ms):
                    if tj == 0:
                        e.wait_ge(sV.h, ms["biasdone"])
                    for k in range(nk):
                        if k >= 8 * t:
                            e.wait_ge(sV.h, mask_ms[k])
                        else:
                            e.wait_ge(sP.h, qk_ms[k])
                        if k >= 2:
                            e.wait_ge(sP.h, pv_ms[k - 2])
                        elif tj - 1 >= 0:
                            e.wait_ge(sP.h, pv_last[tj - 1])
                        ii = e.activation(out=pt_sb[:, (k % 2) * 1024:(k % 2 + 1) * 1024],
                                          in_=psS_[k % 2][:], func=AF.Exp,
                                          bias=bias_s[:], scale=SCALE)
                        ii.then_inc(sA.h, 1); sA.v += 1
                block.scalar(b_act)
                exp_last[tj] = sA.v

                def b_tail(e, tj=tj, nk=nk, pv_ms=pv_ms):
                    e.wait_ge(sP.h, pv_ms[nk - 1])
                    dwait(e, sAST[tj % 2])
                    aslot = (tj % 2) * 8 * 128
                    for qs in range(8):
                        bank = psO_[qs // 3]
                        off = 160 * (qs % 3)
                        ii = e.reciprocal(linv_sb[:, qs:qs + 1], bank[:, off + 128:off + 129])
                    ii.then_inc(sV.h, 1); sV.v += 1
                    e.wait_ge(sV.h, sV.v)
                    for qs in range(8):
                        bank = psO_[qs // 3]
                        off = 160 * (qs % 3)
                        ii = e.tensor_scalar(a_sb[:, aslot + qs * 128:aslot + (qs + 1) * 128],
                                             bank[:, off:off + 128], linv_sb[:, qs:qs + 1],
                                             None, op0=ALU.mult)
                    ii.then_inc(sV.h, 1); sV.v += 1
                    norm_ms[tj] = sV.v
                block.vector(b_tail)

                def b_ast(e, t=t, j=j, tj=tj):
                    e.wait_ge(sV.h, norm_ms[tj])
                    aslot = (tj % 2) * 8
                    av = a_sb[:].rearrange("p (c n) -> p c n", c=16)
                    dst = a_dram[8 * t:8 * t + 8].rearrange("c p d -> p c d")[:, :, j * 128:(j + 1) * 128]
                    dinc(e.dma_start(out=dst, in_=av[:, aslot:aslot + 8, :]), sAST[tj % 2])
                block.gpsimd(b_ast)

                if tj in (7, 15, 23, 31):
                    part = tj // 8
                    def cc_gp(e, part=part):
                        dwait(e, sAST[0])
                        dwait(e, sAST[1])
                        a_flat = a_dram.rearrange("c p d -> (c p) d")
                        e.collective_compute(
                            "AllGather", ALU.bypass, replica_groups=[list(range(8))],
                            ins=[a_flat[part * 2048:(part + 1) * 2048, :].opt()],
                            outs=[ag_dram[part * 16384:(part + 1) * 16384, :].opt()],
                        ).then_inc(sC.h, 1)
                        sC.v += 1
                    block.gpsimd(cc_gp)

                tj += 1
                if tj in (3, 9, 15, 21):
                    emit_a2_group((tj - 3) // 6)

        esB.close()
        esBsb.close()




        esAsb.close()

        esQ = ExitStack()
        psq1 = esQ.enter_context(nc.psum_tensor("psq1", [128, 512], F32))
        psq2 = esQ.enter_context(nc.psum_tensor("psq2", [1, 512], F32))
        w_ring = sbQ("w_ring", [128, 2 * D], F32)
        wabs = sbQ("wabs", [128, 16], F32)
        wabs1 = sbQ("wabs1", [128, 1], F32)
        thr_b = sbQ("thr_b", [128, 1], F32)
        awbuf = sbQ("awbuf", [128, D], F32)
        sgnbuf = sbQ("sgnbuf", [128, D], BF16)
        gebuf = sbQ("gebuf", [128, D], BF16)
        wqbuf = sbQ("wqbuf", [128, 2 * D], BF16)

        for c in range(16):
            def q_ld1(e, c=c):
                if c == 0:
                    dwait(e, sAST[0])
                    dwait(e, sAST[1])
                    dwait(e, sLNST[0])
                    dwait(e, sLNST[1])
                if ("qred", c - 2) in ms:
                    e.wait_ge(sV.h, ms[("qred", c - 2)])
                dinc(e.dma_start(out=w_ring[:, (c % 2) * D:(c % 2 + 1) * D],
                                 in_=wf[c * 128:(c + 1) * 128, :]), sWR[c % 2])
            block.sync(q_ld1)

            def q_red(e, c=c):
                dwait(e, sWR[c % 2])
                ii = e.reduce_sum(wabs[:, c:c + 1], w_ring[:, (c % 2) * D:(c % 2 + 1) * D],
                                  axis=AX.X, apply_absolute_value=True)
                ii.then_inc(sV.h, 1); sV.v += 1
                ms[("qred", c)] = sV.v
            block.vector(q_red)

        def q_sum(e):
            e.wait_ge(sV.h, ms[("qred", 15)])
            ii = e.reduce_sum(wabs1[:], wabs[:], axis=AX.X)
            ii.then_inc(sV.h, 1); sV.v += 1
            ms["qsum"] = sV.v
        block.vector(q_sum)

        def q_mm(e):
            e.wait_ge(sV.h, ms["qsum"])
            ii = e.matmul(psq2[0:1, 0:1], wabs1[:], ones_s[:, 0:1], start=True, stop=True)
            ii.then_inc(sP.h, 1); sP.v += 1
            ms["qtot"] = sP.v
        block.tensor(q_mm)

        def q_thr(e):
            e.wait_ge(sP.h, ms["qtot"])
            ii = e.tensor_scalar(wabs1[0:1, 0:1], psq2[0:1, 0:1], 0.5 / (D * D),
                                 None, op0=ALU.mult)
            ii.then_inc(sV.h, 1); sV.v += 1
            ms["qthr"] = sV.v
        block.vector(q_thr)

        def q_bc(e):
            e.wait_ge(sV.h, ms["qthr"])
            ii = e.matmul(psq1[:, 0:1], ones_s[0:1, :], wabs1[0:1, 0:1], start=True, stop=True)
            ii.then_inc(sP.h, 1); sP.v += 1
            ms["qbc"] = sP.v
        block.tensor(q_bc)

        def q_bcv(e):
            e.wait_ge(sP.h, ms["qbc"])
            ii = e.tensor_copy(thr_b[:], psq1[:, 0:1])
            ii.then_inc(sV.h, 1); sV.v += 1
            ms["qthrb"] = sV.v
        block.vector(q_bcv)

        for c in range(16):
            def q_ld2(e, c=c):
                if c < 2:
                    e.wait_ge(sV.h, ms[("qred", 15)])
                if ("qwq", c - 2) in ms:
                    e.wait_ge(sV.h, ms[("qwq", c - 2)])
                dinc(e.dma_start(out=w_ring[:, (c % 2) * D:(c % 2 + 1) * D],
                                 in_=wf[c * 128:(c + 1) * 128, :]), sWR[c % 2])
            block.sync(q_ld2)

            def q_act(e, c=c):
                dwait(e, sWR[c % 2])
                if ("qwq", c - 1) in ms:
                    e.wait_ge(sV.h, ms[("qwq", c - 1)])
                e.activation(out=awbuf[:], in_=w_ring[:, (c % 2) * D:(c % 2 + 1) * D],
                             func=AF.Abs)
                e.activation(out=sgnbuf[:], in_=w_ring[:, (c % 2) * D:(c % 2 + 1) * D],
                             func=AF.Sign).then_inc(sA.h, 1)
                sA.v += 1
                ms[("qabs", c)] = sA.v
            block.scalar(q_act)

            def q_geq(e, c=c):
                e.wait_ge(sA.h, ms[("qabs", c)])
                if c == 0:
                    e.wait_ge(sV.h, ms["qthrb"])
                ii = e.tensor_scalar(gebuf[:], awbuf[:], thr_b[:], None, op0=ALU.is_ge)
                ii.then_inc(sV.h, 1); sV.v += 1
                e.wait_ge(sV.h, sV.v)
                dwait(e, sWST[c % 2])
                ii = e.tensor_tensor(wqbuf[:, (c % 2) * D:(c % 2 + 1) * D], gebuf[:],
                                     sgnbuf[:], op=ALU.mult)
                ii.then_inc(sV.h, 1); sV.v += 1
                ms[("qwq", c)] = sV.v
            block.vector(q_geq)

            def q_st(e, c=c):
                e.wait_ge(sV.h, ms[("qwq", c)])
                dinc(e.dma_start(out=wfq_dram[c * 128:(c + 1) * 128, :],
                                 in_=wqbuf[:, (c % 2) * D:(c % 2 + 1) * D]), sWST[c % 2])
            block.gpsimd(q_st)

        esQ.close()
        esQsb.close()

        esC = ExitStack()
        psT0c = esC.enter_context(nc.psum_tensor("psT0c", [128, 1024], BF16))
        psT1c = esC.enter_context(nc.psum_tensor("psT1c", [128, 1024], BF16))
        psSo = [esC.enter_context(nc.psum_tensor(f"psSo{i}", [128, 512], F32)) for i in range(2)]
        psGo = [esC.enter_context(nc.psum_tensor(f"psGo{i}", [128, 512], F32)) for i in range(2)]

        atok = sbC("atok", [128, 2 * D], BF16)
        afm = sbC("afm", [128, 16 * 512], BF16)
        lnfm_g = sbC("lnfm_g", [128, 2 * 8192], BF16)
        wo_sb = sbC("wo_sb", [128, 2 * 8192], BF16)
        wfq_sb = sbC("wfq_sb", [128, 2 * 8192], BF16)
        xg_sb = sbC("xg_sb", [128, 4 * D], F32)
        g_sl = sbC("g_sl", [128, 512], F32)
        sg_sl = sbC("sg_sl", [128, 512], F32)
        out_sb = sbC("out_sb", [128, 4 * D], F32)

        tbc_v = {"T0": 0, "T1": 0}
        tbc_p = {"T0": 0, "T1": 0}
        last_bank = [None, None]
        last_gsl = None

        def c_wld_emit(e, dq_seq):
            g2w, dqw = dq_seq // 4, dq_seq % 4
            slot = (dq_seq % 2) * 8192
            if dq_seq == 0:
                dwait(e, sWI)
                dwait(e, sWST[0])
                dwait(e, sWST[1])
            if ("wU", dq_seq - 2) in ms:
                e.wait_ge(sP.h, ms[("wU", dq_seq - 2)])
            dinc(e.dma_start(out=wo_sb[:, slot:slot + 8192].rearrange("p (c n) -> p c n", c=16),
                             in_=wob_dram[:, dqw * 512:(dqw + 1) * 512]
                             .rearrange("(c p) n -> p c n", p=128)), sWL[dq_seq % 2])
            dinc(e.dma_start(out=wfq_sb[:, slot:slot + 8192].rearrange("p (c n) -> p c n", c=16),
                             in_=wfq_dram[:, dqw * 512:(dqw + 1) * 512]
                             .rearrange("(c p) n -> p c n", p=128)), sWL[dq_seq % 2])

        for g2 in range(4):
            def c_lnld(e, g2=g2):
                if g2 == 0:
                    dwait(e, sWST[0])
                    dwait(e, sWST[1])
                    dwait(e, sLNST[0])
                    dwait(e, sLNST[1])
                    e.wait_ge(sV.h, ms[("qwq", 15)])
                if ("lnfmU", g2 - 2) in ms:
                    e.wait_ge(sP.h, ms[("lnfmU", g2 - 2)])
                dinc(e.dma_start(out=lnfm_g[:, (g2 % 2) * 8192:(g2 % 2 + 1) * 8192],
                                 in_=ln2_dram[g2]), sLF[g2 % 2])
            block.sync(c_lnld)

            for u in range(4):
                tt = g2 * 4 + u

                def c_gather(e, tt=tt):
                    if tt == 0:
                        dwait(e, sInit)
                        e.wait_ge(sC.h, 4)
                        dwait(e, sWST[0])
                        dwait(e, sWST[1])
                    if ("atokU", tt - 2) in ms:
                        e.wait_ge(sP.h, ms[("atokU", tt - 2)])
                    slot = (tt % 2) * D
                    for eh in range(4):
                        dinc(e.indirect_dma_start(
                            out=atok[:, slot + eh * 512:slot + (eh + 1) * 512],
                            out_offset=None, in_=ag_dram,
                            in_offset=IndirectOffsetOnAxis(
                                ap=idx_s[:, tt * 4 + eh:tt * 4 + eh + 1], axis=0),
                        ), sAG[tt % 2])
                block.gpsimd(c_gather)

                def c_tr(e, tt=tt):
                    if tt == 0:
                        e.wait_ge(sV.h, ms[("qwq", 15)])
                    dwait(e, sAG[tt % 2])
                    slot = (tt % 2) * D
                    for half, bank, bn in ((0, psT0c, "T0"), (1, psT1c, "T1")):
                        if tbc_v[bn]:
                            e.wait_ge(sV.h, tbc_v[bn])
                        for i in range(8):
                            c = half * 8 + i
                            ii = e.transpose(bank[:, i * 128:(i + 1) * 128],
                                             atok[:, slot + c * 128:slot + (c + 1) * 128],
                                             ident_s[:])
                        ii.then_inc(sP.h, 1); sP.v += 1
                        tbc_p[bn] = sP.v
                    ms[("atokU", tt)] = sP.v
                block.tensor(c_tr)

                def c_ev(e, tt=tt, u=u, g2=g2):
                    if ("cmm", g2 - 1) in ms:
                        e.wait_ge(sP.h, ms[("cmm", g2 - 1)])
                    av = afm[:].rearrange("p (c n) -> p c n", c=16)
                    for half, bank, bn in ((0, psT0c, "T0"), (1, psT1c, "T1")):
                        e.wait_ge(sP.h, tbc_p[bn])
                        dst = av[:, half * 8:(half + 1) * 8, u * 128:(u + 1) * 128]
                        ii = e.tensor_copy(dst, bank[:].rearrange("p (c n) -> p c n", c=8))
                        ii.then_inc(sV.h, 1); sV.v += 1
                        tbc_v[bn] = sV.v
                    ms[("afmev", tt)] = sV.v
                block.vector(c_ev)

            def c_xld(e, g2=g2):
                for ts in range(4):
                    tt = g2 * 4 + ts
                    if ("xgU", tt - 4) in ms:
                        e.wait_ge(sV.h, ms[("xgU", tt - 4)])
                    dinc(e.dma_start(out=xg_sb[:, ts * D:(ts + 1) * D],
                                     in_=xown[tt * 128:(tt + 1) * 128, :]), sXG[ts % 2])
            block.sync(c_xld)

            if g2 == 0:
                def c_wld0(e):
                    c_wld_emit(e, 0)
                    c_wld_emit(e, 1)
                block.sync(c_wld0)

            for dq in range(4):
                dq_seq = g2 * 4 + dq
                for ts in range(4):
                    tt = g2 * 4 + ts
                    par = ts % 2
                    pss = psSo[par]
                    psg = psGo[par]

                    def c_mm(e, g2=g2, ts=ts, dq=dq, dq_seq=dq_seq, par=par,
                             pss=pss, psg=psg, tt=tt):
                        if ts == 0 and dq == 0:
                            e.wait_ge(sV.h, ms[("afmev", g2 * 4 + 3)])
                            dwait(e, sLF[g2 % 2])
                        if ts == 0:
                            dwait(e, sWL[dq_seq % 2])
                        lb = last_bank[par]
                        if lb is not None:
                            e.wait_ge(sV.h, lb[0])
                            e.wait_ge(sA.h, lb[1])
                        wslot = (dq_seq % 2) * 8192
                        lnb_ = (g2 % 2) * 8192
                        for c in range(16):
                            ii = e.matmul(pss[:], afm[:, c * 512 + ts * 128:c * 512 + (ts + 1) * 128],
                                          wo_sb[:, wslot + c * 512:wslot + (c + 1) * 512],
                                          start=(c == 0), stop=(c == 15))
                        ii.then_inc(sP.h, 1); sP.v += 1
                        ms[("cmms", g2, ts, dq)] = sP.v
                        for c in range(16):
                            ii = e.matmul(psg[:],
                                          lnfm_g[:, lnb_ + c * 512 + ts * 128:lnb_ + c * 512 + (ts + 1) * 128],
                                          wfq_sb[:, wslot + c * 512:wslot + (c + 1) * 512],
                                          start=(c == 0), stop=(c == 15))
                        ii.then_inc(sP.h, 1); sP.v += 1
                        ms[("cmmg", g2, ts, dq)] = sP.v
                        if ts == 3:
                            ms[("wU", dq_seq)] = sP.v
                        if ts == 3 and dq == 3:
                            ms[("cmm", g2)] = sP.v
                            ms[("lnfmU", g2)] = sP.v
                    block.tensor(c_mm)

                    def c_silu(e, g2=g2, ts=ts, dq=dq, psg=psg, lg=last_gsl):
                        e.wait_ge(sP.h, ms[("cmmg", g2, ts, dq)])
                        if lg is not None:
                            e.wait_ge(sV.h, lg)
                        e.activation(out=g_sl[:], in_=psg[:], func=AF.Silu).then_inc(sA.h, 1)
                        sA.v += 1
                        ms[("csilu", g2, ts, dq)] = sA.v
                    block.scalar(c_silu)

                    def c_gate(e, g2=g2, ts=ts, dq=dq, par=par, pss=pss, tt=tt):
                        e.wait_ge(sA.h, ms[("csilu", g2, ts, dq)])
                        e.wait_ge(sP.h, ms[("cmms", g2, ts, dq)])
                        if dq == 0:
                            dwait(e, sXG[ts % 2])
                            dwait(e, sYS[ts % 2])
                        ii = e.tensor_tensor(sg_sl[:], pss[:], g_sl[:], op=ALU.mult)
                        ii.then_inc(sV.h, 1); sV.v += 1
                        e.wait_ge(sV.h, sV.v)
                        ii = e.tensor_tensor(
                            out_sb[:, ts * D + dq * 512:ts * D + (dq + 1) * 512],
                            sg_sl[:],
                            xg_sb[:, ts * D + dq * 512:ts * D + (dq + 1) * 512],
                            op=ALU.add)
                        ii.then_inc(sV.h, 1); sV.v += 1
                        ms[("cout", g2, ts, dq)] = sV.v
                        if dq == 3:
                            ms[("xgU", tt)] = sV.v
                    block.vector(c_gate)
                    last_bank[par] = (sV.v, sA.v)
                    last_gsl = sV.v

                if dq_seq + 2 < 16:
                    def c_wldn(e, nseq=dq_seq + 2):
                        c_wld_emit(e, nseq)
                    block.sync(c_wldn)

            for ts in range(4):
                tt = g2 * 4 + ts

                def c_yst(e, g2=g2, ts=ts, tt=tt):
                    e.wait_ge(sV.h, ms[("cout", g2, ts, 3)])
                    dinc(e.dma_start(out=y[tt * 128:(tt + 1) * 128, :],
                                     in_=out_sb[:, ts * D:(ts + 1) * D]), sYS[ts % 2])
                block.gpsimd(c_yst)

        def fin(e):
            dwait(e, sYS[0])
            dwait(e, sYS[1])
        block.gpsimd(fin)

        esC.close()
        esCsb.close()
    es.close()
    return nc


_CACHE = {}


def _host_consts():
    t = np.arange(S, dtype=np.float64)
    inv = 1.0 / (10000.0 ** (np.arange(0, HD, 2, dtype=np.float64) / HD))
    fr = inv[:, None] * t[None, :]
    rc = np.cos(fr).astype(np.float32)
    rs = np.sin(fr).astype(np.float32)
    u = np.arange(MASKW, dtype=np.int64)[None, :] - 896
    si = np.arange(128, dtype=np.int64)[:, None]
    mstrip = np.where(u < si, np.float32(NEGBIG), np.float32(0.0)).astype(np.float32)
    ident = np.eye(128, dtype=np.float32).astype(BF)
    ones = np.ones((128, 128), dtype=np.float32)
    return rc, rs, mstrip, ident, ones


def kernel(L0_attn_x, L0_attn_Wq, L0_attn_Wk, L0_attn_Wv, L0_attn_Wo,
           L0_ffn_W, L0_ffn_ln_w, L0_ffn_ln_b):
    x = np.ascontiguousarray(np.asarray(L0_attn_x, dtype=np.float32))
    Wq = np.ascontiguousarray(np.asarray(L0_attn_Wq, dtype=np.float32))
    Wk = np.ascontiguousarray(np.asarray(L0_attn_Wk, dtype=np.float32))
    Wv = np.ascontiguousarray(np.asarray(L0_attn_Wv, dtype=np.float32))
    Wo = np.ascontiguousarray(np.asarray(L0_attn_Wo, dtype=np.float32))
    W = np.ascontiguousarray(np.asarray(L0_ffn_W, dtype=np.float32))
    lnw = np.asarray(L0_ffn_ln_w, dtype=np.float32)
    lnb = np.asarray(L0_ffn_ln_b, dtype=np.float32)

    if "nc" not in _CACHE:
        _CACHE["nc"] = build_nc()
    nc = _CACHE["nc"]

    rc, rs, mstrip, ident, ones = _host_consts()
    lnwb = np.ascontiguousarray(np.broadcast_to(lnw[None, :], (128, D)))
    lnbb = np.ascontiguousarray(np.broadcast_to(lnb[None, :], (128, D)))

    in_maps = []
    for c in range(8):
        b, r, Q = c // 4, c % 4, c % 4
        idx = np.zeros((128, 64), dtype=np.uint32)
        p = np.arange(128, dtype=np.uint32)
        for uu in range(16):
            for eh in range(4):
                ch = 16 * Q + uu
                part = ch // 16
                idx[:, uu * 4 + eh] = (part * 16384 + (4 * b + eh) * 2048
                                       + (ch - 16 * part) * 128 + p)
        in_maps.append({
            "xb": x[b],
            "xown": np.ascontiguousarray(x[b, 2048 * Q:2048 * (Q + 1)]),
            "wq": np.ascontiguousarray(Wq[:, 512 * r:512 * (r + 1)]),
            "wk": np.ascontiguousarray(Wk[:, 128 * r:128 * (r + 1)]),
            "wv": np.ascontiguousarray(Wv[:, 128 * r:128 * (r + 1)]),
            "wo": Wo, "wf": W,
            "lnwb": lnwb, "lnbb": lnbb,
            "rcd": rc, "rsd": rs, "mstripd": mstrip,
            "identd": ident, "onesd": ones,
            "agidxd": idx,
        })

    res = run_bass_kernel_spmd(nc, in_maps, core_ids=list(range(8)))
    out = np.empty((B, S, D), dtype=np.float32)
    for c in range(8):
        b, Q = c // 4, c % 4
        out[b, 2048 * Q:2048 * (Q + 1), :] = res.results[c]["y"]
    return out


# version 14
# speedup vs baseline: 1.0177x; 1.0177x over previous
import numpy as np
import ml_dtypes
from contextlib import ExitStack

import concourse.bass as bass
import concourse.mybir as mybir
from concourse.bass import IndirectOffsetOnAxis
from concourse.bass_utils import run_bass_kernel_spmd

F32 = mybir.dt.float32
BF16 = mybir.dt.bfloat16
U32 = mybir.dt.uint32
AF = mybir.ActivationFunctionType
ALU = mybir.AluOpType
AX = mybir.AxisListType
BF = ml_dtypes.bfloat16

B, S, D = 2, 8192, 2048
H, KVH, HD = 16, 4, 128
GQ = H // KVH
SCALE = HD ** -0.5
EPS = 1e-5
NG = S // 512
NT = 8
MASKW = 1920
NEGBIG = -1.0e6


class Sem:
    __slots__ = ("h", "v")

    def __init__(self, h):
        self.h = h
        self.v = 0


def build_nc():
    nc = bass.Bass("TRN2", target_bir_lowering=False)
    es = ExitStack()

    xb = nc.dram_tensor("xb", [S, D], F32, kind="ExternalInput").ap()
    xown = nc.dram_tensor("xown", [2048, D], F32, kind="ExternalInput").ap()
    wq = nc.dram_tensor("wq", [D, GQ * HD], F32, kind="ExternalInput").ap()
    wk = nc.dram_tensor("wk", [D, HD], F32, kind="ExternalInput").ap()
    wv = nc.dram_tensor("wv", [D, HD], F32, kind="ExternalInput").ap()
    wo = nc.dram_tensor("wo", [D, D], F32, kind="ExternalInput").ap()
    wf = nc.dram_tensor("wf", [D, D], F32, kind="ExternalInput").ap()
    lnwb = nc.dram_tensor("lnwb", [128, D], F32, kind="ExternalInput").ap()
    lnbb = nc.dram_tensor("lnbb", [128, D], F32, kind="ExternalInput").ap()
    rcd = nc.dram_tensor("rcd", [64, S], F32, kind="ExternalInput").ap()
    rsd = nc.dram_tensor("rsd", [64, S], F32, kind="ExternalInput").ap()
    mstripd = nc.dram_tensor("mstripd", [128, MASKW], BF16, kind="ExternalInput").ap()
    identd = nc.dram_tensor("identd", [128, 128], BF16, kind="ExternalInput").ap()
    onesd = nc.dram_tensor("onesd", [128, 128], F32, kind="ExternalInput").ap()
    agidxd = nc.dram_tensor("agidxd", [128, 64], U32, kind="ExternalInput").ap()
    y = nc.dram_tensor("y", [2048, D], F32, kind="ExternalOutput").ap()

    k_dram = nc.dram_tensor("k_dram", [128, S], BF16).ap()
    v_dram = nc.dram_tensor("v_dram", [64, 128, 132], BF16).ap()
    qt_dram = nc.dram_tensor("qt_dram", [GQ, 128, S], BF16).ap()
    a_dram = nc.dram_tensor("a_dram", [64, 128, 512], BF16).ap()
    ag_dram = nc.dram_tensor("ag_dram", [8 * 8192, 512], BF16).ap()
    ln2_dram = nc.dram_tensor("ln2_dram", [4, 128, 8192], BF16).ap()
    wob_dram = nc.dram_tensor("wob_dram", [D, D], BF16).ap()
    wfq_dram = nc.dram_tensor("wfq_dram", [D, D], BF16).ap()

    sP = Sem(es.enter_context(nc.semaphore("sP")))
    sV = Sem(es.enter_context(nc.semaphore("sV")))
    sA = Sem(es.enter_context(nc.semaphore("sA")))
    sC = Sem(es.enter_context(nc.semaphore("sC")))

    def dsem(name):
        return Sem(es.enter_context(nc.semaphore(name)))

    sInit = dsem("sInit")
    sWI = dsem("sWI")
    sXA = dsem("sXA")
    sRP = [dsem(f"sRP{i}") for i in range(2)]
    sMX = [dsem(f"sMX{i}") for i in range(4)]
    sKST = [dsem(f"sKST{i}") for i in range(2)]
    sVST = [dsem(f"sVST{i}") for i in range(2)]
    sQST = [dsem(f"sQST{i}") for i in range(4)]
    sLNST = [dsem(f"sLNST{i}") for i in range(2)]
    sQT = [dsem(f"sQT{i}") for i in range(2)]
    sSC = [dsem(f"sSC{i}") for i in range(2)]
    sAST = [dsem(f"sAST{i}") for i in range(2)]
    sWR = [dsem(f"sWR{i}") for i in range(2)]
    sWST = [dsem(f"sWST{i}") for i in range(2)]
    sLF = [dsem(f"sLF{i}") for i in range(2)]
    sWL = [dsem(f"sWL{i}") for i in range(2)]
    sXG = [dsem(f"sXG{i}") for i in range(2)]
    sAG = [dsem(f"sAG{i}") for i in range(2)]
    sYS = [dsem(f"sYS{i}") for i in range(2)]
    sMSK = dsem("sMSK")

    def dinc(instr, sem):
        instr.then_inc(sem.h, 16)
        sem.v += 16

    def dwait(e, sem):
        e.wait_ge(sem.h, sem.v)

    def sb(name, shape, dt):
        return es.enter_context(nc.sbuf_tensor(name, shape, dt))

    esAsb = ExitStack()
    esBsb = ExitStack()
    esQsb = ExitStack()
    esCsb = ExitStack()

    esA1sb = ExitStack()

    def sbA(name, shape, dt):
        return esAsb.enter_context(nc.sbuf_tensor(name, shape, dt))

    def sbA1(name, shape, dt):
        return esA1sb.enter_context(nc.sbuf_tensor(name, shape, dt))

    def sbB(name, shape, dt):
        return esBsb.enter_context(nc.sbuf_tensor(name, shape, dt))

    def sbQ(name, shape, dt):
        return esQsb.enter_context(nc.sbuf_tensor(name, shape, dt))

    def sbC(name, shape, dt):
        return esCsb.enter_context(nc.sbuf_tensor(name, shape, dt))

    ident_s = sb("ident_s", [128, 128], BF16)
    ones_s = sb("ones_s", [128, 128], F32)
    bias_s = sb("bias_s", [128, 1], F32)
    idx_s = sb("idx_s", [128, 64], U32)

    w_ring = sbQ("w_ring", [128, 2 * D], F32)
    wabs = sbQ("wabs", [128, 16], F32)
    wabs1 = sbQ("wabs1", [128, 1], F32)
    thr_b = sbQ("thr_b", [128, 1], F32)
    thrn_b = sbQ("thrn_b", [128, 1], F32)

    lnw_s = sbA("lnw_s", [128, D], F32)
    lnb_s = sbA("lnb_s", [128, D], F32)
    x_ring = sbA("x_ring", [128, 4 * D], F32)
    xn_ring = sbA("xn_ring", [128, 4 * D], BF16)
    sums = sbA("sums", [128, 4], F32)
    means = sbA("means", [128, 4], F32)
    varp = sbA("varp", [128, 4], F32)
    rstd = sbA("rstd", [128, 4], F32)
    v2p = sbA("v2p", [128, 4], F32)
    r2f = sbA("r2f", [128, 4], F32)
    ln2t1 = sbA("ln2t1", [128, D], F32)
    ln2t2 = sbA("ln2t2", [128, D], F32)
    ln2bf = sbA("ln2bf", [128, 2 * D], BF16)
    lnfm_st = sbA("lnfm_st", [128, 2 * 2048], BF16)
    xnfm = sbA1("xnfm", [128, 2 * 16 * 512], BF16)
    wq_s = sbA1("wq_s", [128, 16 * 512], BF16)
    wk_s = sbA1("wk_s", [128, 16 * 128], BF16)
    wv_s = sbA1("wv_s", [128, 16 * 128], BF16)
    rc_s = sbA1("rc_s", [64, 2 * 512], F32)
    rs_s = sbA1("rs_s", [64, 2 * 512], F32)
    rt1 = sbA1("rt1", [64, 512], F32)
    rt2 = sbA1("rt2", [64, 512], F32)
    ktg = sbA1("ktg", [128, 2 * 512], BF16)
    vtg = sbA1("vtg", [128, 2 * 4 * 132], BF16)
    qtg = sbA1("qtg", [128, 4 * 512], BF16)
    mx_in = sbA1("mx_in", [128, 2 * 512], BF16)
    sqt = sbA1("sqt", [128, 2 * 512], F32)
    mxsb = sbA1("mxsb", [1, 12 * 512], BF16)
    mkq = sbA1("mkq", [1, 4], F32)

    ms = {}

    with nc.Block() as block:
        def init_sync(e):
            for dst, srcp in ((ident_s, identd), (ones_s, onesd), (lnw_s, lnwb),
                              (lnb_s, lnbb), (idx_s, agidxd)):
                dinc(e.dma_start(out=dst[:], in_=srcp[:]), sInit)
        block.sync(init_sync)

        def init_gp(e):
            dinc(e.dma_start(out=wq_s[:].rearrange("p (c n) -> p c n", c=16),
                             in_=wq.rearrange("(c p) n -> p c n", p=128)), sWI)
            dinc(e.dma_start(out=wk_s[:].rearrange("p (c n) -> p c n", c=16),
                             in_=wk.rearrange("(c p) n -> p c n", p=128)), sWI)
            dinc(e.dma_start(out=wv_s[:].rearrange("p (c n) -> p c n", c=16),
                             in_=wv.rearrange("(c p) n -> p c n", p=128)), sWI)
            dinc(e.dma_start(out=wob_dram, in_=wo), sWI)
        block.gpsimd(init_gp)

        def init_v(e):
            e.memset(mkq[:], 1e-12)
            e.memset(vtg[:], 0.0)
            vv = vtg[:].rearrange("p (c n) -> p c n", c=8)
            e.memset(vv[:, :, 128:129], 1.0).then_inc(sV.h, 1)
            sV.v += 1
        block.vector(init_v)

        esA = ExitStack()
        psT0 = esA.enter_context(nc.psum_tensor("psT0", [128, 1024], BF16))
        psT1 = esA.enter_context(nc.psum_tensor("psT1", [128, 1024], BF16))
        psQ0 = esA.enter_context(nc.psum_tensor("psQ0", [128, 512], F32))
        psQ1 = esA.enter_context(nc.psum_tensor("psQ1", [128, 512], F32))
        psK = esA.enter_context(nc.psum_tensor("psK", [128, 512], F32))
        psVb = esA.enter_context(nc.psum_tensor("psVb", [128, 512], F32))
        psM0 = esA.enter_context(nc.psum_tensor("psM0", [1, 512], F32))
        psM1 = esA.enter_context(nc.psum_tensor("psM1", [1, 512], F32))

        x_cons = {}
        xn_ms = {}
        evac_ms = {}
        tb_v = {"T0": 0, "T1": 0}
        tb_p = {"T0": 0, "T1": 0}
        mm_ms = {}
        rope_ms = {}
        st_ms = {}
        ld_ms = {}
        tr_ms = {}

        def ln_group(tiles, consume_key, tr_guard=None):
            for p0 in (0, 2):
                pair = tiles[p0:p0 + 2]
                cols = [t % 4 for t in pair]
                c0, c1 = min(cols), max(cols) + 1

                def v1(e, pair=pair, cols=cols):
                    for t, cc in zip(pair, cols):
                        xs = x_ring[:, (t % 4) * D:(t % 4 + 1) * D]
                        ii = e.reduce_sum(sums[:, cc:cc + 1], xs, axis=AX.X)
                    ii.then_inc(sV.h, 1); sV.v += 1
                    e.wait_ge(sV.h, sV.v)
                    for t, cc in zip(pair, cols):
                        ii = e.tensor_scalar(means[:, cc:cc + 1], sums[:, cc:cc + 1],
                                             1.0 / D, None, op0=ALU.mult)
                    ii.then_inc(sV.h, 1); sV.v += 1
                    e.wait_ge(sV.h, sV.v)
                    for t, cc in zip(pair, cols):
                        xs = x_ring[:, (t % 4) * D:(t % 4 + 1) * D]
                        ii = e.scalar_tensor_tensor(ln2t1[:], xs, means[:, cc:cc + 1], xs,
                                                    op0=ALU.subtract, op1=ALU.mult,
                                                    accum_out=sums[:, cc:cc + 1])
                    ii.then_inc(sV.h, 1); sV.v += 1
                    e.wait_ge(sV.h, sV.v)
                    for t, cc in zip(pair, cols):
                        ii = e.tensor_scalar(varp[:, cc:cc + 1], sums[:, cc:cc + 1],
                                             1.0 / D, EPS, op0=ALU.mult, op1=ALU.add)
                    ii.then_inc(sV.h, 1); sV.v += 1
                    e.wait_ge(sV.h, sV.v)
                    ii = e.reciprocal(rstd[:, c0:c1], varp[:, c0:c1])
                    ii.then_inc(sV.h, 1); sV.v += 1
                block.vector(v1)
                mv = sV.v

                def s1(e, c0=c0, c1=c1, mv=mv):
                    e.wait_ge(sV.h, mv)
                    e.activation(out=rstd[:, c0:c1], in_=rstd[:, c0:c1],
                                 func=AF.Sqrt).then_inc(sA.h, 1)
                    sA.v += 1
                block.scalar(s1)
                ma = sA.v

                def v2(e, pair=pair, cols=cols, ma=ma):
                    e.wait_ge(sA.h, ma)
                    if tr_guard is not None:
                        for t in pair:
                            gv = tr_guard(t)
                            if gv:
                                e.wait_ge(sP.h, gv)
                    for t, cc in zip(pair, cols):
                        xs = x_ring[:, (t % 4) * D:(t % 4 + 1) * D]
                        ii = e.tensor_scalar(xn_ring[:, (t % 4) * D:(t % 4 + 1) * D], xs,
                                             means[:, cc:cc + 1], rstd[:, cc:cc + 1],
                                             op0=ALU.subtract, op1=ALU.mult)
                    ii.then_inc(sV.h, 1); sV.v += 1
                    for t in pair:
                        xn_ms[t] = sV.v
                        if consume_key is not None:
                            ms[(consume_key, t)] = sV.v
                block.vector(v2)

        def rope6(e, psrc, d0, d1, g):
            sl = (g % 2) * 512
            cs = rc_s[:, sl:sl + 512]
            sn = rs_s[:, sl:sl + 512]
            e.tensor_tensor(rt1[:], psrc[0:64, :], cs, op=ALU.mult)
            ii = e.tensor_tensor(rt2[:], psrc[64:128, :], sn, op=ALU.mult)
            ii.then_inc(sV.h, 1); sV.v += 1
            e.wait_ge(sV.h, sV.v)
            ii = e.tensor_tensor(d0, rt1[:], rt2[:], op=ALU.subtract)
            ii.then_inc(sV.h, 1); sV.v += 1
            e.wait_ge(sV.h, sV.v)
            e.tensor_tensor(rt1[:], psrc[0:64, :], sn, op=ALU.mult)
            ii = e.tensor_tensor(rt2[:], psrc[64:128, :], cs, op=ALU.mult)
            ii.then_inc(sV.h, 1); sV.v += 1
            e.wait_ge(sV.h, sV.v)
            ii = e.tensor_tensor(d1, rt1[:], rt2[:], op=ALU.add)
            ii.then_inc(sV.h, 1); sV.v += 1

        mx_n = [0]
        mx_rn = [0]

        def emit_mx_reds(upto):
            while mx_rn[0] < upto:
                n = mx_rn[0]
                mx_rn[0] += 1
                kind = "k" if n % 5 == 0 else "q"

                def mx_red(e, n=n, kind=kind):
                    e.wait_ge(sA.h, ms[("mxcp", n)])
                    acc = mkq[0:1, 0:1] if kind == "k" else mkq[0:1, 1:2]
                    scr = mkq[0:1, 2:3]
                    sl = (n % 12) * 512
                    ii = e.reduce_max(scr, mxsb[0:1, sl:sl + 512], axis=AX.X)
                    ii.then_inc(sV.h, 1); sV.v += 1
                    e.wait_ge(sV.h, sV.v)
                    ii = e.tensor_tensor(acc, acc, scr, op=ALU.max)
                    ii.then_inc(sV.h, 1); sV.v += 1
                    ms[("mxred", n)] = sV.v
                block.vector(mx_red)

        def emit_mx_items(g):
            for item in [("k", 0, g)] + [("q", j, g) for j in range(GQ)]:
                kind, j, i = item
                n = mx_n[0]
                mx_n[0] += 1
                slot = n % 2
                pslot = n % 2

                def mx_ld(e, n=n, kind=kind, j=j, i=i, slot=slot):
                    if kind == "k":
                        dwait(e, sKST[i % 2])
                    else:
                        dwait(e, sQST[j])
                    if ("mxsq", n - 2) in ms:
                        e.wait_ge(sA.h, ms[("mxsq", n - 2)])
                    srcp = k_dram[:, i * 512:(i + 1) * 512] if kind == "k" else \
                        qt_dram[j][:, i * 512:(i + 1) * 512]
                    dinc(e.dma_start(out=mx_in[:, slot * 512:(slot + 1) * 512],
                                     in_=srcp), sMX[slot])
                block.gpsimd(mx_ld)

                def mx_sq(e, n=n, slot=slot, pslot=pslot):
                    dwait(e, sMX[slot])
                    if ("mxmm", n - 2) in ms:
                        e.wait_ge(sP.h, ms[("mxmm", n - 2)])
                    e.activation(out=sqt[:, pslot * 512:(pslot + 1) * 512],
                                 in_=mx_in[:, slot * 512:(slot + 1) * 512],
                                 func=AF.Square).then_inc(sA.h, 1)
                    sA.v += 1
                    ms[("mxsq", n)] = sA.v
                block.scalar(mx_sq)

                def mx_mm(e, n=n, slot=slot, pslot=pslot):
                    if n == 0:
                        dwait(e, sInit)
                    e.wait_ge(sA.h, ms[("mxsq", n)])
                    if ("mxcp", n - 2) in ms:
                        e.wait_ge(sA.h, ms[("mxcp", n - 2)])
                    psm = psM0 if pslot == 0 else psM1
                    ii = e.matmul(psm[:], ones_s[:, 0:1], sqt[:, pslot * 512:(pslot + 1) * 512],
                                  start=True, stop=True)
                    ii.then_inc(sP.h, 1); sP.v += 1
                    ms[("mxmm", n)] = sP.v
                block.tensor(mx_mm)

                def mx_cp(e, n=n, pslot=pslot):
                    e.wait_ge(sP.h, ms[("mxmm", n)])
                    if ("mxred", n - 12) in ms:
                        e.wait_ge(sV.h, ms[("mxred", n - 12)])
                    psm = psM0 if pslot == 0 else psM1
                    sl = (n % 12) * 512
                    e.copy(out=mxsb[0:1, sl:sl + 512], in_=psm[:]).then_inc(sA.h, 1)
                    sA.v += 1
                    ms[("mxcp", n)] = sA.v
                block.scalar(mx_cp)


        def qz_ld1(c):
            def body(e, c=c):
                if ("qred", c - 2) in ms:
                    e.wait_ge(sV.h, ms[("qred", c - 2)])
                dinc(e.dma_start(out=w_ring[:, (c % 2) * D:(c % 2 + 1) * D],
                                 in_=wf[c * 128:(c + 1) * 128, :]), sWR[c % 2])
            block.gpsimd(body)

        def qz_red(c):
            def body(e, c=c):
                dwait(e, sWR[c % 2])
                ii = e.reduce_sum(wabs[:, c:c + 1],
                                  w_ring[:, (c % 2) * D:(c % 2 + 1) * D],
                                  axis=AX.X, apply_absolute_value=True)
                ii.then_inc(sV.h, 1); sV.v += 1
                ms[("qred", c)] = sV.v
            block.vector(body)

        for g in range(NG):
            tiles = [4 * g + i for i in range(4)]

            qz_ld1(g)
            if g >= 1:
                qz_red(g - 1)

            if g >= 2:
                emit_mx_reds(5 * (g - 1))
            if g >= 1:
                emit_mx_items(g - 1)

            def a_ld(e, g=g, tiles=tiles):
                for t in tiles:
                    if ("xcons", t - 4) in ms:
                        e.wait_ge(sV.h, ms[("xcons", t - 4)])
                    dinc(e.dma_start(out=x_ring[:, (t % 4) * D:(t % 4 + 1) * D],
                                     in_=xb[t * 128:(t + 1) * 128, :]), sXA)
                if ("ropeuse", g - 2) in ms:
                    e.wait_ge(sV.h, ms[("ropeuse", g - 2)])
                sl = (g % 2) * 512
                dinc(e.dma_start(out=rc_s[:, sl:sl + 512],
                                 in_=rcd[:, g * 512:(g + 1) * 512]), sRP[g % 2])
                dinc(e.dma_start(out=rs_s[:, sl:sl + 512],
                                 in_=rsd[:, g * 512:(g + 1) * 512]), sRP[g % 2])
            block.sync(a_ld)

            def a_lnw(e, g=g, tiles=tiles):
                dwait(e, sXA)
                if g == 0:
                    dwait(e, sInit)
            block.vector(a_lnw)
            ln_group(tiles, "xcons", tr_guard=lambda t: tr_ms.get(t - 4))

            for t in tiles:
                def a_tr(e, t=t, g=g):
                    if g == 0 and t == tiles[0]:
                        dwait(e, sInit)
                    e.wait_ge(sV.h, xn_ms[t])
                    for half, bank, bn in ((0, psT0, "T0"), (1, psT1, "T1")):
                        if tb_v[bn]:
                            e.wait_ge(sV.h, tb_v[bn])
                        for i in range(8):
                            c = half * 8 + i
                            ii = e.transpose(bank[:, i * 128:(i + 1) * 128],
                                             xn_ring[:, (t % 4) * D + c * 128:(t % 4) * D + (c + 1) * 128],
                                             ident_s[:])
                        ii.then_inc(sP.h, 1); sP.v += 1
                        tb_p[bn] = sP.v
                    tr_ms[t] = sP.v
                block.tensor(a_tr)

                def a_ev(e, t=t, g=g):
                    if ("mmdone", g - 2) in ms:
                        e.wait_ge(sP.h, ms[("mmdone", g - 2)])
                    base = (g % 2) * 16 * 512
                    xv = xnfm[:, base:base + 16 * 512].rearrange("p (c n) -> p c n", c=16)
                    for half, bank, bn in ((0, psT0, "T0"), (1, psT1, "T1")):
                        e.wait_ge(sP.h, tb_p[bn])
                        dst = xv[:, half * 8:(half + 1) * 8, (t % 4) * 128:(t % 4 + 1) * 128]
                        ii = e.tensor_copy(dst, bank[:].rearrange("p (c n) -> p c n", c=8))
                        ii.then_inc(sV.h, 1); sV.v += 1
                        tb_v[bn] = sV.v
                    evac_ms[t] = sV.v
                block.vector(a_ev)

            base = (g % 2) * 16 * 512

            def a_kmm(e, g=g, base=base):
                e.wait_ge(sV.h, evac_ms[4 * g + 3])
                if g == 0:
                    dwait(e, sWI)
                if ("ropeK", g - 1) in rope_ms:
                    e.wait_ge(sV.h, rope_ms[("ropeK", g - 1)])
                for c in range(16):
                    ii = e.matmul(psK[:], wk_s[:, c * 128:(c + 1) * 128],
                                  xnfm[:, base + c * 512:base + (c + 1) * 512],
                                  start=(c == 0), stop=(c == 15))
                ii.then_inc(sP.h, 1); sP.v += 1
                mm_ms[("K", g)] = sP.v
            block.tensor(a_kmm)

            def a_krope(e, g=g):
                dwait(e, sRP[g % 2])
                e.wait_ge(sP.h, mm_ms[("K", g)])
                dwait(e, sKST[g % 2])
                kb = (g % 2) * 512
                rope6(e, psK, ktg[0:64, kb:kb + 512], ktg[64:128, kb:kb + 512], g)
                rope_ms[("ropeK", g)] = sV.v
            block.vector(a_krope)

            def a_kst(e, g=g):
                e.wait_ge(sV.h, rope_ms[("ropeK", g)])
                kb = (g % 2) * 512
                dinc(e.dma_start(out=k_dram[:, g * 512:(g + 1) * 512],
                                 in_=ktg[:, kb:kb + 512]), sKST[g % 2])
            block.gpsimd(a_kst)

            def a_vmm(e, g=g, base=base):
                if ("vev", g - 1) in rope_ms:
                    e.wait_ge(sV.h, rope_ms[("vev", g - 1)])
                for ts in range(4):
                    for c in range(16):
                        ii = e.matmul(psVb[:, ts * 128:(ts + 1) * 128],
                                      xnfm[:, base + c * 512 + ts * 128:base + c * 512 + (ts + 1) * 128],
                                      wv_s[:, c * 128:(c + 1) * 128],
                                      start=(ts == 0 and c == 0), stop=(c == 15),
                                      skip_group_check=True)
                ii.then_inc(sP.h, 1); sP.v += 1
                mm_ms[("V", g)] = sP.v
            block.tensor(a_vmm)

            def a_vev(e, g=g):
                e.wait_ge(sP.h, mm_ms[("V", g)])
                dwait(e, sVST[g % 2])
                vv = vtg[:].rearrange("p (c n) -> p c n", c=8)
                sl = (g % 2) * 4
                ii = e.tensor_copy(vv[:, sl:sl + 4, 0:128],
                                   psVb[:].rearrange("p (c n) -> p c n", c=4))
                ii.then_inc(sV.h, 1); sV.v += 1
                rope_ms[("vev", g)] = sV.v
            block.vector(a_vev)

            def a_vst(e, g=g):
                e.wait_ge(sV.h, rope_ms[("vev", g)])
                vv = vtg[:].rearrange("p (c n) -> p c n", c=8)
                sl = (g % 2) * 4
                dinc(e.dma_start(out=v_dram[4 * g:4 * g + 4].rearrange("c p d -> p c d"),
                                 in_=vv[:, sl:sl + 4, :]), sVST[g % 2])
            block.gpsimd(a_vst)

            for j in range(GQ):
                def a_qmm(e, g=g, j=j, base=base):
                    key = ("ropeQ", g, j - 2) if j >= 2 else ("ropeQ", g - 1, j + 2)
                    if key in rope_ms:
                        e.wait_ge(sV.h, rope_ms[key])
                    bankq = psQ0 if j % 2 == 0 else psQ1
                    for c in range(16):
                        ii = e.matmul(bankq[:], wq_s[:, c * 512 + j * 128:c * 512 + (j + 1) * 128],
                                      xnfm[:, base + c * 512:base + (c + 1) * 512],
                                      start=(c == 0), stop=(c == 15))
                    ii.then_inc(sP.h, 1); sP.v += 1
                    mm_ms[("Q", g, j)] = sP.v
                block.tensor(a_qmm)

                def a_qrope(e, g=g, j=j):
                    dwait(e, sRP[g % 2])
                    e.wait_ge(sP.h, mm_ms[("Q", g, j)])
                    dwait(e, sQST[j])
                    bankq = psQ0 if j % 2 == 0 else psQ1
                    qb = j * 512
                    rope6(e, bankq, qtg[0:64, qb:qb + 512], qtg[64:128, qb:qb + 512], g)
                    rope_ms[("ropeQ", g, j)] = sV.v
                block.vector(a_qrope)

                def a_qst(e, g=g, j=j):
                    e.wait_ge(sV.h, rope_ms[("ropeQ", g, j)])
                    dinc(e.dma_start(out=qt_dram[j][:, g * 512:(g + 1) * 512],
                                     in_=qtg[:, j * 512:(j + 1) * 512]), sQST[j])
                block.gpsimd(a_qst)

            ms[("mmdone", g)] = sP.v
            ms[("ropeuse", g)] = sV.v

        qz_red(NG - 1)
        emit_mx_items(NG - 1)
        emit_mx_reds(5 * NG)

        def mx_fin_v(e):
            e.wait_ge(sV.h, ms[("mxred", mx_n[0] - 1)])
            ii = e.tensor_tensor(mkq[0:1, 2:3], mkq[0:1, 0:1], mkq[0:1, 1:2], op=ALU.mult)
            ii.then_inc(sV.h, 1); sV.v += 1
            ms["mxmul"] = sV.v
        block.vector(mx_fin_v)

        def mx_fin_a(e):
            e.wait_ge(sV.h, ms["mxmul"])
            e.activation(out=mkq[0:1, 2:3], in_=mkq[0:1, 2:3],
                         func=AF.Sqrt).then_inc(sA.h, 1)
            sA.v += 1
            ms["mxsqrt"] = sA.v
        block.scalar(mx_fin_a)

        def mx_fin_v2(e):
            e.wait_ge(sA.h, ms["mxsqrt"])
            ii = e.tensor_scalar(mkq[0:1, 2:3], mkq[0:1, 2:3], -SCALE, None, op0=ALU.mult)
            ii.then_inc(sV.h, 1); sV.v += 1
            ms["mxneg"] = sV.v
        block.vector(mx_fin_v2)

        def mx_bc(e):
            e.wait_ge(sV.h, ms["mxneg"])
            ii = e.matmul(psK[:, 0:1], ones_s[0:1, :], mkq[0:1, 2:3], start=True, stop=True)
            ii.then_inc(sP.h, 1); sP.v += 1
            ms["mxbc"] = sP.v
        block.tensor(mx_bc)

        def mx_bcv(e):
            e.wait_ge(sP.h, ms["mxbc"])
            ii = e.tensor_copy(bias_s[:], psK[:, 0:1])
            ii.then_inc(sV.h, 1); sV.v += 1
            ms["biasdone"] = sV.v
        block.vector(mx_bcv)

        def qz_sum_v(e):
            e.wait_ge(sV.h, ms[("qred", 15)])
            ii = e.reduce_sum(wabs1[:], wabs[:], axis=AX.X)
            ii.then_inc(sV.h, 1); sV.v += 1
            ms["qsum"] = sV.v
        block.vector(qz_sum_v)

        def qz_mm(e):
            e.wait_ge(sV.h, ms["qsum"])
            e.wait_ge(sA.h, ms[("mxcp", mx_n[0] - 1)])
            ii = e.matmul(psM0[0:1, 0:1], wabs1[:], ones_s[:, 0:1],
                          start=True, stop=True)
            ii.then_inc(sP.h, 1); sP.v += 1
            ms["qtot"] = sP.v
        block.tensor(qz_mm)

        def qz_thr_v(e):
            e.wait_ge(sP.h, ms["qtot"])
            ii = e.tensor_scalar(wabs1[0:1, 0:1], psM0[0:1, 0:1], 0.5 / (D * D),
                                 None, op0=ALU.mult)
            ii.then_inc(sV.h, 1); sV.v += 1
            ms["qthr1"] = sV.v
        block.vector(qz_thr_v)

        def qz_bc(e):
            e.wait_ge(sV.h, ms["qthr1"])
            e.wait_ge(sV.h, ms["biasdone"])
            ii = e.matmul(psK[:, 2:3], ones_s[0:1, :], wabs1[0:1, 0:1],
                          start=True, stop=True)
            ii.then_inc(sP.h, 1); sP.v += 1
            ms["qbc"] = sP.v
        block.tensor(qz_bc)

        def qz_bcv(e):
            e.wait_ge(sP.h, ms["qbc"])
            ii = e.tensor_copy(thr_b[:], psK[:, 2:3])
            ii.then_inc(sV.h, 1); sV.v += 1
            e.wait_ge(sV.h, sV.v)
            ii = e.tensor_scalar(thrn_b[:], thr_b[:], -1.0, None, op0=ALU.mult)
            ii.then_inc(sV.h, 1); sV.v += 1
            ms["qthr"] = sV.v
        block.vector(qz_bcv)

        esA.close()
        esA1sb.close()

        esB = ExitStack()
        psS_ = [esB.enter_context(nc.psum_tensor(f"psS{i}", [128, 1024], F32)) for i in range(2)]
        psO_ = [esB.enter_context(nc.psum_tensor(f"psO{i}", [128, 512], F32)) for i in range(3)]
        psT2b = esB.enter_context(nc.psum_tensor("psT2b", [128, 1024], BF16))

        mstrip_s = sbB("mstrip_s", [128, MASKW], BF16)
        qt_sb = sbB("qt_sb", [128, 2 * 1024], BF16)
        ksc_sb = sbB("ksc_sb", [128, 2 * 1024], BF16)
        vsc_sb = sbB("vsc_sb", [128, 2 * 8 * 132], BF16)
        pt_sb = sbB("pt_sb", [128, 2 * 1024], BF16)
        linv_sb = sbB("linv_sb", [128, 8], F32)
        a_sb = sbB("a_sb", [128, 2 * 8 * 128], BF16)
        araw = sbB("araw", [128, 2 * 1280], F32)
        araw_v = araw[:].rearrange("p (g w) -> p g w", w=160)
        gpbuf = sbB("gpbuf", [128, D], BF16)
        gnbuf = sbB("gnbuf", [128, D], BF16)
        wqbuf = sbB("wqbuf", [128, 2 * D], BF16)

        def qz_p2(c):
            def qz_ld2(e, c=c):
                if c < 2:
                    e.wait_ge(sV.h, ms[("qred", 15)])
                else:
                    e.wait_ge(sV.h, ms[("qp2", c - 2)])
                dinc(e.dma_start(out=w_ring[:, (c % 2) * D:(c % 2 + 1) * D],
                                 in_=wf[c * 128:(c + 1) * 128, :]), sWR[c % 2])
            block.gpsimd(qz_ld2)

            def qz_q(e, c=c):
                dwait(e, sWR[c % 2])
                if c == 0:
                    e.wait_ge(sV.h, ms["qthr"])
                dwait(e, sWST[c % 2])
                ws = w_ring[:, (c % 2) * D:(c % 2 + 1) * D]
                e.tensor_scalar(gpbuf[:], ws, thr_b[:, 0:1], None, op0=ALU.is_ge)
                ii = e.tensor_scalar(gnbuf[:], ws, thrn_b[:, 0:1], None, op0=ALU.is_le)
                ii.then_inc(sV.h, 1); sV.v += 1
                e.wait_ge(sV.h, sV.v)
                ii = e.tensor_tensor(wqbuf[:, (c % 2) * D:(c % 2 + 1) * D],
                                     gpbuf[:], gnbuf[:], op=ALU.subtract)
                ii.then_inc(sV.h, 1); sV.v += 1
                ms[("qp2", c)] = sV.v
            block.vector(qz_q)

            def qz_st(e, c=c):
                e.wait_ge(sV.h, ms[("qp2", c)])
                dinc(e.dma_start(out=wfq_dram[c * 128:(c + 1) * 128, :],
                                 in_=wqbuf[:, (c % 2) * D:(c % 2 + 1) * D]), sWST[c % 2])
            block.gpsimd(qz_st)

        qz_done = [0]

        def qz_emit(tj):
            if tj % 2 == 0 and qz_done[0] < 16:
                qz_p2(qz_done[0])
                qz_done[0] += 1

        ln2tr_ms = {}
        tb2 = {"p": 0, "v": 0}

        def emit_a2_group(g2):
            tiles = [4 * g2 + i for i in range(4)]

            def a2_ld(e, g2=g2, tiles=tiles):
                for t in tiles:
                    if ("a2cons", t - 4) in ms:
                        e.wait_ge(sV.h, ms[("a2cons", t - 4)])
                    elif g2 == 0:
                        e.wait_ge(sV.h, ms[("xcons", 60 + (t % 4))])
                    dinc(e.dma_start(out=x_ring[:, (t % 4) * D:(t % 4 + 1) * D],
                                     in_=xown[t * 128:(t + 1) * 128, :]), sXA)
            block.sync(a2_ld)

            def a2_w(e, g2=g2, tiles=tiles):
                dwait(e, sXA)
            block.vector(a2_w)
            ln_group(tiles, None,
                     tr_guard=(lambda t: tr_ms.get(60 + (t % 4))) if g2 == 0 else None)

            for p0 in (0, 2):
                pair = tiles[p0:p0 + 2]
                cols = [t % 4 for t in pair]
                c0, c1 = min(cols), max(cols) + 1

                def a2_v1(e, pair=pair, c0=c0, c1=c1):
                    e.wait_ge(sV.h, xn_ms[pair[-1]])
                    ii = e.scalar_tensor_tensor(v2p[:, c0:c1], varp[:, c0:c1], -EPS,
                                                rstd[:, c0:c1], op0=ALU.add, op1=ALU.mult)
                    ii.then_inc(sV.h, 1); sV.v += 1
                    e.wait_ge(sV.h, sV.v)
                    for cc in range(c0, c1):
                        ii = e.tensor_scalar(v2p[:, cc:cc + 1], v2p[:, cc:cc + 1],
                                             rstd[:, cc:cc + 1], EPS,
                                             op0=ALU.mult, op1=ALU.add)
                    ii.then_inc(sV.h, 1); sV.v += 1
                    e.wait_ge(sV.h, sV.v)
                    ii = e.reciprocal(r2f[:, c0:c1], v2p[:, c0:c1])
                    ii.then_inc(sV.h, 1); sV.v += 1
                    ms["a2mv"] = sV.v
                block.vector(a2_v1)
                mv = sV.v

                def a2_s(e, c0=c0, c1=c1, mv=mv):
                    e.wait_ge(sV.h, mv)
                    e.activation(out=r2f[:, c0:c1], in_=r2f[:, c0:c1],
                                 func=AF.Sqrt).then_inc(sA.h, 1)
                    sA.v += 1
                block.scalar(a2_s)
                mact = sA.v

                def a2_v2(e, pair=pair, mact=mact):
                    e.wait_ge(sA.h, mact)
                    for t in pair:
                        cc = t % 4
                        ii = e.tensor_tensor(r2f[:, cc:cc + 1], r2f[:, cc:cc + 1],
                                             rstd[:, cc:cc + 1], op=ALU.mult)
                    ii.then_inc(sV.h, 1); sV.v += 1
                    e.wait_ge(sV.h, sV.v)
                    for t in pair:
                        cc = t % 4
                        xs = x_ring[:, cc * D:(cc + 1) * D]
                        ii = e.tensor_scalar(ln2t1[:], xs, means[:, cc:cc + 1],
                                             r2f[:, cc:cc + 1], op0=ALU.subtract, op1=ALU.mult)
                        ii.then_inc(sV.h, 1); sV.v += 1
                        e.wait_ge(sV.h, sV.v)
                        ii = e.tensor_tensor(ln2t2[:], ln2t1[:], lnw_s[:], op=ALU.mult)
                        ii.then_inc(sV.h, 1); sV.v += 1
                        e.wait_ge(sV.h, sV.v)
                        if ("ln2tr", t - 2) in ln2tr_ms:
                            e.wait_ge(sP.h, ln2tr_ms[("ln2tr", t - 2)])
                        ii = e.tensor_tensor(ln2bf[:, (t % 2) * D:(t % 2 + 1) * D],
                                             ln2t2[:], lnb_s[:], op=ALU.add)
                        ii.then_inc(sV.h, 1); sV.v += 1
                        ms[("ln2bf", t)] = sV.v
                        ms[("a2cons", t)] = sV.v
                block.vector(a2_v2)
                for t in pair:
                    for half in (0, 1):
                        def a2_tr(e, t=t, half=half):
                            if half == 0:
                                e.wait_ge(sV.h, ms[("ln2bf", t)])
                            if tb2["v"]:
                                e.wait_ge(sV.h, tb2["v"])
                            for i in range(8):
                                c = half * 8 + i
                                ii = e.transpose(psT2b[:, i * 128:(i + 1) * 128],
                                                 ln2bf[:, (t % 2) * D + c * 128:(t % 2) * D + (c + 1) * 128],
                                                 ident_s[:])
                            ii.then_inc(sP.h, 1); sP.v += 1
                            tb2["p"] = sP.v
                        block.tensor(a2_tr)
                        ln2tr_ms[("ln2tr", t)] = sP.v

                        def a2_ev(e, t=t, half=half):
                            if half == 0:
                                dwait(e, sLNST[t % 2])
                            e.wait_ge(sP.h, tb2["p"])
                            dst = lnfm_st[:, (t % 2) * 2048 + half * 1024:(t % 2) * 2048 + (half + 1) * 1024] \
                                .rearrange("p (c n) -> p c n", c=8)
                            ii = e.tensor_copy(dst, psT2b[:].rearrange("p (c n) -> p c n", c=8))
                            ii.then_inc(sV.h, 1); sV.v += 1
                            tb2["v"] = sV.v
                        block.vector(a2_ev)
                        ms[("lnev", t)] = sV.v

                    def a2_st(e, t=t, g2=g2):
                        e.wait_ge(sV.h, ms[("lnev", t)])
                        lnview = ln2_dram[g2].rearrange("p (c n) -> p c n", c=16)
                        for half in range(2):
                            dst = lnview[:, half * 8:(half + 1) * 8, (t % 4) * 128:(t % 4 + 1) * 128]
                            src = lnfm_st[:, (t % 2) * 2048 + half * 1024:(t % 2) * 2048 + (half + 1) * 1024] \
                                .rearrange("p (c n) -> p c n", c=8)
                            dinc(e.dma_start(out=dst, in_=src), sLNST[t % 2])
                    block.gpsimd(a2_st)



        def b_join(e):
            for s_ in (sKST[0], sKST[1], sVST[0], sVST[1], sQST[0], sQST[1],
                       sQST[2], sQST[3], sLNST[0], sLNST[1]):
                dwait(e, s_)
            e.wait_ge(sV.h, ms["biasdone"])
            dinc(e.dma_start(out=mstrip_s[:], in_=mstripd[:]), sMSK)
        block.sync(b_join)

        def emit_sc_load(e, sc, occ, scld_ms):
            slot = sc % 2
            if occ is not None:
                e.wait_ge(sP.h, occ)
            dinc(e.dma_start(out=ksc_sb[:, slot * 1024:(slot + 1) * 1024],
                             in_=k_dram[:, sc * 1024:(sc + 1) * 1024]), sSC[slot])
            vv = vsc_sb[:].rearrange("p (c n) -> p c n", c=16)
            dinc(e.dma_start(out=vv[:, slot * 8:(slot + 1) * 8, :],
                             in_=v_dram[8 * sc:8 * sc + 8].rearrange("c p d -> p c d")),
                 sSC[slot])
            scld_ms[sc] = sSC[slot].v

        sc_occ = [None, None]
        qk_last = {}
        pv_last = {}
        exp_last = {}
        norm_ms = {}
        evacB_ms = {}

        tj = 0
        for t in range(NT):
            nk = 8 * (t + 1)
            nsc = t + 1
            for j in range(GQ):
                qz_emit(tj)
                slot = tj % 2
                sPb, sAb, sVb = sP.v, sA.v, sV.v
                scld_ms = {}
                qk_ms = {0: sPb + 1}
                pv_ms = {}
                for k in range(1, nk):
                    qk_ms[k] = sPb + 2 * k
                for k in range(nk):
                    pv_ms[k] = sPb + 2 * k + 3 if k < nk - 1 else sPb + 2 * nk
                exp_ms = {k: sAb + k + 1 for k in range(nk)}
                mask_ms = {k: sVb + (k - 8 * t) + 1 for k in range(8 * t, nk)}

                def b_ld(e, t=t, j=j, tj=tj, nsc=nsc, slot=slot,
                         pv_ms=pv_ms, occ0=sc_occ[0], occ1=sc_occ[1],
                         scld_ms=scld_ms):
                    if tj - 2 >= 0:
                        e.wait_ge(sP.h, qk_last[tj - 2])
                    dinc(e.dma_start(out=qt_sb[:, slot * 1024:(slot + 1) * 1024],
                                     in_=qt_dram[j][:, t * 1024:(t + 1) * 1024]), sQT[tj % 2])
                    for sc in range(min(2, nsc)):
                        emit_sc_load(e, sc, occ0 if sc == 0 else occ1, scld_ms)
                    for sc2 in range(2, nsc):
                        emit_sc_load(e, sc2, pv_ms[8 * (sc2 - 2) + 7], scld_ms)
                block.sync(b_ld)

                def b_pe(e, t=t, j=j, tj=tj, nk=nk, slot=slot,
                         qk_ms=qk_ms, pv_ms=pv_ms, exp_ms=exp_ms, scld_ms=scld_ms,
                         mask_ms=mask_ms):
                    qa = qt_sb[:, slot * 1024:(slot + 1) * 1024]
                    vv = vsc_sb[:].rearrange("p (c n) -> p c n", c=16)

                    def emit_qk(k):
                        scs = (k // 8) % 2
                        if k == 0:
                            dwait(e, sQT[tj % 2])
                        if k % 8 == 0:
                            e.wait_ge(sSC[scs].h, scld_ms[k // 8])
                        if k >= 2:
                            e.wait_ge(sA.h, exp_ms[k - 2])
                        elif tj - 1 >= 0:
                            e.wait_ge(sA.h, exp_last[tj - 1])
                        ka = ksc_sb[:, scs * 1024 + (k % 8) * 128:scs * 1024 + (k % 8 + 1) * 128]
                        e.matmul(psS_[k % 2][:, 0:512], ka, qa[:, 0:512], start=True, stop=True)
                        ii = e.matmul(psS_[k % 2][:, 512:1024], ka, qa[:, 512:1024],
                                      start=True, stop=True)
                        ii.then_inc(sP.h, 1); sP.v += 1
                        assert sP.v == qk_ms[k], (tj, k, sP.v, qk_ms[k])

                    def emit_pv(k):
                        if k >= 8 * t:
                            e.wait_ge(sV.h, mask_ms[k])
                        else:
                            e.wait_ge(sA.h, exp_ms[k])
                        if k == 0 and tj - 1 >= 0:
                            e.wait_ge(sV.h, evacB_ms[tj - 1])
                        va = vv[:, ((k // 8) % 2) * 8 + (k % 8), 0:129]
                        for qs in range(8):
                            bank = psO_[qs // 3]
                            off = 160 * (qs % 3)
                            ii = e.matmul(bank[:, off:off + 129],
                                          pt_sb[:, (k % 2) * 1024 + qs * 128:(k % 2) * 1024 + (qs + 1) * 128],
                                          va, start=(k == 0 and qs % 3 == 0),
                                          stop=(k == nk - 1), skip_group_check=True)
                        ii.then_inc(sP.h, 1); sP.v += 1
                        assert sP.v == pv_ms[k], (tj, k, sP.v, pv_ms[k])

                    emit_qk(0)
                    for k in range(nk):
                        if k + 1 < nk:
                            emit_qk(k + 1)
                        emit_pv(k)
                block.tensor(b_pe)
                qk_last[tj] = qk_ms[nk - 1]
                pv_last[tj] = pv_ms[nk - 1]
                sc_occ[0] = pv_ms[min(8 * (nsc - 1 - (nsc - 1) % 2) + 7, nk - 1)]
                if nsc >= 2:
                    sc_occ[1] = pv_ms[min(8 * (nsc - 1 - (nsc % 2)) + 7, nk - 1)]

                def b_msk(e, t=t, tj=tj, nk=nk, exp_ms=exp_ms, mask_ms=mask_ms):
                    if tj == 0:
                        dwait(e, sMSK)
                    for k in range(8 * t, nk):
                        e.wait_ge(sA.h, exp_ms[k])
                        off = 896 - (128 * k - 1024 * t)
                        pts = pt_sb[:, (k % 2) * 1024:(k % 2 + 1) * 1024]
                        ii = e.tensor_tensor(pts, pts,
                                             mstrip_s[:, off:off + 1024], op=ALU.mult)
                        ii.then_inc(sV.h, 1); sV.v += 1
                        assert sV.v == mask_ms[k]
                block.vector(b_msk)

                def b_act(e, t=t, tj=tj, nk=nk, qk_ms=qk_ms, pv_ms=pv_ms):
                    if tj == 0:
                        e.wait_ge(sV.h, ms["biasdone"])
                    for k in range(nk):
                        e.wait_ge(sP.h, qk_ms[k])
                        if k >= 2:
                            e.wait_ge(sP.h, pv_ms[k - 2])
                        elif tj - 1 >= 0:
                            e.wait_ge(sP.h, pv_last[tj - 1])
                        ii = e.activation(out=pt_sb[:, (k % 2) * 1024:(k % 2 + 1) * 1024],
                                          in_=psS_[k % 2][:], func=AF.Exp,
                                          bias=bias_s[:], scale=SCALE)
                        ii.then_inc(sA.h, 1); sA.v += 1
                block.scalar(b_act)
                exp_last[tj] = sA.v

                def b_tail(e, tj=tj, nk=nk, pv_ms=pv_ms):
                    e.wait_ge(sP.h, pv_ms[nk - 1])
                    g0 = (tj % 2) * 8
                    pv0 = psO_[0][:, 0:480].rearrange("p (s w) -> p s w", w=160)
                    pv1 = psO_[1][:, 0:480].rearrange("p (s w) -> p s w", w=160)
                    pv2 = psO_[2][:, 0:320].rearrange("p (s w) -> p s w", w=160)
                    e.tensor_copy(araw_v[:, g0:g0 + 3, 0:129], pv0[:, :, 0:129])
                    e.tensor_copy(araw_v[:, g0 + 3:g0 + 6, 0:129], pv1[:, :, 0:129])
                    ii = e.tensor_copy(araw_v[:, g0 + 6:g0 + 8, 0:129], pv2[:, :, 0:129])
                    ii.then_inc(sV.h, 1); sV.v += 1
                    evacB_ms[tj] = sV.v
                    e.wait_ge(sV.h, sV.v)
                    ii = e.reciprocal(linv_sb[:, 0:8], araw_v[:, g0:g0 + 8, 128:129])
                    ii.then_inc(sV.h, 1); sV.v += 1
                    e.wait_ge(sV.h, sV.v)
                    dwait(e, sAST[tj % 2])
                    aslot = (tj % 2) * 8 * 128
                    for qs in range(8):
                        ii = e.tensor_scalar(a_sb[:, aslot + qs * 128:aslot + (qs + 1) * 128],
                                             araw_v[:, g0 + qs, 0:128], linv_sb[:, qs:qs + 1],
                                             None, op0=ALU.mult)
                    ii.then_inc(sV.h, 1); sV.v += 1
                    norm_ms[tj] = sV.v
                block.vector(b_tail)

                def b_ast(e, t=t, j=j, tj=tj):
                    e.wait_ge(sV.h, norm_ms[tj])
                    aslot = (tj % 2) * 8
                    av = a_sb[:].rearrange("p (c n) -> p c n", c=16)
                    dst = a_dram[8 * t:8 * t + 8].rearrange("c p d -> p c d")[:, :, j * 128:(j + 1) * 128]
                    dinc(e.dma_start(out=dst, in_=av[:, aslot:aslot + 8, :]), sAST[tj % 2])
                block.gpsimd(b_ast)

                if tj in (7, 15, 23, 31):
                    part = tj // 8
                    def cc_gp(e, part=part):
                        dwait(e, sAST[0])
                        dwait(e, sAST[1])
                        a_flat = a_dram.rearrange("c p d -> (c p) d")
                        e.collective_compute(
                            "AllGather", ALU.bypass, replica_groups=[list(range(8))],
                            ins=[a_flat[part * 2048:(part + 1) * 2048, :].opt()],
                            outs=[ag_dram[part * 16384:(part + 1) * 16384, :].opt()],
                        ).then_inc(sC.h, 1)
                        sC.v += 1
                    block.gpsimd(cc_gp)

                tj += 1
                if tj in (3, 9, 15, 21):
                    emit_a2_group((tj - 3) // 6)

        while qz_done[0] < 16:
            qz_p2(qz_done[0])
            qz_done[0] += 1
        ms["qzdone"] = ms[("qp2", 15)]

        esB.close()
        esBsb.close()




        esAsb.close()
        esQsb.close()


        esC = ExitStack()
        psT0c = esC.enter_context(nc.psum_tensor("psT0c", [128, 1024], BF16))
        psT1c = esC.enter_context(nc.psum_tensor("psT1c", [128, 1024], BF16))
        psSo = [esC.enter_context(nc.psum_tensor(f"psSo{i}", [128, 512], F32)) for i in range(2)]
        psGo = [esC.enter_context(nc.psum_tensor(f"psGo{i}", [128, 512], F32)) for i in range(2)]

        atok = sbC("atok", [128, 2 * D], BF16)
        afm = sbC("afm", [128, 16 * 512], BF16)
        lnfm_g = sbC("lnfm_g", [128, 2 * 8192], BF16)
        wo_sb = sbC("wo_sb", [128, 2 * 8192], BF16)
        wfq_sb = sbC("wfq_sb", [128, 2 * 8192], BF16)
        xg_sb = sbC("xg_sb", [128, 4 * D], F32)
        g_sl = sbC("g_sl", [128, 512], F32)
        sg_sl = sbC("sg_sl", [128, 512], F32)
        out_sb = sbC("out_sb", [128, 4 * D], F32)

        tbc_v = {"T0": 0, "T1": 0}
        tbc_p = {"T0": 0, "T1": 0}
        last_bank = [None, None]
        last_gsl = None

        def c_wld_emit(e, dq_seq):
            g2w, dqw = dq_seq // 4, dq_seq % 4
            slot = (dq_seq % 2) * 8192
            if dq_seq == 0:
                dwait(e, sWI)
                dwait(e, sWST[0])
                dwait(e, sWST[1])
            if ("wU", dq_seq - 2) in ms:
                e.wait_ge(sP.h, ms[("wU", dq_seq - 2)])
            dinc(e.dma_start(out=wo_sb[:, slot:slot + 8192].rearrange("p (c n) -> p c n", c=16),
                             in_=wob_dram[:, dqw * 512:(dqw + 1) * 512]
                             .rearrange("(c p) n -> p c n", p=128)), sWL[dq_seq % 2])
            dinc(e.dma_start(out=wfq_sb[:, slot:slot + 8192].rearrange("p (c n) -> p c n", c=16),
                             in_=wfq_dram[:, dqw * 512:(dqw + 1) * 512]
                             .rearrange("(c p) n -> p c n", p=128)), sWL[dq_seq % 2])

        for g2 in range(4):
            def c_lnld(e, g2=g2):
                if g2 == 0:
                    e.wait_ge(sC.h, 4)
                    dwait(e, sWST[0])
                    dwait(e, sWST[1])
                    dwait(e, sLNST[0])
                    dwait(e, sLNST[1])
                    dwait(e, sAST[0])
                    dwait(e, sAST[1])
                    e.wait_ge(sV.h, ms["qzdone"])
                if ("lnfmU", g2 - 2) in ms:
                    e.wait_ge(sP.h, ms[("lnfmU", g2 - 2)])
                dinc(e.dma_start(out=lnfm_g[:, (g2 % 2) * 8192:(g2 % 2 + 1) * 8192],
                                 in_=ln2_dram[g2]), sLF[g2 % 2])
            block.sync(c_lnld)

            for u in range(4):
                tt = g2 * 4 + u

                def c_gather(e, tt=tt):
                    if tt == 0:
                        dwait(e, sInit)
                        e.wait_ge(sC.h, 4)
                        dwait(e, sWST[0])
                        dwait(e, sWST[1])
                    if ("atokU", tt - 2) in ms:
                        e.wait_ge(sP.h, ms[("atokU", tt - 2)])
                    slot = (tt % 2) * D
                    for eh in range(4):
                        dinc(e.indirect_dma_start(
                            out=atok[:, slot + eh * 512:slot + (eh + 1) * 512],
                            out_offset=None, in_=ag_dram,
                            in_offset=IndirectOffsetOnAxis(
                                ap=idx_s[:, tt * 4 + eh:tt * 4 + eh + 1], axis=0),
                        ), sAG[tt % 2])
                block.gpsimd(c_gather)

                def c_tr(e, tt=tt):
                    if tt == 0:
                        e.wait_ge(sV.h, ms["qzdone"])
                    dwait(e, sAG[tt % 2])
                    slot = (tt % 2) * D
                    for half, bank, bn in ((0, psT0c, "T0"), (1, psT1c, "T1")):
                        if tbc_v[bn]:
                            e.wait_ge(sV.h, tbc_v[bn])
                        for i in range(8):
                            c = half * 8 + i
                            ii = e.transpose(bank[:, i * 128:(i + 1) * 128],
                                             atok[:, slot + c * 128:slot + (c + 1) * 128],
                                             ident_s[:])
                        ii.then_inc(sP.h, 1); sP.v += 1
                        tbc_p[bn] = sP.v
                    ms[("atokU", tt)] = sP.v
                block.tensor(c_tr)

                def c_ev(e, tt=tt, u=u, g2=g2):
                    if ("cmm", g2 - 1) in ms:
                        e.wait_ge(sP.h, ms[("cmm", g2 - 1)])
                    av = afm[:].rearrange("p (c n) -> p c n", c=16)
                    for half, bank, bn in ((0, psT0c, "T0"), (1, psT1c, "T1")):
                        e.wait_ge(sP.h, tbc_p[bn])
                        dst = av[:, half * 8:(half + 1) * 8, u * 128:(u + 1) * 128]
                        ii = e.tensor_copy(dst, bank[:].rearrange("p (c n) -> p c n", c=8))
                        ii.then_inc(sV.h, 1); sV.v += 1
                        tbc_v[bn] = sV.v
                    ms[("afmev", tt)] = sV.v
                block.vector(c_ev)

            def c_xld(e, g2=g2):
                for ts in range(4):
                    tt = g2 * 4 + ts
                    if ("xgU", tt - 4) in ms:
                        e.wait_ge(sV.h, ms[("xgU", tt - 4)])
                    dinc(e.dma_start(out=xg_sb[:, ts * D:(ts + 1) * D],
                                     in_=xown[tt * 128:(tt + 1) * 128, :]), sXG[ts % 2])
            block.sync(c_xld)

            if g2 == 0:
                def c_wld0(e):
                    c_wld_emit(e, 0)
                    c_wld_emit(e, 1)
                block.sync(c_wld0)

            for dq in range(4):
                dq_seq = g2 * 4 + dq
                for ts in range(4):
                    tt = g2 * 4 + ts
                    par = ts % 2
                    pss = psSo[par]
                    psg = psGo[par]

                    def c_mm(e, g2=g2, ts=ts, dq=dq, dq_seq=dq_seq, par=par,
                             pss=pss, psg=psg, tt=tt):
                        if ts == 0 and dq == 0:
                            e.wait_ge(sV.h, ms[("afmev", g2 * 4 + 3)])
                            dwait(e, sLF[g2 % 2])
                        if ts == 0:
                            dwait(e, sWL[dq_seq % 2])
                        lb = last_bank[par]
                        if lb is not None:
                            e.wait_ge(sV.h, lb[0])
                            e.wait_ge(sA.h, lb[1])
                        wslot = (dq_seq % 2) * 8192
                        lnb_ = (g2 % 2) * 8192
                        for c in range(16):
                            ii = e.matmul(pss[:], afm[:, c * 512 + ts * 128:c * 512 + (ts + 1) * 128],
                                          wo_sb[:, wslot + c * 512:wslot + (c + 1) * 512],
                                          start=(c == 0), stop=(c == 15))
                        ii.then_inc(sP.h, 1); sP.v += 1
                        ms[("cmms", g2, ts, dq)] = sP.v
                        for c in range(16):
                            ii = e.matmul(psg[:],
                                          lnfm_g[:, lnb_ + c * 512 + ts * 128:lnb_ + c * 512 + (ts + 1) * 128],
                                          wfq_sb[:, wslot + c * 512:wslot + (c + 1) * 512],
                                          start=(c == 0), stop=(c == 15))
                        ii.then_inc(sP.h, 1); sP.v += 1
                        ms[("cmmg", g2, ts, dq)] = sP.v
                        if ts == 3:
                            ms[("wU", dq_seq)] = sP.v
                        if ts == 3 and dq == 3:
                            ms[("cmm", g2)] = sP.v
                            ms[("lnfmU", g2)] = sP.v
                    block.tensor(c_mm)

                    def c_silu(e, g2=g2, ts=ts, dq=dq, psg=psg, lg=last_gsl):
                        e.wait_ge(sP.h, ms[("cmmg", g2, ts, dq)])
                        if lg is not None:
                            e.wait_ge(sV.h, lg)
                        e.activation(out=g_sl[:], in_=psg[:], func=AF.Silu).then_inc(sA.h, 1)
                        sA.v += 1
                        ms[("csilu", g2, ts, dq)] = sA.v
                    block.scalar(c_silu)

                    def c_gate(e, g2=g2, ts=ts, dq=dq, par=par, pss=pss, tt=tt):
                        e.wait_ge(sA.h, ms[("csilu", g2, ts, dq)])
                        e.wait_ge(sP.h, ms[("cmms", g2, ts, dq)])
                        if dq == 0:
                            dwait(e, sXG[ts % 2])
                            dwait(e, sYS[ts % 2])
                        ii = e.tensor_tensor(sg_sl[:], pss[:], g_sl[:], op=ALU.mult)
                        ii.then_inc(sV.h, 1); sV.v += 1
                        e.wait_ge(sV.h, sV.v)
                        ii = e.tensor_tensor(
                            out_sb[:, ts * D + dq * 512:ts * D + (dq + 1) * 512],
                            sg_sl[:],
                            xg_sb[:, ts * D + dq * 512:ts * D + (dq + 1) * 512],
                            op=ALU.add)
                        ii.then_inc(sV.h, 1); sV.v += 1
                        ms[("cout", g2, ts, dq)] = sV.v
                        if dq == 3:
                            ms[("xgU", tt)] = sV.v
                    block.vector(c_gate)
                    last_bank[par] = (sV.v, sA.v)
                    last_gsl = sV.v

                if dq_seq + 2 < 16:
                    def c_wldn(e, nseq=dq_seq + 2):
                        c_wld_emit(e, nseq)
                    block.sync(c_wldn)

            for ts in range(4):
                tt = g2 * 4 + ts

                def c_yst(e, g2=g2, ts=ts, tt=tt):
                    e.wait_ge(sV.h, ms[("cout", g2, ts, 3)])
                    dinc(e.dma_start(out=y[tt * 128:(tt + 1) * 128, :],
                                     in_=out_sb[:, ts * D:(ts + 1) * D]), sYS[ts % 2])
                block.gpsimd(c_yst)

        def fin(e):
            dwait(e, sYS[0])
            dwait(e, sYS[1])
        block.gpsimd(fin)

        esC.close()
        esCsb.close()
    es.close()
    return nc


_CACHE = {}


def _host_consts():
    t = np.arange(S, dtype=np.float64)
    inv = 1.0 / (10000.0 ** (np.arange(0, HD, 2, dtype=np.float64) / HD))
    fr = inv[:, None] * t[None, :]
    rc = np.cos(fr).astype(np.float32)
    rs = np.sin(fr).astype(np.float32)
    u = np.arange(MASKW, dtype=np.int64)[None, :] - 896
    si = np.arange(128, dtype=np.int64)[:, None]
    mstrip = np.where(u < si, np.float32(0.0), np.float32(1.0)).astype(BF)
    ident = np.eye(128, dtype=np.float32).astype(BF)
    ones = np.ones((128, 128), dtype=np.float32)
    return rc, rs, mstrip, ident, ones


def kernel(L0_attn_x, L0_attn_Wq, L0_attn_Wk, L0_attn_Wv, L0_attn_Wo,
           L0_ffn_W, L0_ffn_ln_w, L0_ffn_ln_b):
    x = np.ascontiguousarray(np.asarray(L0_attn_x, dtype=np.float32))
    Wq = np.ascontiguousarray(np.asarray(L0_attn_Wq, dtype=np.float32))
    Wk = np.ascontiguousarray(np.asarray(L0_attn_Wk, dtype=np.float32))
    Wv = np.ascontiguousarray(np.asarray(L0_attn_Wv, dtype=np.float32))
    Wo = np.ascontiguousarray(np.asarray(L0_attn_Wo, dtype=np.float32))
    W = np.ascontiguousarray(np.asarray(L0_ffn_W, dtype=np.float32))
    lnw = np.asarray(L0_ffn_ln_w, dtype=np.float32)
    lnb = np.asarray(L0_ffn_ln_b, dtype=np.float32)

    if "nc" not in _CACHE:
        _CACHE["nc"] = build_nc()
    nc = _CACHE["nc"]

    rc, rs, mstrip, ident, ones = _host_consts()
    lnwb = np.ascontiguousarray(np.broadcast_to(lnw[None, :], (128, D)))
    lnbb = np.ascontiguousarray(np.broadcast_to(lnb[None, :], (128, D)))

    in_maps = []
    for c in range(8):
        b, r, Q = c // 4, c % 4, c % 4
        idx = np.zeros((128, 64), dtype=np.uint32)
        p = np.arange(128, dtype=np.uint32)
        for uu in range(16):
            for eh in range(4):
                ch = 16 * Q + uu
                part = ch // 16
                idx[:, uu * 4 + eh] = (part * 16384 + (4 * b + eh) * 2048
                                       + (ch - 16 * part) * 128 + p)
        in_maps.append({
            "xb": x[b],
            "xown": np.ascontiguousarray(x[b, 2048 * Q:2048 * (Q + 1)]),
            "wq": np.ascontiguousarray(Wq[:, 512 * r:512 * (r + 1)]),
            "wk": np.ascontiguousarray(Wk[:, 128 * r:128 * (r + 1)]),
            "wv": np.ascontiguousarray(Wv[:, 128 * r:128 * (r + 1)]),
            "wo": Wo, "wf": W,
            "lnwb": lnwb, "lnbb": lnbb,
            "rcd": rc, "rsd": rs, "mstripd": mstrip,
            "identd": ident, "onesd": ones,
            "agidxd": idx,
        })

    res = run_bass_kernel_spmd(nc, in_maps, core_ids=list(range(8)))
    out = np.empty((B, S, D), dtype=np.float32)
    for c in range(8):
        b, Q = c // 4, c % 4
        out[b, 2048 * Q:2048 * (Q + 1), :] = res.results[c]["y"]
    return out


# version 17
# speedup vs baseline: 1.0669x; 1.0483x over previous
import numpy as np
import ml_dtypes
from contextlib import ExitStack

import concourse.bass as bass
import concourse.mybir as mybir
from concourse.bass import IndirectOffsetOnAxis
from concourse.bass_utils import run_bass_kernel_spmd

F32 = mybir.dt.float32
BF16 = mybir.dt.bfloat16
U32 = mybir.dt.uint32
AF = mybir.ActivationFunctionType
ALU = mybir.AluOpType
AX = mybir.AxisListType
BF = ml_dtypes.bfloat16

B, S, D = 2, 8192, 2048
H, KVH, HD = 16, 4, 128
GQ = H // KVH
SCALE = HD ** -0.5
EPS = 1e-5
NG = S // 512
NT = 8
MASKW = 1920
NEGBIG = -1.0e6


class Sem:
    __slots__ = ("h", "v")

    def __init__(self, h):
        self.h = h
        self.v = 0


def build_nc():
    nc = bass.Bass("TRN2", target_bir_lowering=False)
    es = ExitStack()

    xb = nc.dram_tensor("xb", [S, D], F32, kind="ExternalInput").ap()
    xown = nc.dram_tensor("xown", [2048, D], F32, kind="ExternalInput").ap()
    wq = nc.dram_tensor("wq", [D, GQ * HD], F32, kind="ExternalInput").ap()
    wk = nc.dram_tensor("wk", [D, HD], F32, kind="ExternalInput").ap()
    wv = nc.dram_tensor("wv", [D, HD], F32, kind="ExternalInput").ap()
    wo = nc.dram_tensor("wo", [D, D], F32, kind="ExternalInput").ap()
    wf = nc.dram_tensor("wf", [D, D], F32, kind="ExternalInput").ap()
    lnwb = nc.dram_tensor("lnwb", [128, D], F32, kind="ExternalInput").ap()
    lnbb = nc.dram_tensor("lnbb", [128, D], F32, kind="ExternalInput").ap()
    rcd = nc.dram_tensor("rcd", [64, S], F32, kind="ExternalInput").ap()
    rsd = nc.dram_tensor("rsd", [64, S], F32, kind="ExternalInput").ap()
    mstripd = nc.dram_tensor("mstripd", [128, MASKW], BF16, kind="ExternalInput").ap()
    identd = nc.dram_tensor("identd", [128, 128], BF16, kind="ExternalInput").ap()
    onesd = nc.dram_tensor("onesd", [128, 128], F32, kind="ExternalInput").ap()
    agidxd = nc.dram_tensor("agidxd", [128, 64], U32, kind="ExternalInput").ap()
    y = nc.dram_tensor("y", [2048, D], F32, kind="ExternalOutput").ap()

    k_dram = nc.dram_tensor("k_dram", [128, S], BF16).ap()
    v_dram = nc.dram_tensor("v_dram", [64, 128, 132], BF16).ap()
    qt_dram = nc.dram_tensor("qt_dram", [GQ, 128, S], BF16).ap()
    a_dram = nc.dram_tensor("a_dram", [64, 128, 512], BF16).ap()
    ag_dram = nc.dram_tensor("ag_dram", [8 * 8192, 512], BF16).ap()
    ln2_dram = nc.dram_tensor("ln2_dram", [4, 128, 8192], BF16).ap()
    wob_dram = nc.dram_tensor("wob_dram", [D, D], BF16).ap()
    wfq_dram = nc.dram_tensor("wfq_dram", [D, D], BF16).ap()

    sP = Sem(es.enter_context(nc.semaphore("sP")))
    sV = Sem(es.enter_context(nc.semaphore("sV")))
    sA = Sem(es.enter_context(nc.semaphore("sA")))
    sC = Sem(es.enter_context(nc.semaphore("sC")))
    sG = Sem(es.enter_context(nc.semaphore("sG")))

    def dsem(name):
        return Sem(es.enter_context(nc.semaphore(name)))

    sInit = dsem("sInit")
    sWI = dsem("sWI")
    sXA = dsem("sXA")
    sRP = [dsem(f"sRP{i}") for i in range(2)]
    sMX = [dsem(f"sMX{i}") for i in range(4)]
    sKST = [dsem(f"sKST{i}") for i in range(2)]
    sVST = [dsem(f"sVST{i}") for i in range(2)]
    sQST = [dsem(f"sQST{i}") for i in range(4)]
    sLNST = [dsem(f"sLNST{i}") for i in range(2)]
    sQT = [dsem(f"sQT{i}") for i in range(2)]
    sSC = [dsem(f"sSC{i}") for i in range(2)]
    sAST = [dsem(f"sAST{i}") for i in range(2)]
    sWR = [dsem(f"sWR{i}") for i in range(2)]
    sWST = [dsem(f"sWST{i}") for i in range(2)]
    sLF = [dsem(f"sLF{i}") for i in range(2)]
    sWL = [dsem(f"sWL{i}") for i in range(2)]
    sXG = [dsem(f"sXG{i}") for i in range(2)]
    sAG = [dsem(f"sAG{i}") for i in range(2)]
    sYS = [dsem(f"sYS{i}") for i in range(2)]
    sMSK = dsem("sMSK")

    def dinc(instr, sem):
        instr.then_inc(sem.h, 16)
        sem.v += 16

    def dwait(e, sem):
        e.wait_ge(sem.h, sem.v)

    def sb(name, shape, dt):
        return es.enter_context(nc.sbuf_tensor(name, shape, dt))

    esAsb = ExitStack()
    esBsb = ExitStack()
    esQsb = ExitStack()
    esCsb = ExitStack()

    esA1sb = ExitStack()

    def sbA(name, shape, dt):
        return esAsb.enter_context(nc.sbuf_tensor(name, shape, dt))

    def sbA1(name, shape, dt):
        return esA1sb.enter_context(nc.sbuf_tensor(name, shape, dt))

    def sbB(name, shape, dt):
        return esBsb.enter_context(nc.sbuf_tensor(name, shape, dt))

    def sbQ(name, shape, dt):
        return esQsb.enter_context(nc.sbuf_tensor(name, shape, dt))

    def sbC(name, shape, dt):
        return esCsb.enter_context(nc.sbuf_tensor(name, shape, dt))

    ident_s = sb("ident_s", [128, 128], BF16)
    ones_s = sb("ones_s", [128, 128], F32)
    bias_s = sb("bias_s", [128, 1], F32)
    idx_s = sb("idx_s", [128, 64], U32)

    w_ring = sbQ("w_ring", [128, 2 * D], F32)
    wabs = sbQ("wabs", [128, 16], F32)
    wabs1 = sbQ("wabs1", [128, 1], F32)
    thr_b = sbQ("thr_b", [128, 1], F32)
    thrn_b = sbQ("thrn_b", [128, 1], F32)

    lnw_s = sbA("lnw_s", [128, D], F32)
    lnb_s = sbA("lnb_s", [128, D], F32)
    x_ring = sbA("x_ring", [128, 4 * D], F32)
    xn_ring = sbA("xn_ring", [128, 4 * D], BF16)
    varp = sbA("varp", [128, 4], F32)
    rstd = sbA("rstd", [128, 4], F32)
    negmp = sbA("negmp", [128, 4], F32)
    bnst = sbA("bnst", [128, 96], F32)
    bnag = sbA("bnag", [128, 8], F32)
    v2p = sbA("v2p", [128, 4], F32)
    r2f = sbA("r2f", [128, 4], F32)
    ln2t1 = sbA("ln2t1", [128, D], F32)
    ln2t2 = sbA("ln2t2", [128, D], F32)
    ln2bf = sbA("ln2bf", [128, 2 * D], BF16)
    lnfm_st = sbA("lnfm_st", [128, 2 * 2048], BF16)
    xnfm = sbA1("xnfm", [128, 2 * 16 * 512], BF16)
    wq_s = sbA1("wq_s", [128, 16 * 512], BF16)
    wk_s = sbA1("wk_s", [128, 16 * 128], BF16)
    wv_s = sbA1("wv_s", [128, 16 * 128], BF16)
    rc_s = sbA1("rc_s", [64, 2 * 512], F32)
    rs_s = sbA1("rs_s", [64, 2 * 512], F32)
    rt1 = sbA1("rt1", [64, 512], F32)
    rt2 = sbA1("rt2", [64, 512], F32)
    ktg = sbA1("ktg", [128, 2 * 512], BF16)
    vtg = sbA1("vtg", [128, 2 * 4 * 132], BF16)
    qtg = sbA1("qtg", [128, 4 * 512], BF16)
    mx_in = sbA1("mx_in", [128, 2 * 512], BF16)
    sqt = sbA1("sqt", [128, 2 * 512], F32)
    mxsb = sbA1("mxsb", [1, 12 * 512], BF16)
    mkq = sbA1("mkq", [1, 4], F32)

    ms = {}

    with nc.Block() as block:
        def init_sync(e):
            for dst, srcp in ((ident_s, identd), (ones_s, onesd), (lnw_s, lnwb),
                              (lnb_s, lnbb), (idx_s, agidxd)):
                dinc(e.dma_start(out=dst[:], in_=srcp[:]), sInit)
        block.sync(init_sync)

        def init_gp(e):
            dinc(e.dma_start(out=wq_s[:].rearrange("p (c n) -> p c n", c=16),
                             in_=wq.rearrange("(c p) n -> p c n", p=128)), sWI)
            dinc(e.dma_start(out=wk_s[:].rearrange("p (c n) -> p c n", c=16),
                             in_=wk.rearrange("(c p) n -> p c n", p=128)), sWI)
            dinc(e.dma_start(out=wv_s[:].rearrange("p (c n) -> p c n", c=16),
                             in_=wv.rearrange("(c p) n -> p c n", p=128)), sWI)
            dinc(e.dma_start(out=wob_dram, in_=wo), sWI)
        block.gpsimd(init_gp)

        def init_v(e):
            e.memset(mkq[:], 1e-12)
            e.memset(vtg[:], 0.0)
            vv = vtg[:].rearrange("p (c n) -> p c n", c=8)
            e.memset(vv[:, :, 128:129], 1.0).then_inc(sV.h, 1)
            sV.v += 1
        block.vector(init_v)
        ms["initv"] = sV.v

        esA = ExitStack()
        psT0 = esA.enter_context(nc.psum_tensor("psT0", [128, 1024], BF16))
        psT1 = esA.enter_context(nc.psum_tensor("psT1", [128, 1024], BF16))
        psQ0 = esA.enter_context(nc.psum_tensor("psQ0", [128, 512], F32))
        psQ1 = esA.enter_context(nc.psum_tensor("psQ1", [128, 512], F32))
        psK = esA.enter_context(nc.psum_tensor("psK", [128, 512], F32))
        psVb = esA.enter_context(nc.psum_tensor("psVb", [128, 512], F32))
        psM0 = esA.enter_context(nc.psum_tensor("psM0", [1, 512], F32))
        psM1 = esA.enter_context(nc.psum_tensor("psM1", [1, 512], F32))

        x_cons = {}
        xn_ms = {}
        evac_ms = {}
        tb_v = {"T0": 0, "T1": 0}
        tb_p = {"T0": 0, "T1": 0}
        mm_ms = {}
        rope_ms = {}
        st_ms = {}
        ld_ms = {}
        tr_ms = {}

        def ln_group(tiles, consume_key, tr_guard=None):
            for p0 in (0, 2):
                pair = tiles[p0:p0 + 2]
                cols = [t % 4 for t in pair]
                c0, c1 = min(cols), max(cols) + 1

                def v1(e, pair=pair, cols=cols, c0=c0, c1=c1):
                    dwait(e, sXA)
                    for t, cc in zip(pair, cols):
                        xs = x_ring[:, (t % 4) * D:(t % 4 + 1) * D]
                        for q in range(4):
                            ii = e.bn_stats(bnst[:, (cc * 4 + q) * 6:(cc * 4 + q + 1) * 6],
                                            xs[:, q * 512:(q + 1) * 512])
                    ii.then_inc(sV.h, 1); sV.v += 1
                    e.wait_ge(sV.h, sV.v)
                    for cc in cols:
                        ii = e.bn_aggr(bnag[:, cc * 2:cc * 2 + 2],
                                       bnst[:, cc * 24:(cc + 1) * 24])
                    ii.then_inc(sV.h, 1); sV.v += 1
                    e.wait_ge(sV.h, sV.v)
                    for cc in cols:
                        ii = e.tensor_scalar(varp[:, cc:cc + 1],
                                             bnag[:, cc * 2 + 1:cc * 2 + 2],
                                             1.0, EPS, op0=ALU.mult, op1=ALU.add)
                    ii.then_inc(sV.h, 1); sV.v += 1
                    e.wait_ge(sV.h, sV.v)
                    ii = e.reciprocal(rstd[:, c0:c1], varp[:, c0:c1])
                    ii.then_inc(sV.h, 1); sV.v += 1
                block.vector(v1)
                mv = sV.v

                def s1(e, c0=c0, c1=c1, mv=mv):
                    e.wait_ge(sV.h, mv)
                    e.activation(out=rstd[:, c0:c1], in_=rstd[:, c0:c1],
                                 func=AF.Sqrt).then_inc(sA.h, 1)
                    sA.v += 1
                block.scalar(s1)
                ma = sA.v

                def v2(e, c0=c0, c1=c1, ma=ma):
                    e.wait_ge(sA.h, ma)
                    for cc in range(c0, c1):
                        ii = e.tensor_scalar(negmp[:, cc:cc + 1], bnag[:, 2 * cc:2 * cc + 1],
                                             rstd[:, cc:cc + 1], -1.0,
                                             op0=ALU.mult, op1=ALU.mult)
                    ii.then_inc(sV.h, 1); sV.v += 1
                block.vector(v2)
                mneg = sV.v

                def s2(e, pair=pair, cols=cols, mneg=mneg):
                    e.wait_ge(sV.h, mneg)
                    if tr_guard is not None:
                        for t in pair:
                            gv = tr_guard(t)
                            if gv:
                                e.wait_ge(sP.h, gv)
                    for t, cc in zip(pair, cols):
                        xs = x_ring[:, (t % 4) * D:(t % 4 + 1) * D]
                        ii = e.activation(out=xn_ring[:, (t % 4) * D:(t % 4 + 1) * D],
                                          in_=xs, func=AF.Identity,
                                          bias=negmp[:, cc:cc + 1],
                                          scale=rstd[:, cc:cc + 1])
                        ii.then_inc(sA.h, 1); sA.v += 1
                        xn_ms[t] = sA.v
                        if consume_key is not None:
                            ms[(consume_key, t)] = sA.v
                block.scalar(s2)

        def rope6(e, psrc, d0, d1, g):
            sl = (g % 2) * 512
            cs = rc_s[:, sl:sl + 512]
            sn = rs_s[:, sl:sl + 512]
            e.tensor_tensor(rt1[:], psrc[0:64, :], cs, op=ALU.mult)
            ii = e.tensor_tensor(rt2[:], psrc[64:128, :], sn, op=ALU.mult)
            ii.then_inc(sV.h, 1); sV.v += 1
            e.wait_ge(sV.h, sV.v)
            ii = e.tensor_tensor(d0, rt1[:], rt2[:], op=ALU.subtract)
            ii.then_inc(sV.h, 1); sV.v += 1
            e.wait_ge(sV.h, sV.v)
            e.tensor_tensor(rt1[:], psrc[0:64, :], sn, op=ALU.mult)
            ii = e.tensor_tensor(rt2[:], psrc[64:128, :], cs, op=ALU.mult)
            ii.then_inc(sV.h, 1); sV.v += 1
            e.wait_ge(sV.h, sV.v)
            ii = e.tensor_tensor(d1, rt1[:], rt2[:], op=ALU.add)
            ii.then_inc(sV.h, 1); sV.v += 1

        mx_n = [0]
        mx_rn = [0]

        def emit_mx_reds(upto):
            while mx_rn[0] < upto:
                n = mx_rn[0]
                mx_rn[0] += 1
                kind = "k" if n % 5 == 0 else "q"

                def mx_red(e, n=n, kind=kind):
                    e.wait_ge(sA.h, ms[("mxcp", n)])
                    acc = mkq[0:1, 0:1] if kind == "k" else mkq[0:1, 1:2]
                    scr = mkq[0:1, 2:3]
                    sl = (n % 12) * 512
                    ii = e.reduce_max(scr, mxsb[0:1, sl:sl + 512], axis=AX.X)
                    ii.then_inc(sV.h, 1); sV.v += 1
                    e.wait_ge(sV.h, sV.v)
                    ii = e.tensor_tensor(acc, acc, scr, op=ALU.max)
                    ii.then_inc(sV.h, 1); sV.v += 1
                    ms[("mxred", n)] = sV.v
                block.vector(mx_red)

        def emit_mx_items(g):
            for item in [("k", 0, g)] + [("q", j, g) for j in range(GQ)]:
                kind, j, i = item
                n = mx_n[0]
                mx_n[0] += 1
                slot = n % 2
                pslot = n % 2

                def mx_ld(e, n=n, kind=kind, j=j, i=i, slot=slot):
                    if kind == "k":
                        dwait(e, sKST[i % 2])
                    else:
                        dwait(e, sQST[j])
                    if ("mxsq", n - 2) in ms:
                        e.wait_ge(sA.h, ms[("mxsq", n - 2)])
                    srcp = k_dram[:, i * 512:(i + 1) * 512] if kind == "k" else \
                        qt_dram[j][:, i * 512:(i + 1) * 512]
                    dinc(e.dma_start(out=mx_in[:, slot * 512:(slot + 1) * 512],
                                     in_=srcp), sMX[slot])
                block.gpsimd(mx_ld)

                def mx_sq(e, n=n, slot=slot, pslot=pslot):
                    dwait(e, sMX[slot])
                    if ("mxmm", n - 2) in ms:
                        e.wait_ge(sP.h, ms[("mxmm", n - 2)])
                    e.activation(out=sqt[:, pslot * 512:(pslot + 1) * 512],
                                 in_=mx_in[:, slot * 512:(slot + 1) * 512],
                                 func=AF.Square).then_inc(sA.h, 1)
                    sA.v += 1
                    ms[("mxsq", n)] = sA.v
                block.scalar(mx_sq)

                def mx_mm(e, n=n, slot=slot, pslot=pslot):
                    if n == 0:
                        dwait(e, sInit)
                    e.wait_ge(sA.h, ms[("mxsq", n)])
                    if ("mxcp", n - 2) in ms:
                        e.wait_ge(sA.h, ms[("mxcp", n - 2)])
                    psm = psM0 if pslot == 0 else psM1
                    ii = e.matmul(psm[:], ones_s[:, 0:1], sqt[:, pslot * 512:(pslot + 1) * 512],
                                  start=True, stop=True)
                    ii.then_inc(sP.h, 1); sP.v += 1
                    ms[("mxmm", n)] = sP.v
                block.tensor(mx_mm)

                def mx_cp(e, n=n, pslot=pslot):
                    e.wait_ge(sP.h, ms[("mxmm", n)])
                    if ("mxred", n - 12) in ms:
                        e.wait_ge(sV.h, ms[("mxred", n - 12)])
                    psm = psM0 if pslot == 0 else psM1
                    sl = (n % 12) * 512
                    e.copy(out=mxsb[0:1, sl:sl + 512], in_=psm[:]).then_inc(sA.h, 1)
                    sA.v += 1
                    ms[("mxcp", n)] = sA.v
                block.scalar(mx_cp)


        def qz_ld1(c):
            def body(e, c=c):
                if ("qred", c - 2) in ms:
                    e.wait_ge(sV.h, ms[("qred", c - 2)])
                dinc(e.dma_start(out=w_ring[:, (c % 2) * D:(c % 2 + 1) * D],
                                 in_=wf[c * 128:(c + 1) * 128, :]), sWR[c % 2])
            block.gpsimd(body)

        def qz_red(c):
            def body(e, c=c):
                dwait(e, sWR[c % 2])
                ii = e.reduce_sum(wabs[:, c:c + 1],
                                  w_ring[:, (c % 2) * D:(c % 2 + 1) * D],
                                  axis=AX.X, apply_absolute_value=True)
                ii.then_inc(sV.h, 1); sV.v += 1
                ms[("qred", c)] = sV.v
            block.vector(body)

        for g in range(NG):
            tiles = [4 * g + i for i in range(4)]

            qz_ld1(g)
            if g >= 1:
                qz_red(g - 1)

            if g >= 2:
                emit_mx_reds(5 * (g - 1))
            if g >= 1:
                emit_mx_items(g - 1)

            def a_ld(e, g=g, tiles=tiles):
                for t in tiles:
                    if ("xcons", t - 4) in ms:
                        e.wait_ge(sA.h, ms[("xcons", t - 4)])
                    dinc(e.dma_start(out=x_ring[:, (t % 4) * D:(t % 4 + 1) * D],
                                     in_=xb[t * 128:(t + 1) * 128, :]), sXA)
                if ("ropeuse", g - 2) in ms:
                    e.wait_ge(sV.h, ms[("ropeuse", g - 2)])
                sl = (g % 2) * 512
                dinc(e.dma_start(out=rc_s[:, sl:sl + 512],
                                 in_=rcd[:, g * 512:(g + 1) * 512]), sRP[g % 2])
                dinc(e.dma_start(out=rs_s[:, sl:sl + 512],
                                 in_=rsd[:, g * 512:(g + 1) * 512]), sRP[g % 2])
            block.sync(a_ld)

            def a_lnw(e, g=g, tiles=tiles):
                dwait(e, sXA)
                if g == 0:
                    dwait(e, sInit)
            block.vector(a_lnw)
            ln_group(tiles, "xcons", tr_guard=lambda t: tr_ms.get(t - 4))

            for t in tiles:
                def a_tr(e, t=t, g=g):
                    if g == 0 and t == tiles[0]:
                        dwait(e, sInit)
                    e.wait_ge(sA.h, xn_ms[t])
                    for half, bank, bn in ((0, psT0, "T0"), (1, psT1, "T1")):
                        if tb_v[bn]:
                            e.wait_ge(sV.h, tb_v[bn])
                        for i in range(8):
                            c = half * 8 + i
                            ii = e.transpose(bank[:, i * 128:(i + 1) * 128],
                                             xn_ring[:, (t % 4) * D + c * 128:(t % 4) * D + (c + 1) * 128],
                                             ident_s[:])
                        ii.then_inc(sP.h, 1); sP.v += 1
                        tb_p[bn] = sP.v
                    tr_ms[t] = sP.v
                block.tensor(a_tr)

                def a_ev(e, t=t, g=g):
                    if ("mmdone", g - 2) in ms:
                        e.wait_ge(sP.h, ms[("mmdone", g - 2)])
                    base = (g % 2) * 16 * 512
                    xv = xnfm[:, base:base + 16 * 512].rearrange("p (c n) -> p c n", c=16)
                    for half, bank, bn in ((0, psT0, "T0"), (1, psT1, "T1")):
                        e.wait_ge(sP.h, tb_p[bn])
                        dst = xv[:, half * 8:(half + 1) * 8, (t % 4) * 128:(t % 4 + 1) * 128]
                        ii = e.tensor_copy(dst, bank[:].rearrange("p (c n) -> p c n", c=8))
                        ii.then_inc(sV.h, 1); sV.v += 1
                        tb_v[bn] = sV.v
                    evac_ms[t] = sV.v
                block.vector(a_ev)

            base = (g % 2) * 16 * 512

            def a_kmm(e, g=g, base=base):
                e.wait_ge(sV.h, evac_ms[4 * g + 3])
                if g == 0:
                    dwait(e, sWI)
                if ("ropeK", g - 1) in rope_ms:
                    e.wait_ge(sV.h, rope_ms[("ropeK", g - 1)])
                for c in range(16):
                    ii = e.matmul(psK[:], wk_s[:, c * 128:(c + 1) * 128],
                                  xnfm[:, base + c * 512:base + (c + 1) * 512],
                                  start=(c == 0), stop=(c == 15))
                ii.then_inc(sP.h, 1); sP.v += 1
                mm_ms[("K", g)] = sP.v
            block.tensor(a_kmm)

            def a_krope(e, g=g):
                dwait(e, sRP[g % 2])
                e.wait_ge(sP.h, mm_ms[("K", g)])
                dwait(e, sKST[g % 2])
                kb = (g % 2) * 512
                rope6(e, psK, ktg[0:64, kb:kb + 512], ktg[64:128, kb:kb + 512], g)
                rope_ms[("ropeK", g)] = sV.v
            block.vector(a_krope)

            def a_kst(e, g=g):
                e.wait_ge(sV.h, rope_ms[("ropeK", g)])
                kb = (g % 2) * 512
                dinc(e.dma_start(out=k_dram[:, g * 512:(g + 1) * 512],
                                 in_=ktg[:, kb:kb + 512]), sKST[g % 2])
            block.gpsimd(a_kst)

            def a_vmm(e, g=g, base=base):
                if ("vev", g - 1) in rope_ms:
                    e.wait_ge(sV.h, rope_ms[("vev", g - 1)])
                for ts in range(4):
                    for c in range(16):
                        ii = e.matmul(psVb[:, ts * 128:(ts + 1) * 128],
                                      xnfm[:, base + c * 512 + ts * 128:base + c * 512 + (ts + 1) * 128],
                                      wv_s[:, c * 128:(c + 1) * 128],
                                      start=(ts == 0 and c == 0), stop=(c == 15),
                                      skip_group_check=True)
                ii.then_inc(sP.h, 1); sP.v += 1
                mm_ms[("V", g)] = sP.v
            block.tensor(a_vmm)

            def a_vev(e, g=g):
                e.wait_ge(sP.h, mm_ms[("V", g)])
                dwait(e, sVST[g % 2])
                vv = vtg[:].rearrange("p (c n) -> p c n", c=8)
                sl = (g % 2) * 4
                ii = e.tensor_copy(vv[:, sl:sl + 4, 0:128],
                                   psVb[:].rearrange("p (c n) -> p c n", c=4))
                ii.then_inc(sV.h, 1); sV.v += 1
                rope_ms[("vev", g)] = sV.v
            block.vector(a_vev)

            def a_vst(e, g=g):
                e.wait_ge(sV.h, rope_ms[("vev", g)])
                vv = vtg[:].rearrange("p (c n) -> p c n", c=8)
                sl = (g % 2) * 4
                dinc(e.dma_start(out=v_dram[4 * g:4 * g + 4].rearrange("c p d -> p c d"),
                                 in_=vv[:, sl:sl + 4, :]), sVST[g % 2])
            block.gpsimd(a_vst)

            for j in range(GQ):
                def a_qmm(e, g=g, j=j, base=base):
                    key = ("ropeQ", g, j - 2) if j >= 2 else ("ropeQ", g - 1, j + 2)
                    if key in rope_ms:
                        e.wait_ge(sV.h, rope_ms[key])
                    bankq = psQ0 if j % 2 == 0 else psQ1
                    for c in range(16):
                        ii = e.matmul(bankq[:], wq_s[:, c * 512 + j * 128:c * 512 + (j + 1) * 128],
                                      xnfm[:, base + c * 512:base + (c + 1) * 512],
                                      start=(c == 0), stop=(c == 15))
                    ii.then_inc(sP.h, 1); sP.v += 1
                    mm_ms[("Q", g, j)] = sP.v
                block.tensor(a_qmm)

                def a_qrope(e, g=g, j=j):
                    dwait(e, sRP[g % 2])
                    e.wait_ge(sP.h, mm_ms[("Q", g, j)])
                    dwait(e, sQST[j])
                    bankq = psQ0 if j % 2 == 0 else psQ1
                    qb = j * 512
                    rope6(e, bankq, qtg[0:64, qb:qb + 512], qtg[64:128, qb:qb + 512], g)
                    rope_ms[("ropeQ", g, j)] = sV.v
                block.vector(a_qrope)

                def a_qst(e, g=g, j=j):
                    e.wait_ge(sV.h, rope_ms[("ropeQ", g, j)])
                    dinc(e.dma_start(out=qt_dram[j][:, g * 512:(g + 1) * 512],
                                     in_=qtg[:, j * 512:(j + 1) * 512]), sQST[j])
                block.gpsimd(a_qst)

            ms[("mmdone", g)] = sP.v
            ms[("ropeuse", g)] = sV.v

        qz_red(NG - 1)
        emit_mx_items(NG - 1)
        emit_mx_reds(5 * NG)

        def mx_fin_v(e):
            e.wait_ge(sV.h, ms[("mxred", mx_n[0] - 1)])
            ii = e.tensor_tensor(mkq[0:1, 2:3], mkq[0:1, 0:1], mkq[0:1, 1:2], op=ALU.mult)
            ii.then_inc(sV.h, 1); sV.v += 1
            ms["mxmul"] = sV.v
        block.vector(mx_fin_v)

        def mx_fin_a(e):
            e.wait_ge(sV.h, ms["mxmul"])
            e.activation(out=mkq[0:1, 2:3], in_=mkq[0:1, 2:3],
                         func=AF.Sqrt).then_inc(sA.h, 1)
            sA.v += 1
            ms["mxsqrt"] = sA.v
        block.scalar(mx_fin_a)

        def mx_fin_v2(e):
            e.wait_ge(sA.h, ms["mxsqrt"])
            ii = e.tensor_scalar(mkq[0:1, 2:3], mkq[0:1, 2:3], -SCALE, None, op0=ALU.mult)
            ii.then_inc(sV.h, 1); sV.v += 1
            ms["mxneg"] = sV.v
        block.vector(mx_fin_v2)

        def mx_bc(e):
            e.wait_ge(sV.h, ms["mxneg"])
            ii = e.matmul(psK[:, 0:1], ones_s[0:1, :], mkq[0:1, 2:3], start=True, stop=True)
            ii.then_inc(sP.h, 1); sP.v += 1
            ms["mxbc"] = sP.v
        block.tensor(mx_bc)

        def mx_bcv(e):
            e.wait_ge(sP.h, ms["mxbc"])
            ii = e.tensor_copy(bias_s[:], psK[:, 0:1])
            ii.then_inc(sV.h, 1); sV.v += 1
            ms["biasdone"] = sV.v
        block.vector(mx_bcv)

        def qz_sum_v(e):
            e.wait_ge(sV.h, ms[("qred", 15)])
            ii = e.reduce_sum(wabs1[:], wabs[:], axis=AX.X)
            ii.then_inc(sV.h, 1); sV.v += 1
            ms["qsum"] = sV.v
        block.vector(qz_sum_v)

        def qz_mm(e):
            e.wait_ge(sV.h, ms["qsum"])
            e.wait_ge(sA.h, ms[("mxcp", mx_n[0] - 1)])
            ii = e.matmul(psM0[0:1, 0:1], wabs1[:], ones_s[:, 0:1],
                          start=True, stop=True)
            ii.then_inc(sP.h, 1); sP.v += 1
            ms["qtot"] = sP.v
        block.tensor(qz_mm)

        def qz_thr_v(e):
            e.wait_ge(sP.h, ms["qtot"])
            ii = e.tensor_scalar(wabs1[0:1, 0:1], psM0[0:1, 0:1], 0.5 / (D * D),
                                 None, op0=ALU.mult)
            ii.then_inc(sV.h, 1); sV.v += 1
            ms["qthr1"] = sV.v
        block.vector(qz_thr_v)

        def qz_bc(e):
            e.wait_ge(sV.h, ms["qthr1"])
            e.wait_ge(sV.h, ms["biasdone"])
            ii = e.matmul(psK[:, 2:3], ones_s[0:1, :], wabs1[0:1, 0:1],
                          start=True, stop=True)
            ii.then_inc(sP.h, 1); sP.v += 1
            ms["qbc"] = sP.v
        block.tensor(qz_bc)

        def qz_bcv(e):
            e.wait_ge(sP.h, ms["qbc"])
            ii = e.tensor_copy(thr_b[:], psK[:, 2:3])
            ii.then_inc(sV.h, 1); sV.v += 1
            e.wait_ge(sV.h, sV.v)
            ii = e.tensor_scalar(thrn_b[:], thr_b[:], -1.0, None, op0=ALU.mult)
            ii.then_inc(sV.h, 1); sV.v += 1
            ms["qthr"] = sV.v
        block.vector(qz_bcv)

        esA.close()
        esA1sb.close()

        esB = ExitStack()
        psS_ = [esB.enter_context(nc.psum_tensor(f"psS{i}", [128, 1024], F32)) for i in range(2)]
        psO_ = [esB.enter_context(nc.psum_tensor(f"psO{i}", [128, 512], F32)) for i in range(3)]
        psT2b = esB.enter_context(nc.psum_tensor("psT2b", [128, 1024], BF16))

        mstrip_s = sbB("mstrip_s", [128, MASKW], BF16)
        qt_sb = sbB("qt_sb", [128, 2 * 1024], BF16)
        ksc_sb = sbB("ksc_sb", [128, 2 * 1024], BF16)
        vsc_sb = sbB("vsc_sb", [128, 2 * 8 * 132], BF16)
        pt_sb = sbB("pt_sb", [128, 4 * 1024], BF16)
        linv_sb = sbB("linv_sb", [128, 16], F32)
        a_sb = sbB("a_sb", [128, 2 * 8 * 128], BF16)
        araw = sbB("araw", [128, 2 * 1280], F32)
        araw_v = araw[:].rearrange("p (g w) -> p g w", w=160)
        gpbuf = sbB("gpbuf", [128, D], BF16)
        gnbuf = sbB("gnbuf", [128, D], BF16)
        wqbuf = sbB("wqbuf", [128, 2 * D], BF16)

        def qz_p2(c):
            def qz_ld2(e, c=c):
                if c < 2:
                    e.wait_ge(sV.h, ms[("qred", 15)])
                else:
                    e.wait_ge(sV.h, ms[("qp2", c - 2)])
                dinc(e.dma_start(out=w_ring[:, (c % 2) * D:(c % 2 + 1) * D],
                                 in_=wf[c * 128:(c + 1) * 128, :]), sWR[c % 2])
            block.gpsimd(qz_ld2)

            def qz_q(e, c=c):
                dwait(e, sWR[c % 2])
                if c == 0:
                    e.wait_ge(sV.h, ms["qthr"])
                dwait(e, sWST[c % 2])
                ws = w_ring[:, (c % 2) * D:(c % 2 + 1) * D]
                e.tensor_scalar(gpbuf[:], ws, thr_b[:, 0:1], None, op0=ALU.is_ge)
                ii = e.tensor_scalar(gnbuf[:], ws, thrn_b[:, 0:1], None, op0=ALU.is_le)
                ii.then_inc(sV.h, 1); sV.v += 1
                e.wait_ge(sV.h, sV.v)
                ii = e.tensor_tensor(wqbuf[:, (c % 2) * D:(c % 2 + 1) * D],
                                     gpbuf[:], gnbuf[:], op=ALU.subtract)
                ii.then_inc(sV.h, 1); sV.v += 1
                ms[("qp2", c)] = sV.v
            block.vector(qz_q)

            def qz_st(e, c=c):
                e.wait_ge(sV.h, ms[("qp2", c)])
                dinc(e.dma_start(out=wfq_dram[c * 128:(c + 1) * 128, :],
                                 in_=wqbuf[:, (c % 2) * D:(c % 2 + 1) * D]), sWST[c % 2])
            block.gpsimd(qz_st)

        qz_done = [0]

        def qz_emit(tj):
            if tj % 2 == 0 and qz_done[0] < 16:
                qz_p2(qz_done[0])
                qz_done[0] += 1

        ln2tr_ms = {}
        tb2 = {"p": 0, "v": 0}

        def emit_a2_group(g2):
            tiles = [4 * g2 + i for i in range(4)]

            def a2_ld(e, g2=g2, tiles=tiles):
                for t in tiles:
                    if ("a2cons", t - 4) in ms:
                        e.wait_ge(sV.h, ms[("a2cons", t - 4)])
                    elif g2 == 0:
                        e.wait_ge(sA.h, ms[("xcons", 60 + (t % 4))])
                    dinc(e.dma_start(out=x_ring[:, (t % 4) * D:(t % 4 + 1) * D],
                                     in_=xown[t * 128:(t + 1) * 128, :]), sXA)
            block.sync(a2_ld)

            def a2_w(e, g2=g2, tiles=tiles):
                dwait(e, sXA)
            block.vector(a2_w)
            ln_group(tiles, None,
                     tr_guard=(lambda t: tr_ms.get(60 + (t % 4))) if g2 == 0 else None)

            for p0 in (0, 2):
                pair = tiles[p0:p0 + 2]
                cols = [t % 4 for t in pair]
                c0, c1 = min(cols), max(cols) + 1

                def a2_v1(e, pair=pair, c0=c0, c1=c1):
                    e.wait_ge(sA.h, xn_ms[pair[-1]])
                    ii = e.scalar_tensor_tensor(v2p[:, c0:c1], varp[:, c0:c1], -EPS,
                                                rstd[:, c0:c1], op0=ALU.add, op1=ALU.mult)
                    ii.then_inc(sV.h, 1); sV.v += 1
                    e.wait_ge(sV.h, sV.v)
                    for cc in range(c0, c1):
                        ii = e.tensor_scalar(v2p[:, cc:cc + 1], v2p[:, cc:cc + 1],
                                             rstd[:, cc:cc + 1], EPS,
                                             op0=ALU.mult, op1=ALU.add)
                    ii.then_inc(sV.h, 1); sV.v += 1
                    e.wait_ge(sV.h, sV.v)
                    ii = e.reciprocal(r2f[:, c0:c1], v2p[:, c0:c1])
                    ii.then_inc(sV.h, 1); sV.v += 1
                    ms["a2mv"] = sV.v
                block.vector(a2_v1)
                mv = sV.v

                def a2_s(e, c0=c0, c1=c1, mv=mv):
                    e.wait_ge(sV.h, mv)
                    e.activation(out=r2f[:, c0:c1], in_=r2f[:, c0:c1],
                                 func=AF.Sqrt).then_inc(sA.h, 1)
                    sA.v += 1
                block.scalar(a2_s)
                mact = sA.v

                def a2_v2(e, pair=pair, mact=mact):
                    e.wait_ge(sA.h, mact)
                    for t in pair:
                        cc = t % 4
                        ii = e.tensor_tensor(r2f[:, cc:cc + 1], r2f[:, cc:cc + 1],
                                             rstd[:, cc:cc + 1], op=ALU.mult)
                    ii.then_inc(sV.h, 1); sV.v += 1
                    e.wait_ge(sV.h, sV.v)
                    for t in pair:
                        cc = t % 4
                        xs = x_ring[:, cc * D:(cc + 1) * D]
                        ii = e.tensor_scalar(ln2t1[:], xs, bnag[:, 2 * cc:2 * cc + 1],
                                             r2f[:, cc:cc + 1], op0=ALU.subtract, op1=ALU.mult)
                        ii.then_inc(sV.h, 1); sV.v += 1
                        e.wait_ge(sV.h, sV.v)
                        ii = e.tensor_tensor(ln2t2[:], ln2t1[:], lnw_s[:], op=ALU.mult)
                        ii.then_inc(sV.h, 1); sV.v += 1
                        e.wait_ge(sV.h, sV.v)
                        if ("ln2tr", t - 2) in ln2tr_ms:
                            e.wait_ge(sP.h, ln2tr_ms[("ln2tr", t - 2)])
                        ii = e.tensor_tensor(ln2bf[:, (t % 2) * D:(t % 2 + 1) * D],
                                             ln2t2[:], lnb_s[:], op=ALU.add)
                        ii.then_inc(sV.h, 1); sV.v += 1
                        ms[("ln2bf", t)] = sV.v
                        ms[("a2cons", t)] = sV.v
                block.vector(a2_v2)
                for t in pair:
                    for half in (0, 1):
                        def a2_tr(e, t=t, half=half):
                            if half == 0:
                                e.wait_ge(sV.h, ms[("ln2bf", t)])
                            if tb2["v"]:
                                e.wait_ge(sV.h, tb2["v"])
                            for i in range(8):
                                c = half * 8 + i
                                ii = e.transpose(psT2b[:, i * 128:(i + 1) * 128],
                                                 ln2bf[:, (t % 2) * D + c * 128:(t % 2) * D + (c + 1) * 128],
                                                 ident_s[:])
                            ii.then_inc(sP.h, 1); sP.v += 1
                            tb2["p"] = sP.v
                        block.tensor(a2_tr)
                        ln2tr_ms[("ln2tr", t)] = sP.v

                        def a2_ev(e, t=t, half=half):
                            if half == 0:
                                dwait(e, sLNST[t % 2])
                            e.wait_ge(sP.h, tb2["p"])
                            dst = lnfm_st[:, (t % 2) * 2048 + half * 1024:(t % 2) * 2048 + (half + 1) * 1024] \
                                .rearrange("p (c n) -> p c n", c=8)
                            ii = e.tensor_copy(dst, psT2b[:].rearrange("p (c n) -> p c n", c=8))
                            ii.then_inc(sV.h, 1); sV.v += 1
                            tb2["v"] = sV.v
                        block.vector(a2_ev)
                        ms[("lnev", t)] = sV.v

                    def a2_st(e, t=t, g2=g2):
                        e.wait_ge(sV.h, ms[("lnev", t)])
                        lnview = ln2_dram[g2].rearrange("p (c n) -> p c n", c=16)
                        for half in range(2):
                            dst = lnview[:, half * 8:(half + 1) * 8, (t % 4) * 128:(t % 4 + 1) * 128]
                            src = lnfm_st[:, (t % 2) * 2048 + half * 1024:(t % 2) * 2048 + (half + 1) * 1024] \
                                .rearrange("p (c n) -> p c n", c=8)
                            dinc(e.dma_start(out=dst, in_=src), sLNST[t % 2])
                    block.gpsimd(a2_st)



        def b_join(e):
            for s_ in (sKST[0], sKST[1], sVST[0], sVST[1], sQST[0], sQST[1],
                       sQST[2], sQST[3], sLNST[0], sLNST[1]):
                dwait(e, s_)
            e.wait_ge(sV.h, ms["biasdone"])
            dinc(e.dma_start(out=mstrip_s[:], in_=mstripd[:]), sMSK)
        block.sync(b_join)

        def emit_sc_load(e, sc, occ, scld_ms):
            slot = sc % 2
            if occ is not None:
                e.wait_ge(sP.h, occ)
            dinc(e.dma_start(out=ksc_sb[:, slot * 1024:(slot + 1) * 1024],
                             in_=k_dram[:, sc * 1024:(sc + 1) * 1024]), sSC[slot])
            vv = vsc_sb[:].rearrange("p (c n) -> p c n", c=16)
            dinc(e.dma_start(out=vv[:, slot * 8:(slot + 1) * 8, :],
                             in_=v_dram[8 * sc:8 * sc + 8].rearrange("c p d -> p c d")),
                 sSC[slot])
            scld_ms[sc] = sSC[slot].v

        sc_occ = [None, None]
        qk_last = {}
        pv_last = {}
        exp_last = {}
        norm_ms = {}
        evacB_ms = {}
        recip_ms = {}

        tj = 0
        for t in range(NT):
            nk = 8 * (t + 1)
            nsc = t + 1
            for j in range(GQ):
                qz_emit(tj)
                slot = tj % 2
                sPb, sAb, sVb = sP.v, sA.v, sV.v
                scld_ms = {}
                qk_ms = {0: sPb + 1}
                pv_ms = {}
                for k in range(1, nk):
                    qk_ms[k] = sPb + 2 * k
                for k in range(nk):
                    pv_ms[k] = sPb + 2 * k + 3 if k < nk - 1 else sPb + 2 * nk
                exp_ms = {k: sAb + k + 1 for k in range(nk)}
                mask_ms = {k: sVb + (k - 8 * t) + 1 for k in range(8 * t, nk)}

                def b_ld(e, t=t, j=j, tj=tj, nsc=nsc, slot=slot,
                         pv_ms=pv_ms, occ0=sc_occ[0], occ1=sc_occ[1],
                         scld_ms=scld_ms):
                    if tj - 2 >= 0:
                        e.wait_ge(sP.h, qk_last[tj - 2])
                    dinc(e.dma_start(out=qt_sb[:, slot * 1024:(slot + 1) * 1024],
                                     in_=qt_dram[j][:, t * 1024:(t + 1) * 1024]), sQT[tj % 2])
                    for sc in range(min(2, nsc)):
                        emit_sc_load(e, sc, occ0 if sc == 0 else occ1, scld_ms)
                    for sc2 in range(2, nsc):
                        emit_sc_load(e, sc2, pv_ms[8 * (sc2 - 2) + 7], scld_ms)
                block.sync(b_ld)

                def b_pe(e, t=t, j=j, tj=tj, nk=nk, slot=slot,
                         qk_ms=qk_ms, pv_ms=pv_ms, exp_ms=exp_ms, scld_ms=scld_ms,
                         mask_ms=mask_ms):
                    qa = qt_sb[:, slot * 1024:(slot + 1) * 1024]
                    vv = vsc_sb[:].rearrange("p (c n) -> p c n", c=16)

                    def emit_qk(k):
                        scs = (k // 8) % 2
                        if k == 0:
                            dwait(e, sQT[tj % 2])
                        if k % 8 == 0:
                            e.wait_ge(sSC[scs].h, scld_ms[k // 8])
                        if k >= 2:
                            e.wait_ge(sA.h, exp_ms[k - 2])
                        elif tj - 1 >= 0:
                            e.wait_ge(sA.h, exp_last[tj - 1])
                        ka = ksc_sb[:, scs * 1024 + (k % 8) * 128:scs * 1024 + (k % 8 + 1) * 128]
                        if not (k >= 8 * t and k - 8 * t >= 4):
                            e.matmul(psS_[k % 2][:, 0:512], ka, qa[:, 0:512],
                                     start=True, stop=True)
                        ii = e.matmul(psS_[k % 2][:, 512:1024], ka, qa[:, 512:1024],
                                      start=True, stop=True)
                        ii.then_inc(sP.h, 1); sP.v += 1
                        assert sP.v == qk_ms[k], (tj, k, sP.v, qk_ms[k])

                    def emit_pv(k):
                        if k >= 8 * t:
                            m = k - 8 * t
                            if k == 0:
                                qs_list = [(qs, qs == m) for qs in range(m, 8)]
                            else:
                                qs_list = [(qs, False) for qs in range(m + 1, 8)] + [(m, True)]
                        else:
                            qs_list = [(qs, False) for qs in range(8)]
                        e.wait_ge(sA.h, exp_ms[k])
                        if k == 0 and 8 * t == 0:
                            e.wait_ge(sV.h, mask_ms[k])
                        if k == 0 and tj - 1 >= 0:
                            e.wait_ge(sV.h, evacB_ms[tj - 1])
                        va = vv[:, ((k // 8) % 2) * 8 + (k % 8), 0:129]
                        for qs, needs_mask in qs_list:
                            if needs_mask and k != 0:
                                e.wait_ge(sV.h, mask_ms[k])
                            bank = psO_[qs // 3]
                            off = 160 * (qs % 3)
                            ii = e.matmul(bank[:, off:off + 129],
                                          pt_sb[:, (k % 4) * 1024 + qs * 128:(k % 4) * 1024 + (qs + 1) * 128],
                                          va, start=(k == 0 and qs % 3 == 0),
                                          stop=(k == 8 * t + qs), skip_group_check=True)
                        ii.then_inc(sP.h, 1); sP.v += 1
                        assert sP.v == pv_ms[k], (tj, k, sP.v, pv_ms[k])

                    emit_qk(0)
                    for k in range(nk):
                        if k + 1 < nk:
                            emit_qk(k + 1)
                        emit_pv(k)
                block.tensor(b_pe)
                qk_last[tj] = qk_ms[nk - 1]
                pv_last[tj] = pv_ms[nk - 1]
                sc_occ[0] = pv_ms[min(8 * (nsc - 1 - (nsc - 1) % 2) + 7, nk - 1)]
                if nsc >= 2:
                    sc_occ[1] = pv_ms[min(8 * (nsc - 1 - (nsc % 2)) + 7, nk - 1)]

                def b_msk(e, t=t, tj=tj, nk=nk, exp_ms=exp_ms, mask_ms=mask_ms):
                    if tj == 0:
                        dwait(e, sMSK)
                    for k in range(8 * t, nk):
                        e.wait_ge(sA.h, exp_ms[k])
                        m = k - 8 * t
                        pts = pt_sb[:, (k % 4) * 1024 + m * 128:(k % 4) * 1024 + (m + 1) * 128]
                        ii = e.tensor_tensor(pts, pts,
                                             mstrip_s[:, 896:1024], op=ALU.mult)
                        ii.then_inc(sV.h, 1); sV.v += 1
                        assert sV.v == mask_ms[k]
                block.vector(b_msk)

                def b_act(e, t=t, tj=tj, nk=nk, qk_ms=qk_ms, pv_ms=pv_ms):
                    if tj == 0:
                        e.wait_ge(sV.h, ms["biasdone"])
                    for k in range(nk):
                        e.wait_ge(sP.h, qk_ms[k])
                        if k >= 4:
                            e.wait_ge(sP.h, pv_ms[k - 4])
                        elif tj - 1 >= 0:
                            e.wait_ge(sP.h, pv_last[tj - 1])
                        c0 = (k - 8 * t) * 128 if k >= 8 * t else 0
                        ii = e.activation(out=pt_sb[:, (k % 4) * 1024 + c0:(k % 4 + 1) * 1024],
                                          in_=psS_[k % 2][:, c0:1024], func=AF.Exp,
                                          bias=bias_s[:], scale=SCALE)
                        ii.then_inc(sA.h, 1); sA.v += 1
                block.scalar(b_act)
                exp_last[tj] = sA.v

                def b_tail(e, tj=tj, nk=nk, pv_ms=pv_ms):
                    e.wait_ge(sP.h, pv_ms[nk - 1])
                    if tj - 2 >= 0:
                        e.wait_ge(sG.h, norm_ms[tj - 2])
                    g0 = (tj % 2) * 8
                    pv0 = psO_[0][:, 0:480].rearrange("p (s w) -> p s w", w=160)
                    pv1 = psO_[1][:, 0:480].rearrange("p (s w) -> p s w", w=160)
                    pv2 = psO_[2][:, 0:320].rearrange("p (s w) -> p s w", w=160)
                    e.tensor_copy(araw_v[:, g0:g0 + 3, 0:129], pv0[:, :, 0:129])
                    e.tensor_copy(araw_v[:, g0 + 3:g0 + 6, 0:129], pv1[:, :, 0:129])
                    ii = e.tensor_copy(araw_v[:, g0 + 6:g0 + 8, 0:129], pv2[:, :, 0:129])
                    ii.then_inc(sV.h, 1); sV.v += 1
                    evacB_ms[tj] = sV.v
                    e.wait_ge(sV.h, sV.v)
                    ii = e.reciprocal(linv_sb[:, g0:g0 + 8], araw_v[:, g0:g0 + 8, 128:129])
                    ii.then_inc(sV.h, 1); sV.v += 1
                    recip_ms[tj] = sV.v
                block.vector(b_tail)

                def b_norm(e, tj=tj):
                    e.wait_ge(sV.h, recip_ms[tj])
                    dwait(e, sAST[tj % 2])
                    g0 = (tj % 2) * 8
                    aslot = (tj % 2) * 8 * 128
                    for qs in range(8):
                        ii = e.tensor_scalar(a_sb[:, aslot + qs * 128:aslot + (qs + 1) * 128],
                                             araw_v[:, g0 + qs, 0:128],
                                             linv_sb[:, g0 + qs:g0 + qs + 1],
                                             None, op0=ALU.mult)
                    ii.then_inc(sG.h, 1); sG.v += 1
                    norm_ms[tj] = sG.v
                block.gpsimd(b_norm)

                def b_ast(e, t=t, j=j, tj=tj):
                    e.wait_ge(sG.h, norm_ms[tj])
                    aslot = (tj % 2) * 8
                    av = a_sb[:].rearrange("p (c n) -> p c n", c=16)
                    dst = a_dram[8 * t:8 * t + 8].rearrange("c p d -> p c d")[:, :, j * 128:(j + 1) * 128]
                    dinc(e.dma_start(out=dst, in_=av[:, aslot:aslot + 8, :]), sAST[tj % 2])
                block.gpsimd(b_ast)

                if tj in (7, 15, 23, 31):
                    part = tj // 8
                    def cc_gp(e, part=part):
                        dwait(e, sAST[0])
                        dwait(e, sAST[1])
                        a_flat = a_dram.rearrange("c p d -> (c p) d")
                        e.collective_compute(
                            "AllGather", ALU.bypass, replica_groups=[list(range(8))],
                            ins=[a_flat[part * 2048:(part + 1) * 2048, :].opt()],
                            outs=[ag_dram[part * 16384:(part + 1) * 16384, :].opt()],
                        ).then_inc(sC.h, 1)
                        sC.v += 1
                    block.gpsimd(cc_gp)

                tj += 1
                if tj in (3, 9, 15, 21):
                    emit_a2_group((tj - 3) // 6)

        while qz_done[0] < 16:
            qz_p2(qz_done[0])
            qz_done[0] += 1
        ms["qzdone"] = ms[("qp2", 15)]

        esB.close()
        esBsb.close()




        esAsb.close()
        esQsb.close()


        esC = ExitStack()
        psT0c = esC.enter_context(nc.psum_tensor("psT0c", [128, 1024], BF16))
        psT1c = esC.enter_context(nc.psum_tensor("psT1c", [128, 1024], BF16))
        psSo = [esC.enter_context(nc.psum_tensor(f"psSo{i}", [128, 512], F32)) for i in range(2)]
        psGo = [esC.enter_context(nc.psum_tensor(f"psGo{i}", [128, 512], F32)) for i in range(2)]

        atok = sbC("atok", [128, 2 * D], BF16)
        afm = sbC("afm", [128, 16 * 512], BF16)
        lnfm_g = sbC("lnfm_g", [128, 2 * 8192], BF16)
        wo_sb = sbC("wo_sb", [128, 2 * 8192], BF16)
        wfq_sb = sbC("wfq_sb", [128, 2 * 8192], BF16)
        xg_sb = sbC("xg_sb", [128, 4 * D], F32)
        g_sl = sbC("g_sl", [128, 512], F32)
        sg_sl = sbC("sg_sl", [128, 512], F32)
        out_sb = sbC("out_sb", [128, 4 * D], F32)

        tbc_v = {"T0": 0, "T1": 0}
        tbc_p = {"T0": 0, "T1": 0}
        last_bank = [None, None]
        last_gsl = None

        def c_wld_emit(e, dq_seq):
            g2w, dqw = dq_seq // 4, dq_seq % 4
            slot = (dq_seq % 2) * 8192
            if dq_seq == 0:
                dwait(e, sWI)
                dwait(e, sWST[0])
                dwait(e, sWST[1])
            if ("wU", dq_seq - 2) in ms:
                e.wait_ge(sP.h, ms[("wU", dq_seq - 2)])
            dinc(e.dma_start(out=wo_sb[:, slot:slot + 8192].rearrange("p (c n) -> p c n", c=16),
                             in_=wob_dram[:, dqw * 512:(dqw + 1) * 512]
                             .rearrange("(c p) n -> p c n", p=128)), sWL[dq_seq % 2])
            dinc(e.dma_start(out=wfq_sb[:, slot:slot + 8192].rearrange("p (c n) -> p c n", c=16),
                             in_=wfq_dram[:, dqw * 512:(dqw + 1) * 512]
                             .rearrange("(c p) n -> p c n", p=128)), sWL[dq_seq % 2])

        for g2 in range(4):
            def c_lnld(e, g2=g2):
                if g2 == 0:
                    e.wait_ge(sC.h, 4)
                    dwait(e, sWST[0])
                    dwait(e, sWST[1])
                    dwait(e, sLNST[0])
                    dwait(e, sLNST[1])
                    dwait(e, sAST[0])
                    dwait(e, sAST[1])
                    e.wait_ge(sV.h, ms["qzdone"])
                if ("lnfmU", g2 - 2) in ms:
                    e.wait_ge(sP.h, ms[("lnfmU", g2 - 2)])
                dinc(e.dma_start(out=lnfm_g[:, (g2 % 2) * 8192:(g2 % 2 + 1) * 8192],
                                 in_=ln2_dram[g2]), sLF[g2 % 2])
            block.sync(c_lnld)

            for u in range(4):
                tt = g2 * 4 + u

                def c_gather(e, tt=tt):
                    if tt == 0:
                        dwait(e, sInit)
                        e.wait_ge(sC.h, 4)
                        dwait(e, sWST[0])
                        dwait(e, sWST[1])
                    if ("atokU", tt - 2) in ms:
                        e.wait_ge(sP.h, ms[("atokU", tt - 2)])
                    slot = (tt % 2) * D
                    for eh in range(4):
                        dinc(e.indirect_dma_start(
                            out=atok[:, slot + eh * 512:slot + (eh + 1) * 512],
                            out_offset=None, in_=ag_dram,
                            in_offset=IndirectOffsetOnAxis(
                                ap=idx_s[:, tt * 4 + eh:tt * 4 + eh + 1], axis=0),
                        ), sAG[tt % 2])
                block.gpsimd(c_gather)

                def c_tr(e, tt=tt):
                    if tt == 0:
                        e.wait_ge(sV.h, ms["qzdone"])
                    dwait(e, sAG[tt % 2])
                    slot = (tt % 2) * D
                    for half, bank, bn in ((0, psT0c, "T0"), (1, psT1c, "T1")):
                        if tbc_v[bn]:
                            e.wait_ge(sV.h, tbc_v[bn])
                        for i in range(8):
                            c = half * 8 + i
                            ii = e.transpose(bank[:, i * 128:(i + 1) * 128],
                                             atok[:, slot + c * 128:slot + (c + 1) * 128],
                                             ident_s[:])
                        ii.then_inc(sP.h, 1); sP.v += 1
                        tbc_p[bn] = sP.v
                    ms[("atokU", tt)] = sP.v
                block.tensor(c_tr)

                def c_ev(e, tt=tt, u=u, g2=g2):
                    if ("cmm", g2 - 1) in ms:
                        e.wait_ge(sP.h, ms[("cmm", g2 - 1)])
                    av = afm[:].rearrange("p (c n) -> p c n", c=16)
                    for half, bank, bn in ((0, psT0c, "T0"), (1, psT1c, "T1")):
                        e.wait_ge(sP.h, tbc_p[bn])
                        dst = av[:, half * 8:(half + 1) * 8, u * 128:(u + 1) * 128]
                        ii = e.tensor_copy(dst, bank[:].rearrange("p (c n) -> p c n", c=8))
                        ii.then_inc(sV.h, 1); sV.v += 1
                        tbc_v[bn] = sV.v
                    ms[("afmev", tt)] = sV.v
                block.vector(c_ev)

            def c_xld(e, g2=g2):
                for ts in range(4):
                    tt = g2 * 4 + ts
                    if ("xgU", tt - 4) in ms:
                        e.wait_ge(sV.h, ms[("xgU", tt - 4)])
                    dinc(e.dma_start(out=xg_sb[:, ts * D:(ts + 1) * D],
                                     in_=xown[tt * 128:(tt + 1) * 128, :]), sXG[ts % 2])
            block.sync(c_xld)

            if g2 == 0:
                def c_wld0(e):
                    c_wld_emit(e, 0)
                    c_wld_emit(e, 1)
                block.sync(c_wld0)

            for dq in range(4):
                dq_seq = g2 * 4 + dq
                for ts in range(4):
                    tt = g2 * 4 + ts
                    par = ts % 2
                    pss = psSo[par]
                    psg = psGo[par]

                    def c_mm(e, g2=g2, ts=ts, dq=dq, dq_seq=dq_seq, par=par,
                             pss=pss, psg=psg, tt=tt):
                        if ts == 0 and dq == 0:
                            e.wait_ge(sV.h, ms[("afmev", g2 * 4 + 3)])
                            dwait(e, sLF[g2 % 2])
                        if ts == 0:
                            dwait(e, sWL[dq_seq % 2])
                        lb = last_bank[par]
                        if lb is not None:
                            e.wait_ge(sV.h, lb[0])
                            e.wait_ge(sA.h, lb[1])
                        wslot = (dq_seq % 2) * 8192
                        lnb_ = (g2 % 2) * 8192
                        for c in range(16):
                            ii = e.matmul(pss[:], afm[:, c * 512 + ts * 128:c * 512 + (ts + 1) * 128],
                                          wo_sb[:, wslot + c * 512:wslot + (c + 1) * 512],
                                          start=(c == 0), stop=(c == 15))
                        ii.then_inc(sP.h, 1); sP.v += 1
                        ms[("cmms", g2, ts, dq)] = sP.v
                        for c in range(16):
                            ii = e.matmul(psg[:],
                                          lnfm_g[:, lnb_ + c * 512 + ts * 128:lnb_ + c * 512 + (ts + 1) * 128],
                                          wfq_sb[:, wslot + c * 512:wslot + (c + 1) * 512],
                                          start=(c == 0), stop=(c == 15))
                        ii.then_inc(sP.h, 1); sP.v += 1
                        ms[("cmmg", g2, ts, dq)] = sP.v
                        if ts == 3:
                            ms[("wU", dq_seq)] = sP.v
                        if ts == 3 and dq == 3:
                            ms[("cmm", g2)] = sP.v
                            ms[("lnfmU", g2)] = sP.v
                    block.tensor(c_mm)

                    def c_silu(e, g2=g2, ts=ts, dq=dq, psg=psg, lg=last_gsl):
                        e.wait_ge(sP.h, ms[("cmmg", g2, ts, dq)])
                        if lg is not None:
                            e.wait_ge(sV.h, lg)
                        e.activation(out=g_sl[:], in_=psg[:], func=AF.Silu).then_inc(sA.h, 1)
                        sA.v += 1
                        ms[("csilu", g2, ts, dq)] = sA.v
                    block.scalar(c_silu)

                    def c_gate(e, g2=g2, ts=ts, dq=dq, par=par, pss=pss, tt=tt):
                        e.wait_ge(sA.h, ms[("csilu", g2, ts, dq)])
                        e.wait_ge(sP.h, ms[("cmms", g2, ts, dq)])
                        if dq == 0:
                            dwait(e, sXG[ts % 2])
                            dwait(e, sYS[ts % 2])
                        ii = e.tensor_tensor(sg_sl[:], pss[:], g_sl[:], op=ALU.mult)
                        ii.then_inc(sV.h, 1); sV.v += 1
                        e.wait_ge(sV.h, sV.v)
                        ii = e.tensor_tensor(
                            out_sb[:, ts * D + dq * 512:ts * D + (dq + 1) * 512],
                            sg_sl[:],
                            xg_sb[:, ts * D + dq * 512:ts * D + (dq + 1) * 512],
                            op=ALU.add)
                        ii.then_inc(sV.h, 1); sV.v += 1
                        ms[("cout", g2, ts, dq)] = sV.v
                        if dq == 3:
                            ms[("xgU", tt)] = sV.v
                    block.vector(c_gate)
                    last_bank[par] = (sV.v, sA.v)
                    last_gsl = sV.v

                if dq_seq + 2 < 16:
                    def c_wldn(e, nseq=dq_seq + 2):
                        c_wld_emit(e, nseq)
                    block.sync(c_wldn)

            for ts in range(4):
                tt = g2 * 4 + ts

                def c_yst(e, g2=g2, ts=ts, tt=tt):
                    e.wait_ge(sV.h, ms[("cout", g2, ts, 3)])
                    dinc(e.dma_start(out=y[tt * 128:(tt + 1) * 128, :],
                                     in_=out_sb[:, ts * D:(ts + 1) * D]), sYS[ts % 2])
                block.gpsimd(c_yst)

        def fin(e):
            dwait(e, sYS[0])
            dwait(e, sYS[1])
        block.gpsimd(fin)

        esC.close()
        esCsb.close()
    es.close()
    return nc


_CACHE = {}


def _host_consts():
    t = np.arange(S, dtype=np.float64)
    inv = 1.0 / (10000.0 ** (np.arange(0, HD, 2, dtype=np.float64) / HD))
    fr = inv[:, None] * t[None, :]
    rc = np.cos(fr).astype(np.float32)
    rs = np.sin(fr).astype(np.float32)
    u = np.arange(MASKW, dtype=np.int64)[None, :] - 896
    si = np.arange(128, dtype=np.int64)[:, None]
    mstrip = np.where(u < si, np.float32(0.0), np.float32(1.0)).astype(BF)
    ident = np.eye(128, dtype=np.float32).astype(BF)
    ones = np.ones((128, 128), dtype=np.float32)
    return rc, rs, mstrip, ident, ones


def kernel(L0_attn_x, L0_attn_Wq, L0_attn_Wk, L0_attn_Wv, L0_attn_Wo,
           L0_ffn_W, L0_ffn_ln_w, L0_ffn_ln_b):
    x = np.ascontiguousarray(np.asarray(L0_attn_x, dtype=np.float32))
    Wq = np.ascontiguousarray(np.asarray(L0_attn_Wq, dtype=np.float32))
    Wk = np.ascontiguousarray(np.asarray(L0_attn_Wk, dtype=np.float32))
    Wv = np.ascontiguousarray(np.asarray(L0_attn_Wv, dtype=np.float32))
    Wo = np.ascontiguousarray(np.asarray(L0_attn_Wo, dtype=np.float32))
    W = np.ascontiguousarray(np.asarray(L0_ffn_W, dtype=np.float32))
    lnw = np.asarray(L0_ffn_ln_w, dtype=np.float32)
    lnb = np.asarray(L0_ffn_ln_b, dtype=np.float32)

    if "nc" not in _CACHE:
        _CACHE["nc"] = build_nc()
    nc = _CACHE["nc"]

    rc, rs, mstrip, ident, ones = _host_consts()
    lnwb = np.ascontiguousarray(np.broadcast_to(lnw[None, :], (128, D)))
    lnbb = np.ascontiguousarray(np.broadcast_to(lnb[None, :], (128, D)))

    in_maps = []
    for c in range(8):
        b, r, Q = c // 4, c % 4, c % 4
        idx = np.zeros((128, 64), dtype=np.uint32)
        p = np.arange(128, dtype=np.uint32)
        for uu in range(16):
            for eh in range(4):
                ch = 16 * Q + uu
                part = ch // 16
                idx[:, uu * 4 + eh] = (part * 16384 + (4 * b + eh) * 2048
                                       + (ch - 16 * part) * 128 + p)
        in_maps.append({
            "xb": x[b],
            "xown": np.ascontiguousarray(x[b, 2048 * Q:2048 * (Q + 1)]),
            "wq": np.ascontiguousarray(Wq[:, 512 * r:512 * (r + 1)]),
            "wk": np.ascontiguousarray(Wk[:, 128 * r:128 * (r + 1)]),
            "wv": np.ascontiguousarray(Wv[:, 128 * r:128 * (r + 1)]),
            "wo": Wo, "wf": W,
            "lnwb": lnwb, "lnbb": lnbb,
            "rcd": rc, "rsd": rs, "mstripd": mstrip,
            "identd": ident, "onesd": ones,
            "agidxd": idx,
        })

    res = run_bass_kernel_spmd(nc, in_maps, core_ids=list(range(8)))
    out = np.empty((B, S, D), dtype=np.float32)
    for c in range(8):
        b, Q = c // 4, c % 4
        out[b, 2048 * Q:2048 * (Q + 1), :] = res.results[c]["y"]
    return out


# version 18
# speedup vs baseline: 1.0899x; 1.0216x over previous
import numpy as np
import ml_dtypes
from contextlib import ExitStack

import concourse.bass as bass
import concourse.mybir as mybir
from concourse.bass import IndirectOffsetOnAxis
from concourse.bass_utils import run_bass_kernel_spmd

F32 = mybir.dt.float32
BF16 = mybir.dt.bfloat16
U32 = mybir.dt.uint32
AF = mybir.ActivationFunctionType
ALU = mybir.AluOpType
AX = mybir.AxisListType
BF = ml_dtypes.bfloat16

B, S, D = 2, 8192, 2048
H, KVH, HD = 16, 4, 128
GQ = H // KVH
SCALE = HD ** -0.5
EPS = 1e-5
NG = S // 512
NT = 8
MASKW = 1920
NEGBIG = -1.0e6


class Sem:
    __slots__ = ("h", "v")

    def __init__(self, h):
        self.h = h
        self.v = 0


def build_nc():
    nc = bass.Bass("TRN2", target_bir_lowering=False)
    es = ExitStack()

    xb = nc.dram_tensor("xb", [S, D], F32, kind="ExternalInput").ap()
    xown = nc.dram_tensor("xown", [2048, D], F32, kind="ExternalInput").ap()
    wq = nc.dram_tensor("wq", [D, GQ * HD], F32, kind="ExternalInput").ap()
    wk = nc.dram_tensor("wk", [D, HD], F32, kind="ExternalInput").ap()
    wv = nc.dram_tensor("wv", [D, HD], F32, kind="ExternalInput").ap()
    wo = nc.dram_tensor("wo", [D, D], F32, kind="ExternalInput").ap()
    wf = nc.dram_tensor("wf", [D, D], F32, kind="ExternalInput").ap()
    lnwb = nc.dram_tensor("lnwb", [128, D], F32, kind="ExternalInput").ap()
    lnbb = nc.dram_tensor("lnbb", [128, D], F32, kind="ExternalInput").ap()
    rcd = nc.dram_tensor("rcd", [64, S], F32, kind="ExternalInput").ap()
    rsd = nc.dram_tensor("rsd", [64, S], F32, kind="ExternalInput").ap()
    mstripd = nc.dram_tensor("mstripd", [128, MASKW], BF16, kind="ExternalInput").ap()
    identd = nc.dram_tensor("identd", [128, 128], BF16, kind="ExternalInput").ap()
    onesd = nc.dram_tensor("onesd", [128, 128], F32, kind="ExternalInput").ap()
    agidxd = nc.dram_tensor("agidxd", [128, 64], U32, kind="ExternalInput").ap()
    y = nc.dram_tensor("y", [2048, D], F32, kind="ExternalOutput").ap()

    k_dram = nc.dram_tensor("k_dram", [128, S], BF16).ap()
    v_dram = nc.dram_tensor("v_dram", [64, 128, 132], BF16).ap()
    qt_dram = nc.dram_tensor("qt_dram", [GQ, 128, S], BF16).ap()
    a_dram = nc.dram_tensor("a_dram", [64, 128, 512], BF16).ap()
    ag_dram = nc.dram_tensor("ag_dram", [8 * 8192, 512], BF16).ap()
    ln2_dram = nc.dram_tensor("ln2_dram", [4, 128, 8192], BF16).ap()
    wob_dram = nc.dram_tensor("wob_dram", [D, D], BF16).ap()
    wfq_dram = nc.dram_tensor("wfq_dram", [D, D], BF16).ap()

    sP = Sem(es.enter_context(nc.semaphore("sP")))
    sV = Sem(es.enter_context(nc.semaphore("sV")))
    sA = Sem(es.enter_context(nc.semaphore("sA")))
    sC = Sem(es.enter_context(nc.semaphore("sC")))
    sG = Sem(es.enter_context(nc.semaphore("sG")))

    def dsem(name):
        return Sem(es.enter_context(nc.semaphore(name)))

    sInit = dsem("sInit")
    sWI = dsem("sWI")
    sXA = dsem("sXA")
    sRP = [dsem(f"sRP{i}") for i in range(2)]
    sMX = [dsem(f"sMX{i}") for i in range(4)]
    sKST = [dsem(f"sKST{i}") for i in range(2)]
    sVST = [dsem(f"sVST{i}") for i in range(2)]
    sQST = [dsem(f"sQST{i}") for i in range(4)]
    sLNST = [dsem(f"sLNST{i}") for i in range(2)]
    sQT = [dsem(f"sQT{i}") for i in range(2)]
    sSC = [dsem(f"sSC{i}") for i in range(2)]
    sAST = [dsem(f"sAST{i}") for i in range(2)]
    sWR = [dsem(f"sWR{i}") for i in range(2)]
    sWST = [dsem(f"sWST{i}") for i in range(2)]
    sLF = [dsem(f"sLF{i}") for i in range(2)]
    sWL = [dsem(f"sWL{i}") for i in range(2)]
    sXG = [dsem(f"sXG{i}") for i in range(2)]
    sAG = [dsem(f"sAG{i}") for i in range(2)]
    sYS = [dsem(f"sYS{i}") for i in range(2)]
    sMSK = dsem("sMSK")

    def dinc(instr, sem):
        instr.then_inc(sem.h, 16)
        sem.v += 16

    def dwait(e, sem):
        e.wait_ge(sem.h, sem.v)

    def sb(name, shape, dt):
        return es.enter_context(nc.sbuf_tensor(name, shape, dt))

    esAsb = ExitStack()
    esBsb = ExitStack()
    esQsb = ExitStack()
    esCsb = ExitStack()

    esA1sb = ExitStack()

    def sbA(name, shape, dt):
        return esAsb.enter_context(nc.sbuf_tensor(name, shape, dt))

    def sbA1(name, shape, dt):
        return esA1sb.enter_context(nc.sbuf_tensor(name, shape, dt))

    def sbB(name, shape, dt):
        return esBsb.enter_context(nc.sbuf_tensor(name, shape, dt))

    def sbQ(name, shape, dt):
        return esQsb.enter_context(nc.sbuf_tensor(name, shape, dt))

    def sbC(name, shape, dt):
        return esCsb.enter_context(nc.sbuf_tensor(name, shape, dt))

    ident_s = sb("ident_s", [128, 128], BF16)
    ones_s = sb("ones_s", [128, 128], F32)
    bias_s = sb("bias_s", [128, 1], F32)
    idx_s = sb("idx_s", [128, 64], U32)

    w_ring = sbQ("w_ring", [128, 2 * D], F32)
    wabs = sbQ("wabs", [128, 16], F32)
    wabs1 = sbQ("wabs1", [128, 1], F32)
    thr_b = sbQ("thr_b", [128, 1], F32)
    thrn_b = sbQ("thrn_b", [128, 1], F32)

    lnw_s = sbA("lnw_s", [128, D], F32)
    lnb_s = sbA("lnb_s", [128, D], F32)
    x_ring = sbA("x_ring", [128, 4 * D], F32)
    xn_ring = sbA("xn_ring", [128, 4 * D], BF16)
    varp = sbA("varp", [128, 4], F32)
    rstd = sbA("rstd", [128, 4], F32)
    negmp = sbA("negmp", [128, 4], F32)
    bnst = sbA("bnst", [128, 96], F32)
    bnag = sbA("bnag", [128, 8], F32)
    v2p = sbA("v2p", [128, 4], F32)
    r2f = sbA("r2f", [128, 4], F32)
    ln2t1 = sbA("ln2t1", [128, D], F32)
    ln2t2 = sbA("ln2t2", [128, D], F32)
    ln2bf = sbA("ln2bf", [128, 2 * D], BF16)
    lnfm_st = sbA("lnfm_st", [128, 2 * 2048], BF16)
    xnfm = sbA1("xnfm", [128, 2 * 16 * 512], BF16)
    wq_s = sbA1("wq_s", [128, 16 * 512], BF16)
    wk_s = sbA1("wk_s", [128, 16 * 128], BF16)
    wv_s = sbA1("wv_s", [128, 16 * 128], BF16)
    rc_s = sbA1("rc_s", [64, 2 * 512], F32)
    rs_s = sbA1("rs_s", [64, 2 * 512], F32)
    rt1 = sbA1("rt1", [64, 512], F32)
    rt2 = sbA1("rt2", [64, 512], F32)
    ktg = sbA1("ktg", [128, 2 * 512], BF16)
    vtg = sbA1("vtg", [128, 2 * 4 * 132], BF16)
    qtg = sbA1("qtg", [128, 4 * 512], BF16)
    mx_in = sbA1("mx_in", [128, 2 * 512], BF16)
    sqt = sbA1("sqt", [128, 2 * 512], F32)
    mxsb = sbA1("mxsb", [1, 12 * 512], BF16)
    mkq = sbA1("mkq", [1, 4], F32)

    ms = {}

    with nc.Block() as block:
        def init_sync(e):
            for dst, srcp in ((ident_s, identd), (ones_s, onesd), (lnw_s, lnwb),
                              (lnb_s, lnbb), (idx_s, agidxd)):
                dinc(e.dma_start(out=dst[:], in_=srcp[:]), sInit)
        block.sync(init_sync)

        def init_gp(e):
            dinc(e.dma_start(out=wq_s[:].rearrange("p (c n) -> p c n", c=16),
                             in_=wq.rearrange("(c p) n -> p c n", p=128)), sWI)
            dinc(e.dma_start(out=wk_s[:].rearrange("p (c n) -> p c n", c=16),
                             in_=wk.rearrange("(c p) n -> p c n", p=128)), sWI)
            dinc(e.dma_start(out=wv_s[:].rearrange("p (c n) -> p c n", c=16),
                             in_=wv.rearrange("(c p) n -> p c n", p=128)), sWI)
            dinc(e.dma_start(out=wob_dram, in_=wo), sWI)
        block.gpsimd(init_gp)

        def init_v(e):
            e.memset(mkq[:], 1e-12)
            e.memset(vtg[:], 0.0)
            vv = vtg[:].rearrange("p (c n) -> p c n", c=8)
            e.memset(vv[:, :, 128:129], 1.0).then_inc(sV.h, 1)
            sV.v += 1
        block.vector(init_v)
        ms["initv"] = sV.v

        esA = ExitStack()
        psT0 = esA.enter_context(nc.psum_tensor("psT0", [128, 1024], BF16))
        psT1 = esA.enter_context(nc.psum_tensor("psT1", [128, 1024], BF16))
        psQ0 = esA.enter_context(nc.psum_tensor("psQ0", [128, 512], F32))
        psQ1 = esA.enter_context(nc.psum_tensor("psQ1", [128, 512], F32))
        psK = esA.enter_context(nc.psum_tensor("psK", [128, 512], F32))
        psVb = esA.enter_context(nc.psum_tensor("psVb", [128, 512], F32))
        psM0 = esA.enter_context(nc.psum_tensor("psM0", [1, 512], F32))
        psM1 = esA.enter_context(nc.psum_tensor("psM1", [1, 512], F32))

        x_cons = {}
        xn_ms = {}
        evac_ms = {}
        tb_v = {"T0": 0, "T1": 0}
        tb_p = {"T0": 0, "T1": 0}
        mm_ms = {}
        rope_ms = {}
        st_ms = {}
        ld_ms = {}
        tr_ms = {}

        def ln_group(tiles, consume_key, tr_guard=None, xn_out=True):
            for p0 in (0, 2):
                pair = tiles[p0:p0 + 2]
                cols = [t % 4 for t in pair]
                c0, c1 = min(cols), max(cols) + 1

                def v1(e, pair=pair, cols=cols, c0=c0, c1=c1):
                    dwait(e, sXA)
                    for t, cc in zip(pair, cols):
                        xs = x_ring[:, (t % 4) * D:(t % 4 + 1) * D]
                        for q in range(4):
                            ii = e.bn_stats(bnst[:, (cc * 4 + q) * 6:(cc * 4 + q + 1) * 6],
                                            xs[:, q * 512:(q + 1) * 512])
                    ii.then_inc(sV.h, 1); sV.v += 1
                    e.wait_ge(sV.h, sV.v)
                    for cc in cols:
                        ii = e.bn_aggr(bnag[:, cc * 2:cc * 2 + 2],
                                       bnst[:, cc * 24:(cc + 1) * 24])
                    ii.then_inc(sV.h, 1); sV.v += 1
                    e.wait_ge(sV.h, sV.v)
                    for cc in cols:
                        ii = e.tensor_scalar(varp[:, cc:cc + 1],
                                             bnag[:, cc * 2 + 1:cc * 2 + 2],
                                             1.0, EPS, op0=ALU.mult, op1=ALU.add)
                    ii.then_inc(sV.h, 1); sV.v += 1
                    e.wait_ge(sV.h, sV.v)
                    ii = e.reciprocal(rstd[:, c0:c1], varp[:, c0:c1])
                    ii.then_inc(sV.h, 1); sV.v += 1
                block.vector(v1)
                mv = sV.v

                def s1(e, c0=c0, c1=c1, mv=mv):
                    e.wait_ge(sV.h, mv)
                    e.activation(out=rstd[:, c0:c1], in_=rstd[:, c0:c1],
                                 func=AF.Sqrt).then_inc(sA.h, 1)
                    sA.v += 1
                block.scalar(s1)
                ma = sA.v

                def v2(e, c0=c0, c1=c1, ma=ma):
                    e.wait_ge(sA.h, ma)
                    for cc in range(c0, c1):
                        ii = e.tensor_scalar(negmp[:, cc:cc + 1], bnag[:, 2 * cc:2 * cc + 1],
                                             rstd[:, cc:cc + 1], -1.0,
                                             op0=ALU.mult, op1=ALU.mult)
                    ii.then_inc(sV.h, 1); sV.v += 1
                block.vector(v2)
                mneg = sV.v

                if not xn_out:
                    for t in pair:
                        xn_ms[t] = sA.v
                    continue

                def s2(e, pair=pair, cols=cols, mneg=mneg):
                    e.wait_ge(sV.h, mneg)
                    if tr_guard is not None:
                        for t in pair:
                            gv = tr_guard(t)
                            if gv:
                                e.wait_ge(sP.h, gv)
                    for t, cc in zip(pair, cols):
                        xs = x_ring[:, (t % 4) * D:(t % 4 + 1) * D]
                        ii = e.activation(out=xn_ring[:, (t % 4) * D:(t % 4 + 1) * D],
                                          in_=xs, func=AF.Identity,
                                          bias=negmp[:, cc:cc + 1],
                                          scale=rstd[:, cc:cc + 1])
                        ii.then_inc(sA.h, 1); sA.v += 1
                        xn_ms[t] = sA.v
                        if consume_key is not None:
                            ms[(consume_key, t)] = sA.v
                block.scalar(s2)

        def rope6(e, psrc, d0, d1, g):
            sl = (g % 2) * 512
            cs = rc_s[:, sl:sl + 512]
            sn = rs_s[:, sl:sl + 512]
            e.tensor_tensor(rt1[:], psrc[0:64, :], cs, op=ALU.mult)
            ii = e.tensor_tensor(rt2[:], psrc[64:128, :], sn, op=ALU.mult)
            ii.then_inc(sV.h, 1); sV.v += 1
            e.wait_ge(sV.h, sV.v)
            ii = e.tensor_tensor(d0, rt1[:], rt2[:], op=ALU.subtract)
            ii.then_inc(sV.h, 1); sV.v += 1
            e.wait_ge(sV.h, sV.v)
            e.tensor_tensor(rt1[:], psrc[0:64, :], sn, op=ALU.mult)
            ii = e.tensor_tensor(rt2[:], psrc[64:128, :], cs, op=ALU.mult)
            ii.then_inc(sV.h, 1); sV.v += 1
            e.wait_ge(sV.h, sV.v)
            ii = e.tensor_tensor(d1, rt1[:], rt2[:], op=ALU.add)
            ii.then_inc(sV.h, 1); sV.v += 1

        mx_n = [0]
        mx_rn = [0]

        def emit_mx_reds(upto):
            while mx_rn[0] < upto:
                n = mx_rn[0]
                mx_rn[0] += 1
                kind = "k" if n % 5 == 0 else "q"

                def mx_red(e, n=n, kind=kind):
                    e.wait_ge(sA.h, ms[("mxcp", n)])
                    acc = mkq[0:1, 0:1] if kind == "k" else mkq[0:1, 1:2]
                    scr = mkq[0:1, 2:3]
                    sl = (n % 12) * 512
                    ii = e.reduce_max(scr, mxsb[0:1, sl:sl + 512], axis=AX.X)
                    ii.then_inc(sV.h, 1); sV.v += 1
                    e.wait_ge(sV.h, sV.v)
                    ii = e.tensor_tensor(acc, acc, scr, op=ALU.max)
                    ii.then_inc(sV.h, 1); sV.v += 1
                    ms[("mxred", n)] = sV.v
                block.vector(mx_red)

        def emit_mx_items(g):
            for item in [("k", 0, g)] + [("q", j, g) for j in range(GQ)]:
                kind, j, i = item
                n = mx_n[0]
                mx_n[0] += 1
                slot = n % 2
                pslot = n % 2

                def mx_ld(e, n=n, kind=kind, j=j, i=i, slot=slot):
                    if kind == "k":
                        dwait(e, sKST[i % 2])
                    else:
                        dwait(e, sQST[j])
                    if ("mxsq", n - 2) in ms:
                        e.wait_ge(sA.h, ms[("mxsq", n - 2)])
                    srcp = k_dram[:, i * 512:(i + 1) * 512] if kind == "k" else \
                        qt_dram[j][:, i * 512:(i + 1) * 512]
                    dinc(e.dma_start(out=mx_in[:, slot * 512:(slot + 1) * 512],
                                     in_=srcp), sMX[slot])
                block.gpsimd(mx_ld)

                def mx_sq(e, n=n, slot=slot, pslot=pslot):
                    dwait(e, sMX[slot])
                    if ("mxmm", n - 2) in ms:
                        e.wait_ge(sP.h, ms[("mxmm", n - 2)])
                    e.activation(out=sqt[:, pslot * 512:(pslot + 1) * 512],
                                 in_=mx_in[:, slot * 512:(slot + 1) * 512],
                                 func=AF.Square).then_inc(sA.h, 1)
                    sA.v += 1
                    ms[("mxsq", n)] = sA.v
                block.scalar(mx_sq)

                def mx_mm(e, n=n, slot=slot, pslot=pslot):
                    if n == 0:
                        dwait(e, sInit)
                    e.wait_ge(sA.h, ms[("mxsq", n)])
                    if ("mxcp", n - 2) in ms:
                        e.wait_ge(sA.h, ms[("mxcp", n - 2)])
                    psm = psM0 if pslot == 0 else psM1
                    ii = e.matmul(psm[:], ones_s[:, 0:1], sqt[:, pslot * 512:(pslot + 1) * 512],
                                  start=True, stop=True)
                    ii.then_inc(sP.h, 1); sP.v += 1
                    ms[("mxmm", n)] = sP.v
                block.tensor(mx_mm)

                def mx_cp(e, n=n, pslot=pslot):
                    e.wait_ge(sP.h, ms[("mxmm", n)])
                    if ("mxred", n - 12) in ms:
                        e.wait_ge(sV.h, ms[("mxred", n - 12)])
                    psm = psM0 if pslot == 0 else psM1
                    sl = (n % 12) * 512
                    e.copy(out=mxsb[0:1, sl:sl + 512], in_=psm[:]).then_inc(sA.h, 1)
                    sA.v += 1
                    ms[("mxcp", n)] = sA.v
                block.scalar(mx_cp)


        def qz_ld1(c):
            def body(e, c=c):
                if ("qred", c - 2) in ms:
                    e.wait_ge(sV.h, ms[("qred", c - 2)])
                dinc(e.dma_start(out=w_ring[:, (c % 2) * D:(c % 2 + 1) * D],
                                 in_=wf[c * 128:(c + 1) * 128, :]), sWR[c % 2])
            block.gpsimd(body)

        def qz_red(c):
            def body(e, c=c):
                dwait(e, sWR[c % 2])
                ii = e.reduce_sum(wabs[:, c:c + 1],
                                  w_ring[:, (c % 2) * D:(c % 2 + 1) * D],
                                  axis=AX.X, apply_absolute_value=True)
                ii.then_inc(sV.h, 1); sV.v += 1
                ms[("qred", c)] = sV.v
            block.vector(body)

        for g in range(NG):
            tiles = [4 * g + i for i in range(4)]

            qz_ld1(g)
            if g >= 1:
                qz_red(g - 1)

            if g >= 2:
                emit_mx_reds(5 * (g - 1))
            if g >= 1:
                emit_mx_items(g - 1)

            def a_ld(e, g=g, tiles=tiles):
                for t in tiles:
                    if ("xcons", t - 4) in ms:
                        e.wait_ge(sA.h, ms[("xcons", t - 4)])
                    dinc(e.dma_start(out=x_ring[:, (t % 4) * D:(t % 4 + 1) * D],
                                     in_=xb[t * 128:(t + 1) * 128, :]), sXA)
                if ("ropeuse", g - 2) in ms:
                    e.wait_ge(sV.h, ms[("ropeuse", g - 2)])
                sl = (g % 2) * 512
                dinc(e.dma_start(out=rc_s[:, sl:sl + 512],
                                 in_=rcd[:, g * 512:(g + 1) * 512]), sRP[g % 2])
                dinc(e.dma_start(out=rs_s[:, sl:sl + 512],
                                 in_=rsd[:, g * 512:(g + 1) * 512]), sRP[g % 2])
            block.sync(a_ld)

            def a_lnw(e, g=g, tiles=tiles):
                dwait(e, sXA)
                if g == 0:
                    dwait(e, sInit)
            block.vector(a_lnw)
            ln_group(tiles, "xcons", tr_guard=lambda t: tr_ms.get(t - 4))

            for t in tiles:
                def a_tr(e, t=t, g=g):
                    if g == 0 and t == tiles[0]:
                        dwait(e, sInit)
                    e.wait_ge(sA.h, xn_ms[t])
                    for half, bank, bn in ((0, psT0, "T0"), (1, psT1, "T1")):
                        if tb_v[bn]:
                            e.wait_ge(sV.h, tb_v[bn])
                        for i in range(8):
                            c = half * 8 + i
                            ii = e.transpose(bank[:, i * 128:(i + 1) * 128],
                                             xn_ring[:, (t % 4) * D + c * 128:(t % 4) * D + (c + 1) * 128],
                                             ident_s[:])
                        ii.then_inc(sP.h, 1); sP.v += 1
                        tb_p[bn] = sP.v
                    tr_ms[t] = sP.v
                block.tensor(a_tr)

                def a_ev(e, t=t, g=g):
                    if ("mmdone", g - 2) in ms:
                        e.wait_ge(sP.h, ms[("mmdone", g - 2)])
                    base = (g % 2) * 16 * 512
                    xv = xnfm[:, base:base + 16 * 512].rearrange("p (c n) -> p c n", c=16)
                    for half, bank, bn in ((0, psT0, "T0"), (1, psT1, "T1")):
                        e.wait_ge(sP.h, tb_p[bn])
                        dst = xv[:, half * 8:(half + 1) * 8, (t % 4) * 128:(t % 4 + 1) * 128]
                        ii = e.tensor_copy(dst, bank[:].rearrange("p (c n) -> p c n", c=8))
                        ii.then_inc(sV.h, 1); sV.v += 1
                        tb_v[bn] = sV.v
                    evac_ms[t] = sV.v
                block.vector(a_ev)

            base = (g % 2) * 16 * 512

            def a_kmm(e, g=g, base=base):
                e.wait_ge(sV.h, evac_ms[4 * g + 3])
                if g == 0:
                    dwait(e, sWI)
                if ("ropeK", g - 1) in rope_ms:
                    e.wait_ge(sV.h, rope_ms[("ropeK", g - 1)])
                for c in range(16):
                    ii = e.matmul(psK[:], wk_s[:, c * 128:(c + 1) * 128],
                                  xnfm[:, base + c * 512:base + (c + 1) * 512],
                                  start=(c == 0), stop=(c == 15))
                ii.then_inc(sP.h, 1); sP.v += 1
                mm_ms[("K", g)] = sP.v
            block.tensor(a_kmm)

            def a_krope(e, g=g):
                dwait(e, sRP[g % 2])
                e.wait_ge(sP.h, mm_ms[("K", g)])
                dwait(e, sKST[g % 2])
                kb = (g % 2) * 512
                rope6(e, psK, ktg[0:64, kb:kb + 512], ktg[64:128, kb:kb + 512], g)
                rope_ms[("ropeK", g)] = sV.v
            block.vector(a_krope)

            def a_kst(e, g=g):
                e.wait_ge(sV.h, rope_ms[("ropeK", g)])
                kb = (g % 2) * 512
                dinc(e.dma_start(out=k_dram[:, g * 512:(g + 1) * 512],
                                 in_=ktg[:, kb:kb + 512]), sKST[g % 2])
            block.gpsimd(a_kst)

            def a_vmm(e, g=g, base=base):
                if ("vev", g - 1) in rope_ms:
                    e.wait_ge(sV.h, rope_ms[("vev", g - 1)])
                for ts in range(4):
                    for c in range(16):
                        ii = e.matmul(psVb[:, ts * 128:(ts + 1) * 128],
                                      xnfm[:, base + c * 512 + ts * 128:base + c * 512 + (ts + 1) * 128],
                                      wv_s[:, c * 128:(c + 1) * 128],
                                      start=(ts == 0 and c == 0), stop=(c == 15),
                                      skip_group_check=True)
                ii.then_inc(sP.h, 1); sP.v += 1
                mm_ms[("V", g)] = sP.v
            block.tensor(a_vmm)

            def a_vev(e, g=g):
                e.wait_ge(sP.h, mm_ms[("V", g)])
                dwait(e, sVST[g % 2])
                vv = vtg[:].rearrange("p (c n) -> p c n", c=8)
                sl = (g % 2) * 4
                ii = e.tensor_copy(vv[:, sl:sl + 4, 0:128],
                                   psVb[:].rearrange("p (c n) -> p c n", c=4))
                ii.then_inc(sV.h, 1); sV.v += 1
                rope_ms[("vev", g)] = sV.v
            block.vector(a_vev)

            def a_vst(e, g=g):
                e.wait_ge(sV.h, rope_ms[("vev", g)])
                vv = vtg[:].rearrange("p (c n) -> p c n", c=8)
                sl = (g % 2) * 4
                dinc(e.dma_start(out=v_dram[4 * g:4 * g + 4].rearrange("c p d -> p c d"),
                                 in_=vv[:, sl:sl + 4, :]), sVST[g % 2])
            block.gpsimd(a_vst)

            for j in range(GQ):
                def a_qmm(e, g=g, j=j, base=base):
                    key = ("ropeQ", g, j - 2) if j >= 2 else ("ropeQ", g - 1, j + 2)
                    if key in rope_ms:
                        e.wait_ge(sV.h, rope_ms[key])
                    bankq = psQ0 if j % 2 == 0 else psQ1
                    for c in range(16):
                        ii = e.matmul(bankq[:], wq_s[:, c * 512 + j * 128:c * 512 + (j + 1) * 128],
                                      xnfm[:, base + c * 512:base + (c + 1) * 512],
                                      start=(c == 0), stop=(c == 15))
                    ii.then_inc(sP.h, 1); sP.v += 1
                    mm_ms[("Q", g, j)] = sP.v
                block.tensor(a_qmm)

                def a_qrope(e, g=g, j=j):
                    dwait(e, sRP[g % 2])
                    e.wait_ge(sP.h, mm_ms[("Q", g, j)])
                    dwait(e, sQST[j])
                    bankq = psQ0 if j % 2 == 0 else psQ1
                    qb = j * 512
                    rope6(e, bankq, qtg[0:64, qb:qb + 512], qtg[64:128, qb:qb + 512], g)
                    rope_ms[("ropeQ", g, j)] = sV.v
                block.vector(a_qrope)

                def a_qst(e, g=g, j=j):
                    e.wait_ge(sV.h, rope_ms[("ropeQ", g, j)])
                    dinc(e.dma_start(out=qt_dram[j][:, g * 512:(g + 1) * 512],
                                     in_=qtg[:, j * 512:(j + 1) * 512]), sQST[j])
                block.gpsimd(a_qst)

            ms[("mmdone", g)] = sP.v
            ms[("ropeuse", g)] = sV.v

        qz_red(NG - 1)
        emit_mx_items(NG - 1)
        emit_mx_reds(5 * NG)

        def mx_fin_v(e):
            e.wait_ge(sV.h, ms[("mxred", mx_n[0] - 1)])
            ii = e.tensor_tensor(mkq[0:1, 2:3], mkq[0:1, 0:1], mkq[0:1, 1:2], op=ALU.mult)
            ii.then_inc(sV.h, 1); sV.v += 1
            ms["mxmul"] = sV.v
        block.vector(mx_fin_v)

        def mx_fin_a(e):
            e.wait_ge(sV.h, ms["mxmul"])
            e.activation(out=mkq[0:1, 2:3], in_=mkq[0:1, 2:3],
                         func=AF.Sqrt).then_inc(sA.h, 1)
            sA.v += 1
            ms["mxsqrt"] = sA.v
        block.scalar(mx_fin_a)

        def mx_fin_v2(e):
            e.wait_ge(sA.h, ms["mxsqrt"])
            ii = e.tensor_scalar(mkq[0:1, 2:3], mkq[0:1, 2:3], -SCALE, None, op0=ALU.mult)
            ii.then_inc(sV.h, 1); sV.v += 1
            ms["mxneg"] = sV.v
        block.vector(mx_fin_v2)

        def mx_bc(e):
            e.wait_ge(sV.h, ms["mxneg"])
            ii = e.matmul(psK[:, 0:1], ones_s[0:1, :], mkq[0:1, 2:3], start=True, stop=True)
            ii.then_inc(sP.h, 1); sP.v += 1
            ms["mxbc"] = sP.v
        block.tensor(mx_bc)

        def mx_bcv(e):
            e.wait_ge(sP.h, ms["mxbc"])
            ii = e.tensor_copy(bias_s[:], psK[:, 0:1])
            ii.then_inc(sV.h, 1); sV.v += 1
            ms["biasdone"] = sV.v
        block.vector(mx_bcv)

        def qz_sum_v(e):
            e.wait_ge(sV.h, ms[("qred", 15)])
            ii = e.reduce_sum(wabs1[:], wabs[:], axis=AX.X)
            ii.then_inc(sV.h, 1); sV.v += 1
            ms["qsum"] = sV.v
        block.vector(qz_sum_v)

        def qz_mm(e):
            e.wait_ge(sV.h, ms["qsum"])
            e.wait_ge(sA.h, ms[("mxcp", mx_n[0] - 1)])
            ii = e.matmul(psM0[0:1, 0:1], wabs1[:], ones_s[:, 0:1],
                          start=True, stop=True)
            ii.then_inc(sP.h, 1); sP.v += 1
            ms["qtot"] = sP.v
        block.tensor(qz_mm)

        def qz_thr_v(e):
            e.wait_ge(sP.h, ms["qtot"])
            ii = e.tensor_scalar(wabs1[0:1, 0:1], psM0[0:1, 0:1], 0.5 / (D * D),
                                 None, op0=ALU.mult)
            ii.then_inc(sV.h, 1); sV.v += 1
            ms["qthr1"] = sV.v
        block.vector(qz_thr_v)

        def qz_bc(e):
            e.wait_ge(sV.h, ms["qthr1"])
            e.wait_ge(sV.h, ms["biasdone"])
            ii = e.matmul(psK[:, 2:3], ones_s[0:1, :], wabs1[0:1, 0:1],
                          start=True, stop=True)
            ii.then_inc(sP.h, 1); sP.v += 1
            ms["qbc"] = sP.v
        block.tensor(qz_bc)

        def qz_bcv(e):
            e.wait_ge(sP.h, ms["qbc"])
            ii = e.tensor_copy(thr_b[:], psK[:, 2:3])
            ii.then_inc(sV.h, 1); sV.v += 1
            e.wait_ge(sV.h, sV.v)
            ii = e.tensor_scalar(thrn_b[:], thr_b[:], -1.0, None, op0=ALU.mult)
            ii.then_inc(sV.h, 1); sV.v += 1
            ms["qthr"] = sV.v
        block.vector(qz_bcv)

        esA.close()
        esA1sb.close()

        esB = ExitStack()
        psS_ = [esB.enter_context(nc.psum_tensor(f"psS{i}", [128, 1024], F32)) for i in range(2)]
        psO_ = [esB.enter_context(nc.psum_tensor(f"psO{i}", [128, 512], F32)) for i in range(3)]
        psT2b = esB.enter_context(nc.psum_tensor("psT2b", [128, 1024], BF16))

        mstrip_s = sbB("mstrip_s", [128, MASKW], BF16)
        qt_sb = sbB("qt_sb", [128, 2 * 1024], BF16)
        ksc_sb = sbB("ksc_sb", [128, 2 * 1024], BF16)
        vsc_sb = sbB("vsc_sb", [128, 2 * 8 * 132], BF16)
        pt_sb = sbB("pt_sb", [128, 4 * 1024], BF16)
        linv_sb = sbB("linv_sb", [128, 16], F32)
        a_sb = sbB("a_sb", [128, 2 * 8 * 128], BF16)
        araw = sbB("araw", [128, 2 * 1280], F32)
        araw_v = araw[:].rearrange("p (g w) -> p g w", w=160)
        gpbuf = sbB("gpbuf", [128, D], BF16)
        gnbuf = sbB("gnbuf", [128, D], BF16)
        wqbuf = sbB("wqbuf", [128, 2 * D], BF16)

        def qz_p2_ld(c):
            def qz_ld2(e, c=c):
                if c < 2:
                    e.wait_ge(sV.h, ms[("qred", 15)])
                else:
                    e.wait_ge(sV.h, ms[("qp2", c - 2)])
                dinc(e.dma_start(out=w_ring[:, (c % 2) * D:(c % 2 + 1) * D],
                                 in_=wf[c * 128:(c + 1) * 128, :]), sWR[c % 2])
            block.gpsimd(qz_ld2)

        def qz_p2_q(c):
            def qz_q(e, c=c):
                dwait(e, sWR[c % 2])
                if c == 0:
                    e.wait_ge(sV.h, ms["qthr"])
                dwait(e, sWST[c % 2])
                ws = w_ring[:, (c % 2) * D:(c % 2 + 1) * D]
                e.tensor_scalar(gpbuf[:], ws, thr_b[:, 0:1], None, op0=ALU.is_ge)
                ii = e.tensor_scalar(gnbuf[:], ws, thrn_b[:, 0:1], None, op0=ALU.is_le)
                ii.then_inc(sV.h, 1); sV.v += 1
                e.wait_ge(sV.h, sV.v)
                ii = e.tensor_tensor(wqbuf[:, (c % 2) * D:(c % 2 + 1) * D],
                                     gpbuf[:], gnbuf[:], op=ALU.subtract)
                ii.then_inc(sV.h, 1); sV.v += 1
                ms[("qp2", c)] = sV.v
            block.vector(qz_q)

            def qz_st(e, c=c):
                e.wait_ge(sV.h, ms[("qp2", c)])
                dinc(e.dma_start(out=wfq_dram[c * 128:(c + 1) * 128, :],
                                 in_=wqbuf[:, (c % 2) * D:(c % 2 + 1) * D]), sWST[c % 2])
            block.gpsimd(qz_st)

        qz_done = [0]
        qz_ld_done = [0]

        def qz_emit(tj):
            if tj % 2 == 0 and qz_ld_done[0] < 16:
                qz_p2_ld(qz_ld_done[0])
                qz_ld_done[0] += 1
            if tj % 2 == 1 and tj >= 3 and qz_done[0] < 16:
                qz_p2_q(qz_done[0])
                qz_done[0] += 1

        ln2tr_ms = {}
        tb2 = {"p": 0, "v": 0}

        def emit_a2_load(g2):
            tiles = [4 * g2 + i for i in range(4)]

            def a2_ld(e, g2=g2, tiles=tiles):
                for t in tiles:
                    if ("a2cons", t - 4) in ms:
                        e.wait_ge(sV.h, ms[("a2cons", t - 4)])
                    elif g2 == 0:
                        e.wait_ge(sA.h, ms[("xcons", 60 + (t % 4))])
                    dinc(e.dma_start(out=x_ring[:, (t % 4) * D:(t % 4 + 1) * D],
                                     in_=xown[t * 128:(t + 1) * 128, :]), sXA)
            block.sync(a2_ld)

        def emit_a2_group(g2):
            tiles = [4 * g2 + i for i in range(4)]

            def a2_w(e, g2=g2, tiles=tiles):
                dwait(e, sXA)
            block.vector(a2_w)
            ln_group(tiles, None,
                     tr_guard=(lambda t: tr_ms.get(60 + (t % 4))) if g2 == 0 else None,
                     xn_out=False)

            for p0 in (0, 2):
                pair = tiles[p0:p0 + 2]
                cols = [t % 4 for t in pair]
                c0, c1 = min(cols), max(cols) + 1

                def a2_v1(e, pair=pair, c0=c0, c1=c1):
                    e.wait_ge(sA.h, xn_ms[pair[-1]])
                    ii = e.scalar_tensor_tensor(v2p[:, c0:c1], varp[:, c0:c1], -EPS,
                                                rstd[:, c0:c1], op0=ALU.add, op1=ALU.mult)
                    ii.then_inc(sV.h, 1); sV.v += 1
                    e.wait_ge(sV.h, sV.v)
                    for cc in range(c0, c1):
                        ii = e.tensor_scalar(v2p[:, cc:cc + 1], v2p[:, cc:cc + 1],
                                             rstd[:, cc:cc + 1], EPS,
                                             op0=ALU.mult, op1=ALU.add)
                    ii.then_inc(sV.h, 1); sV.v += 1
                    e.wait_ge(sV.h, sV.v)
                    ii = e.reciprocal(r2f[:, c0:c1], v2p[:, c0:c1])
                    ii.then_inc(sV.h, 1); sV.v += 1
                    ms["a2mv"] = sV.v
                block.vector(a2_v1)
                mv = sV.v

                def a2_s(e, c0=c0, c1=c1, mv=mv):
                    e.wait_ge(sV.h, mv)
                    e.activation(out=r2f[:, c0:c1], in_=r2f[:, c0:c1],
                                 func=AF.Sqrt).then_inc(sA.h, 1)
                    sA.v += 1
                block.scalar(a2_s)
                mact = sA.v

                def a2_v2(e, pair=pair, mact=mact):
                    e.wait_ge(sA.h, mact)
                    for t in pair:
                        cc = t % 4
                        ii = e.tensor_tensor(r2f[:, cc:cc + 1], r2f[:, cc:cc + 1],
                                             rstd[:, cc:cc + 1], op=ALU.mult)
                    ii.then_inc(sV.h, 1); sV.v += 1
                    e.wait_ge(sV.h, sV.v)
                    for t in pair:
                        cc = t % 4
                        xs = x_ring[:, cc * D:(cc + 1) * D]
                        ii = e.tensor_scalar(ln2t1[:], xs, bnag[:, 2 * cc:2 * cc + 1],
                                             r2f[:, cc:cc + 1], op0=ALU.subtract, op1=ALU.mult)
                        ii.then_inc(sV.h, 1); sV.v += 1
                        e.wait_ge(sV.h, sV.v)
                        ii = e.tensor_tensor(ln2t2[:], ln2t1[:], lnw_s[:], op=ALU.mult)
                        ii.then_inc(sV.h, 1); sV.v += 1
                        e.wait_ge(sV.h, sV.v)
                        if ("ln2tr", t - 2) in ln2tr_ms:
                            e.wait_ge(sP.h, ln2tr_ms[("ln2tr", t - 2)])
                        ii = e.tensor_tensor(ln2bf[:, (t % 2) * D:(t % 2 + 1) * D],
                                             ln2t2[:], lnb_s[:], op=ALU.add)
                        ii.then_inc(sV.h, 1); sV.v += 1
                        ms[("ln2bf", t)] = sV.v
                        ms[("a2cons", t)] = sV.v
                block.vector(a2_v2)
                for t in pair:
                    for half in (0, 1):
                        def a2_tr(e, t=t, half=half):
                            if half == 0:
                                e.wait_ge(sV.h, ms[("ln2bf", t)])
                            if tb2["v"]:
                                e.wait_ge(sV.h, tb2["v"])
                            for i in range(8):
                                c = half * 8 + i
                                ii = e.transpose(psT2b[:, i * 128:(i + 1) * 128],
                                                 ln2bf[:, (t % 2) * D + c * 128:(t % 2) * D + (c + 1) * 128],
                                                 ident_s[:])
                            ii.then_inc(sP.h, 1); sP.v += 1
                            tb2["p"] = sP.v
                        block.tensor(a2_tr)
                        ln2tr_ms[("ln2tr", t)] = sP.v

                        def a2_ev(e, t=t, half=half):
                            if half == 0:
                                dwait(e, sLNST[t % 2])
                            e.wait_ge(sP.h, tb2["p"])
                            dst = lnfm_st[:, (t % 2) * 2048 + half * 1024:(t % 2) * 2048 + (half + 1) * 1024] \
                                .rearrange("p (c n) -> p c n", c=8)
                            ii = e.tensor_copy(dst, psT2b[:].rearrange("p (c n) -> p c n", c=8))
                            ii.then_inc(sV.h, 1); sV.v += 1
                            tb2["v"] = sV.v
                        block.vector(a2_ev)
                        ms[("lnev", t)] = sV.v

                    def a2_st(e, t=t, g2=g2):
                        e.wait_ge(sV.h, ms[("lnev", t)])
                        lnview = ln2_dram[g2].rearrange("p (c n) -> p c n", c=16)
                        for half in range(2):
                            dst = lnview[:, half * 8:(half + 1) * 8, (t % 4) * 128:(t % 4 + 1) * 128]
                            src = lnfm_st[:, (t % 2) * 2048 + half * 1024:(t % 2) * 2048 + (half + 1) * 1024] \
                                .rearrange("p (c n) -> p c n", c=8)
                            dinc(e.dma_start(out=dst, in_=src), sLNST[t % 2])
                    block.gpsimd(a2_st)



        def b_join(e):
            for s_ in (sKST[0], sKST[1], sVST[0], sVST[1], sQST[0], sQST[1],
                       sQST[2], sQST[3], sLNST[0], sLNST[1]):
                dwait(e, s_)
            e.wait_ge(sV.h, ms["biasdone"])
            dinc(e.dma_start(out=mstrip_s[:], in_=mstripd[:]), sMSK)
        block.sync(b_join)
        emit_a2_load(0)

        def emit_sc_load(e, sc, occ, scld_ms):
            slot = sc % 2
            if occ is not None:
                e.wait_ge(sP.h, occ)
            dinc(e.dma_start(out=ksc_sb[:, slot * 1024:(slot + 1) * 1024],
                             in_=k_dram[:, sc * 1024:(sc + 1) * 1024]), sSC[slot])
            vv = vsc_sb[:].rearrange("p (c n) -> p c n", c=16)
            dinc(e.dma_start(out=vv[:, slot * 8:(slot + 1) * 8, :],
                             in_=v_dram[8 * sc:8 * sc + 8].rearrange("c p d -> p c d")),
                 sSC[slot])
            scld_ms[sc] = sSC[slot].v

        sc_occ = [None, None]
        qk_last = {}
        pv_last = {}
        exp_last = {}
        norm_ms = {}
        evacB_ms = {}
        recip_ms = {}

        tj = 0
        for t in range(NT):
            nk = 8 * (t + 1)
            nsc = t + 1
            for j in range(GQ):
                qz_emit(tj)
                slot = tj % 2
                sPb, sAb, sVb = sP.v, sA.v, sV.v
                scld_ms = {}
                qk_ms = {0: sPb + 1}
                pv_ms = {}
                for k in range(1, nk):
                    qk_ms[k] = sPb + 2 * k
                for k in range(nk):
                    pv_ms[k] = sPb + 2 * k + 3 if k < nk - 1 else sPb + 2 * nk
                exp_ms = {k: sAb + k + 1 for k in range(nk)}
                mask_ms = {k: sVb + (k - 8 * t) + 1 for k in range(8 * t, nk)}

                def b_ld(e, t=t, j=j, tj=tj, nsc=nsc, slot=slot,
                         pv_ms=pv_ms, occ0=sc_occ[0], occ1=sc_occ[1],
                         scld_ms=scld_ms):
                    if tj - 2 >= 0:
                        e.wait_ge(sP.h, qk_last[tj - 2])
                    dinc(e.dma_start(out=qt_sb[:, slot * 1024:(slot + 1) * 1024],
                                     in_=qt_dram[j][:, t * 1024:(t + 1) * 1024]), sQT[tj % 2])
                    for sc in range(min(2, nsc)):
                        emit_sc_load(e, sc, occ0 if sc == 0 else occ1, scld_ms)
                    for sc2 in range(2, nsc):
                        emit_sc_load(e, sc2, pv_ms[8 * (sc2 - 2) + 7], scld_ms)
                block.sync(b_ld)

                def b_pe(e, t=t, j=j, tj=tj, nk=nk, slot=slot,
                         qk_ms=qk_ms, pv_ms=pv_ms, exp_ms=exp_ms, scld_ms=scld_ms,
                         mask_ms=mask_ms):
                    qa = qt_sb[:, slot * 1024:(slot + 1) * 1024]
                    vv = vsc_sb[:].rearrange("p (c n) -> p c n", c=16)

                    def emit_qk(k):
                        scs = (k // 8) % 2
                        if k == 0:
                            dwait(e, sQT[tj % 2])
                        if k % 8 == 0:
                            e.wait_ge(sSC[scs].h, scld_ms[k // 8])
                        if k >= 2:
                            e.wait_ge(sA.h, exp_ms[k - 2])
                        elif tj - 1 >= 0:
                            e.wait_ge(sA.h, exp_last[tj - 1])
                        ka = ksc_sb[:, scs * 1024 + (k % 8) * 128:scs * 1024 + (k % 8 + 1) * 128]
                        if not (k >= 8 * t and k - 8 * t >= 4):
                            e.matmul(psS_[k % 2][:, 0:512], ka, qa[:, 0:512],
                                     start=True, stop=True)
                        ii = e.matmul(psS_[k % 2][:, 512:1024], ka, qa[:, 512:1024],
                                      start=True, stop=True)
                        ii.then_inc(sP.h, 1); sP.v += 1
                        assert sP.v == qk_ms[k], (tj, k, sP.v, qk_ms[k])

                    def emit_pv(k):
                        if k >= 8 * t:
                            m = k - 8 * t
                            if k == 0:
                                qs_list = [(qs, qs == m) for qs in range(m, 8)]
                            else:
                                qs_list = [(qs, False) for qs in range(m + 1, 8)] + [(m, True)]
                        else:
                            qs_list = [(qs, False) for qs in range(8)]
                        e.wait_ge(sA.h, exp_ms[k])
                        if k == 0 and 8 * t == 0:
                            e.wait_ge(sV.h, mask_ms[k])
                        if k == 0 and tj - 1 >= 0:
                            e.wait_ge(sV.h, evacB_ms[tj - 1])
                        va = vv[:, ((k // 8) % 2) * 8 + (k % 8), 0:129]
                        for qs, needs_mask in qs_list:
                            if needs_mask and k != 0:
                                e.wait_ge(sV.h, mask_ms[k])
                            bank = psO_[qs // 3]
                            off = 160 * (qs % 3)
                            ii = e.matmul(bank[:, off:off + 129],
                                          pt_sb[:, (k % 4) * 1024 + qs * 128:(k % 4) * 1024 + (qs + 1) * 128],
                                          va, start=(k == 0 and qs % 3 == 0),
                                          stop=(k == 8 * t + qs), skip_group_check=True)
                        ii.then_inc(sP.h, 1); sP.v += 1
                        assert sP.v == pv_ms[k], (tj, k, sP.v, pv_ms[k])

                    emit_qk(0)
                    for k in range(nk):
                        if k + 1 < nk:
                            emit_qk(k + 1)
                        emit_pv(k)
                block.tensor(b_pe)
                qk_last[tj] = qk_ms[nk - 1]
                pv_last[tj] = pv_ms[nk - 1]
                sc_occ[0] = pv_ms[min(8 * (nsc - 1 - (nsc - 1) % 2) + 7, nk - 1)]
                if nsc >= 2:
                    sc_occ[1] = pv_ms[min(8 * (nsc - 1 - (nsc % 2)) + 7, nk - 1)]

                def b_msk(e, t=t, tj=tj, nk=nk, exp_ms=exp_ms, mask_ms=mask_ms):
                    if tj == 0:
                        dwait(e, sMSK)
                    for k in range(8 * t, nk):
                        e.wait_ge(sA.h, exp_ms[k])
                        m = k - 8 * t
                        pts = pt_sb[:, (k % 4) * 1024 + m * 128:(k % 4) * 1024 + (m + 1) * 128]
                        ii = e.tensor_tensor(pts, pts,
                                             mstrip_s[:, 896:1024], op=ALU.mult)
                        ii.then_inc(sV.h, 1); sV.v += 1
                        assert sV.v == mask_ms[k]
                block.vector(b_msk)

                def b_act(e, t=t, tj=tj, nk=nk, qk_ms=qk_ms, pv_ms=pv_ms):
                    if tj == 0:
                        e.wait_ge(sV.h, ms["biasdone"])
                    for k in range(nk):
                        e.wait_ge(sP.h, qk_ms[k])
                        if k >= 4:
                            e.wait_ge(sP.h, pv_ms[k - 4])
                        elif tj - 1 >= 0:
                            e.wait_ge(sP.h, pv_last[tj - 1])
                        c0 = (k - 8 * t) * 128 if k >= 8 * t else 0
                        ii = e.activation(out=pt_sb[:, (k % 4) * 1024 + c0:(k % 4 + 1) * 1024],
                                          in_=psS_[k % 2][:, c0:1024], func=AF.Exp,
                                          bias=bias_s[:], scale=SCALE)
                        ii.then_inc(sA.h, 1); sA.v += 1
                block.scalar(b_act)
                exp_last[tj] = sA.v

                def b_tail(e, tj=tj, nk=nk, pv_ms=pv_ms):
                    e.wait_ge(sP.h, pv_ms[nk - 1])
                    if tj - 2 >= 0:
                        e.wait_ge(sG.h, norm_ms[tj - 2])
                    g0 = (tj % 2) * 8
                    pv0 = psO_[0][:, 0:480].rearrange("p (s w) -> p s w", w=160)
                    pv1 = psO_[1][:, 0:480].rearrange("p (s w) -> p s w", w=160)
                    pv2 = psO_[2][:, 0:320].rearrange("p (s w) -> p s w", w=160)
                    e.tensor_copy(araw_v[:, g0:g0 + 3, 0:129], pv0[:, :, 0:129])
                    e.tensor_copy(araw_v[:, g0 + 3:g0 + 6, 0:129], pv1[:, :, 0:129])
                    ii = e.tensor_copy(araw_v[:, g0 + 6:g0 + 8, 0:129], pv2[:, :, 0:129])
                    ii.then_inc(sV.h, 1); sV.v += 1
                    evacB_ms[tj] = sV.v
                    e.wait_ge(sV.h, sV.v)
                    ii = e.reciprocal(linv_sb[:, g0:g0 + 8], araw_v[:, g0:g0 + 8, 128:129])
                    ii.then_inc(sV.h, 1); sV.v += 1
                    recip_ms[tj] = sV.v
                block.vector(b_tail)

                def b_norm(e, tj=tj):
                    e.wait_ge(sV.h, recip_ms[tj])
                    dwait(e, sAST[tj % 2])
                    g0 = (tj % 2) * 8
                    aslot = (tj % 2) * 8 * 128
                    for qs in range(8):
                        ii = e.tensor_scalar(a_sb[:, aslot + qs * 128:aslot + (qs + 1) * 128],
                                             araw_v[:, g0 + qs, 0:128],
                                             linv_sb[:, g0 + qs:g0 + qs + 1],
                                             None, op0=ALU.mult)
                    ii.then_inc(sG.h, 1); sG.v += 1
                    norm_ms[tj] = sG.v
                block.gpsimd(b_norm)

                def b_ast(e, t=t, j=j, tj=tj):
                    e.wait_ge(sG.h, norm_ms[tj])
                    aslot = (tj % 2) * 8
                    av = a_sb[:].rearrange("p (c n) -> p c n", c=16)
                    dst = a_dram[8 * t:8 * t + 8].rearrange("c p d -> p c d")[:, :, j * 128:(j + 1) * 128]
                    dinc(e.dma_start(out=dst, in_=av[:, aslot:aslot + 8, :]), sAST[tj % 2])
                block.gpsimd(b_ast)

                if tj in (7, 15, 23, 31):
                    part = tj // 8
                    def cc_gp(e, part=part):
                        dwait(e, sAST[0])
                        dwait(e, sAST[1])
                        a_flat = a_dram.rearrange("c p d -> (c p) d")
                        e.collective_compute(
                            "AllGather", ALU.bypass, replica_groups=[list(range(8))],
                            ins=[a_flat[part * 2048:(part + 1) * 2048, :].opt()],
                            outs=[ag_dram[part * 16384:(part + 1) * 16384, :].opt()],
                        ).then_inc(sC.h, 1)
                        sC.v += 1
                    block.gpsimd(cc_gp)

                tj += 1
                if tj in (6, 12, 18):
                    emit_a2_load(tj // 6)
                if tj in (3, 9, 15, 21):
                    emit_a2_group((tj - 3) // 6)

        while qz_ld_done[0] < 16:
            qz_p2_ld(qz_ld_done[0])
            qz_ld_done[0] += 1
        while qz_done[0] < 16:
            qz_p2_q(qz_done[0])
            qz_done[0] += 1
        ms["qzdone"] = ms[("qp2", 15)]

        esB.close()
        esBsb.close()




        esAsb.close()
        esQsb.close()


        esC = ExitStack()
        psT0c = esC.enter_context(nc.psum_tensor("psT0c", [128, 1024], BF16))
        psT1c = esC.enter_context(nc.psum_tensor("psT1c", [128, 1024], BF16))
        psSo = [esC.enter_context(nc.psum_tensor(f"psSo{i}", [128, 512], F32)) for i in range(2)]
        psGo = [esC.enter_context(nc.psum_tensor(f"psGo{i}", [128, 512], F32)) for i in range(2)]

        atok = sbC("atok", [128, 2 * D], BF16)
        afm = sbC("afm", [128, 16 * 512], BF16)
        lnfm_g = sbC("lnfm_g", [128, 2 * 8192], BF16)
        wo_sb = sbC("wo_sb", [128, 2 * 8192], BF16)
        wfq_sb = sbC("wfq_sb", [128, 2 * 8192], BF16)
        xg_sb = sbC("xg_sb", [128, 4 * D], F32)
        g_sl = sbC("g_sl", [128, 512], F32)
        sg_sl = sbC("sg_sl", [128, 512], F32)
        out_sb = sbC("out_sb", [128, 4 * D], F32)

        tbc_v = {"T0": 0, "T1": 0}
        tbc_p = {"T0": 0, "T1": 0}
        last_bank = [None, None]
        last_gsl = None

        def c_wld_emit(e, dq_seq):
            g2w, dqw = dq_seq // 4, dq_seq % 4
            slot = (dq_seq % 2) * 8192
            if dq_seq == 0:
                dwait(e, sWI)
                dwait(e, sWST[0])
                dwait(e, sWST[1])
            if ("wU", dq_seq - 2) in ms:
                e.wait_ge(sP.h, ms[("wU", dq_seq - 2)])
            dinc(e.dma_start(out=wo_sb[:, slot:slot + 8192].rearrange("p (c n) -> p c n", c=16),
                             in_=wob_dram[:, dqw * 512:(dqw + 1) * 512]
                             .rearrange("(c p) n -> p c n", p=128)), sWL[dq_seq % 2])
            dinc(e.dma_start(out=wfq_sb[:, slot:slot + 8192].rearrange("p (c n) -> p c n", c=16),
                             in_=wfq_dram[:, dqw * 512:(dqw + 1) * 512]
                             .rearrange("(c p) n -> p c n", p=128)), sWL[dq_seq % 2])

        for g2 in range(4):
            def c_lnld(e, g2=g2):
                if g2 == 0:
                    e.wait_ge(sC.h, 4)
                    dwait(e, sWST[0])
                    dwait(e, sWST[1])
                    dwait(e, sLNST[0])
                    dwait(e, sLNST[1])
                    dwait(e, sAST[0])
                    dwait(e, sAST[1])
                    e.wait_ge(sV.h, ms["qzdone"])
                if ("lnfmU", g2 - 2) in ms:
                    e.wait_ge(sP.h, ms[("lnfmU", g2 - 2)])
                dinc(e.dma_start(out=lnfm_g[:, (g2 % 2) * 8192:(g2 % 2 + 1) * 8192],
                                 in_=ln2_dram[g2]), sLF[g2 % 2])
            block.sync(c_lnld)

            for u in range(4):
                tt = g2 * 4 + u

                def c_gather(e, tt=tt):
                    if tt == 0:
                        dwait(e, sInit)
                        e.wait_ge(sC.h, 4)
                        dwait(e, sWST[0])
                        dwait(e, sWST[1])
                    if ("atokU", tt - 2) in ms:
                        e.wait_ge(sP.h, ms[("atokU", tt - 2)])
                    slot = (tt % 2) * D
                    for eh in range(4):
                        dinc(e.indirect_dma_start(
                            out=atok[:, slot + eh * 512:slot + (eh + 1) * 512],
                            out_offset=None, in_=ag_dram,
                            in_offset=IndirectOffsetOnAxis(
                                ap=idx_s[:, tt * 4 + eh:tt * 4 + eh + 1], axis=0),
                        ), sAG[tt % 2])
                block.gpsimd(c_gather)

                def c_tr(e, tt=tt):
                    if tt == 0:
                        e.wait_ge(sV.h, ms["qzdone"])
                    dwait(e, sAG[tt % 2])
                    slot = (tt % 2) * D
                    for half, bank, bn in ((0, psT0c, "T0"), (1, psT1c, "T1")):
                        if tbc_v[bn]:
                            e.wait_ge(sV.h, tbc_v[bn])
                        for i in range(8):
                            c = half * 8 + i
                            ii = e.transpose(bank[:, i * 128:(i + 1) * 128],
                                             atok[:, slot + c * 128:slot + (c + 1) * 128],
                                             ident_s[:])
                        ii.then_inc(sP.h, 1); sP.v += 1
                        tbc_p[bn] = sP.v
                    ms[("atokU", tt)] = sP.v
                block.tensor(c_tr)

                def c_ev(e, tt=tt, u=u, g2=g2):
                    if ("cmm", g2 - 1) in ms:
                        e.wait_ge(sP.h, ms[("cmm", g2 - 1)])
                    av = afm[:].rearrange("p (c n) -> p c n", c=16)
                    for half, bank, bn in ((0, psT0c, "T0"), (1, psT1c, "T1")):
                        e.wait_ge(sP.h, tbc_p[bn])
                        dst = av[:, half * 8:(half + 1) * 8, u * 128:(u + 1) * 128]
                        ii = e.tensor_copy(dst, bank[:].rearrange("p (c n) -> p c n", c=8))
                        ii.then_inc(sV.h, 1); sV.v += 1
                        tbc_v[bn] = sV.v
                    ms[("afmev", tt)] = sV.v
                block.vector(c_ev)

            def c_xld(e, g2=g2):
                for ts in range(4):
                    tt = g2 * 4 + ts
                    if ("xgU", tt - 4) in ms:
                        e.wait_ge(sV.h, ms[("xgU", tt - 4)])
                    dinc(e.dma_start(out=xg_sb[:, ts * D:(ts + 1) * D],
                                     in_=xown[tt * 128:(tt + 1) * 128, :]), sXG[ts % 2])
            block.sync(c_xld)

            if g2 == 0:
                def c_wld0(e):
                    c_wld_emit(e, 0)
                    c_wld_emit(e, 1)
                block.sync(c_wld0)

            for dq in range(4):
                dq_seq = g2 * 4 + dq
                for ts in range(4):
                    tt = g2 * 4 + ts
                    par = ts % 2
                    pss = psSo[par]
                    psg = psGo[par]

                    def c_mm(e, g2=g2, ts=ts, dq=dq, dq_seq=dq_seq, par=par,
                             pss=pss, psg=psg, tt=tt):
                        if ts == 0 and dq == 0:
                            e.wait_ge(sV.h, ms[("afmev", g2 * 4 + 3)])
                            dwait(e, sLF[g2 % 2])
                        if ts == 0:
                            dwait(e, sWL[dq_seq % 2])
                        lb = last_bank[par]
                        if lb is not None:
                            e.wait_ge(sV.h, lb[0])
                            e.wait_ge(sA.h, lb[1])
                        wslot = (dq_seq % 2) * 8192
                        lnb_ = (g2 % 2) * 8192
                        for c in range(16):
                            ii = e.matmul(pss[:], afm[:, c * 512 + ts * 128:c * 512 + (ts + 1) * 128],
                                          wo_sb[:, wslot + c * 512:wslot + (c + 1) * 512],
                                          start=(c == 0), stop=(c == 15))
                        ii.then_inc(sP.h, 1); sP.v += 1
                        ms[("cmms", g2, ts, dq)] = sP.v
                        for c in range(16):
                            ii = e.matmul(psg[:],
                                          lnfm_g[:, lnb_ + c * 512 + ts * 128:lnb_ + c * 512 + (ts + 1) * 128],
                                          wfq_sb[:, wslot + c * 512:wslot + (c + 1) * 512],
                                          start=(c == 0), stop=(c == 15))
                        ii.then_inc(sP.h, 1); sP.v += 1
                        ms[("cmmg", g2, ts, dq)] = sP.v
                        if ts == 3:
                            ms[("wU", dq_seq)] = sP.v
                        if ts == 3 and dq == 3:
                            ms[("cmm", g2)] = sP.v
                            ms[("lnfmU", g2)] = sP.v
                    block.tensor(c_mm)

                    def c_silu(e, g2=g2, ts=ts, dq=dq, psg=psg, lg=last_gsl):
                        e.wait_ge(sP.h, ms[("cmmg", g2, ts, dq)])
                        if lg is not None:
                            e.wait_ge(sV.h, lg)
                        e.activation(out=g_sl[:], in_=psg[:], func=AF.Silu).then_inc(sA.h, 1)
                        sA.v += 1
                        ms[("csilu", g2, ts, dq)] = sA.v
                    block.scalar(c_silu)

                    def c_gate(e, g2=g2, ts=ts, dq=dq, par=par, pss=pss, tt=tt):
                        e.wait_ge(sA.h, ms[("csilu", g2, ts, dq)])
                        e.wait_ge(sP.h, ms[("cmms", g2, ts, dq)])
                        if dq == 0:
                            dwait(e, sXG[ts % 2])
                            dwait(e, sYS[ts % 2])
                        ii = e.tensor_tensor(sg_sl[:], pss[:], g_sl[:], op=ALU.mult)
                        ii.then_inc(sV.h, 1); sV.v += 1
                        e.wait_ge(sV.h, sV.v)
                        ii = e.tensor_tensor(
                            out_sb[:, ts * D + dq * 512:ts * D + (dq + 1) * 512],
                            sg_sl[:],
                            xg_sb[:, ts * D + dq * 512:ts * D + (dq + 1) * 512],
                            op=ALU.add)
                        ii.then_inc(sV.h, 1); sV.v += 1
                        ms[("cout", g2, ts, dq)] = sV.v
                        if dq == 3:
                            ms[("xgU", tt)] = sV.v
                    block.vector(c_gate)
                    last_bank[par] = (sV.v, sA.v)
                    last_gsl = sV.v

                if dq_seq + 2 < 16:
                    def c_wldn(e, nseq=dq_seq + 2):
                        c_wld_emit(e, nseq)
                    block.sync(c_wldn)

            for ts in range(4):
                tt = g2 * 4 + ts

                def c_yst(e, g2=g2, ts=ts, tt=tt):
                    e.wait_ge(sV.h, ms[("cout", g2, ts, 3)])
                    dinc(e.dma_start(out=y[tt * 128:(tt + 1) * 128, :],
                                     in_=out_sb[:, ts * D:(ts + 1) * D]), sYS[ts % 2])
                block.gpsimd(c_yst)

        def fin(e):
            dwait(e, sYS[0])
            dwait(e, sYS[1])
        block.gpsimd(fin)

        esC.close()
        esCsb.close()
    es.close()
    return nc


_CACHE = {}


def _host_consts():
    t = np.arange(S, dtype=np.float64)
    inv = 1.0 / (10000.0 ** (np.arange(0, HD, 2, dtype=np.float64) / HD))
    fr = inv[:, None] * t[None, :]
    rc = np.cos(fr).astype(np.float32)
    rs = np.sin(fr).astype(np.float32)
    u = np.arange(MASKW, dtype=np.int64)[None, :] - 896
    si = np.arange(128, dtype=np.int64)[:, None]
    mstrip = np.where(u < si, np.float32(0.0), np.float32(1.0)).astype(BF)
    ident = np.eye(128, dtype=np.float32).astype(BF)
    ones = np.ones((128, 128), dtype=np.float32)
    return rc, rs, mstrip, ident, ones


def kernel(L0_attn_x, L0_attn_Wq, L0_attn_Wk, L0_attn_Wv, L0_attn_Wo,
           L0_ffn_W, L0_ffn_ln_w, L0_ffn_ln_b):
    x = np.ascontiguousarray(np.asarray(L0_attn_x, dtype=np.float32))
    Wq = np.ascontiguousarray(np.asarray(L0_attn_Wq, dtype=np.float32))
    Wk = np.ascontiguousarray(np.asarray(L0_attn_Wk, dtype=np.float32))
    Wv = np.ascontiguousarray(np.asarray(L0_attn_Wv, dtype=np.float32))
    Wo = np.ascontiguousarray(np.asarray(L0_attn_Wo, dtype=np.float32))
    W = np.ascontiguousarray(np.asarray(L0_ffn_W, dtype=np.float32))
    lnw = np.asarray(L0_ffn_ln_w, dtype=np.float32)
    lnb = np.asarray(L0_ffn_ln_b, dtype=np.float32)

    if "nc" not in _CACHE:
        _CACHE["nc"] = build_nc()
    nc = _CACHE["nc"]

    rc, rs, mstrip, ident, ones = _host_consts()
    lnwb = np.ascontiguousarray(np.broadcast_to(lnw[None, :], (128, D)))
    lnbb = np.ascontiguousarray(np.broadcast_to(lnb[None, :], (128, D)))

    in_maps = []
    for c in range(8):
        b, r, Q = c // 4, c % 4, c % 4
        idx = np.zeros((128, 64), dtype=np.uint32)
        p = np.arange(128, dtype=np.uint32)
        for uu in range(16):
            for eh in range(4):
                ch = 16 * Q + uu
                part = ch // 16
                idx[:, uu * 4 + eh] = (part * 16384 + (4 * b + eh) * 2048
                                       + (ch - 16 * part) * 128 + p)
        in_maps.append({
            "xb": x[b],
            "xown": np.ascontiguousarray(x[b, 2048 * Q:2048 * (Q + 1)]),
            "wq": np.ascontiguousarray(Wq[:, 512 * r:512 * (r + 1)]),
            "wk": np.ascontiguousarray(Wk[:, 128 * r:128 * (r + 1)]),
            "wv": np.ascontiguousarray(Wv[:, 128 * r:128 * (r + 1)]),
            "wo": Wo, "wf": W,
            "lnwb": lnwb, "lnbb": lnbb,
            "rcd": rc, "rsd": rs, "mstripd": mstrip,
            "identd": ident, "onesd": ones,
            "agidxd": idx,
        })

    res = run_bass_kernel_spmd(nc, in_maps, core_ids=list(range(8)))
    out = np.empty((B, S, D), dtype=np.float32)
    for c in range(8):
        b, Q = c // 4, c % 4
        out[b, 2048 * Q:2048 * (Q + 1), :] = res.results[c]["y"]
    return out
